# version 09
# baseline (speedup 1.0000x reference)
import contextlib
import numpy as np
import ml_dtypes
import concourse.bass as bass
import concourse.mybir as mybir
from concourse.bass_isa_rust import ReduceOp
from concourse.bass_utils import run_bass_kernel_spmd

F32 = mybir.dt.float32
BF16 = mybir.dt.bfloat16
AF = mybir.ActivationFunctionType
ALU = mybir.AluOpType
AXX = mybir.AxisListType.X

P = 128
NCORE = 8
TCORE = 2048
NT = 16
D = 1024
KD = 8
H, N8 = 4, 8
HN = 32
U = 352
OFF_SIGN, OFF_PERM, OFF_GATE, OFF_IN = 0, 32, 288, 320
LCH = 16
OF = 4096
NOC = 32
FT = 512
NFT = 4
LN_EPS = 1e-5
BIG = 1e30
MAGIC = 12582912.0
SEGW = 2 * P * OF + 4 * P * D
ENGS = ("sync", "vector", "scalar", "tensor", "gpsimd")


class Sched:

    LOCKSTEP = False

    def __init__(self, nc, ctx, sem_names):
        self.nc = nc
        self.ops = {e: [] for e in ENGS}
        self.sem = {}
        self.cnt = {}
        for s in sem_names:
            self.sem[s] = ctx.enter_context(nc.semaphore(s))
            self.cnt[s] = 0
        self.nbar = 0

    def op(self, eng, fn, wait=(), inc=(), tag=""):
        if self.LOCKSTEP:
            sel = self.LOCKSTEP if isinstance(self.LOCKSTEP, (tuple, list, set)) \
                else list(self.cnt)
            sel2 = []
            for n in sel:
                if n.startswith("vrange:"):
                    _, a_, b_ = n.split(":")
                    if eng == "vector" and int(a_) <= len(self.ops["vector"]) < int(b_):
                        sel2.append("ve")
                elif "@" in n:
                    nm, e_ = n.split("@")
                    if e_ == eng:
                        sel2.append(nm)
                else:
                    sel2.append(n)
            wait = list(wait) + [(n, self.cnt[n]) for n in sel2]
        waits = []
        for w in wait:
            if isinstance(w, tuple):
                waits.append((self.sem[w[0]], w[1]))
            else:
                waits.append((self.sem[w], self.cnt[w]))
        incs = []
        for i in inc:
            if isinstance(i, tuple):
                name, n = i
            else:
                name, n = i, 1
            self.cnt[name] += n
            incs.append((self.sem[name], n))
        self.ops[eng].append((fn, waits, incs, tag))

    def val(self, name):
        return self.cnt[name]

    def barrier(self):
        self.nbar += 1
        tgt = 5 * self.nbar
        dma_sems = ("ld", "st", "wq", "ls", "lm", "lq", "lx0", "lx1", "lxT",
                    "lay", "lg", "lw0", "lw1", "lwd0", "lwd1", "ly0", "qa",
                    "dq")
        for e in ENGS:
            wl = [(n, self.cnt[n]) for n in dma_sems] if e == "sync" else []
            self.op(e, lambda eng: eng.nop(), wait=wl, inc=[("ph", 1)])
        for e in ENGS:
            self.op(e, lambda eng: eng.nop(), wait=[("ph", tgt)])

    def emit(self, block):
        sched = self

        def mk(engname):
            serial_sem = {"vector": "ve", "gpsimd": "ge"}.get(engname)

            def emitter(eng):
                prior = 0
                for fn, waits, incs, tag in sched.ops[engname]:
                    if serial_sem and prior:
                        eng.wait_ge(sched.sem[serial_sem], prior)
                    for h, v in waits:
                        eng.wait_ge(h, v)
                    instr = fn(eng)
                    if incs:
                        assert instr is not None, f"op {tag} returned no instr"
                        instr.then_inc(*incs[0])
                        for h, n in incs[1:]:
                            eng.nop().then_inc(h, n)
                    if serial_sem:
                        for hn, n in incs:
                            if hn is sched.sem[serial_sem]:
                                prior += n
            return emitter

        block.sync(mk("sync"))
        block.vector(mk("vector"))
        block.scalar(mk("scalar"))
        block.tensor(mk("tensor"))
        block.gpsimd(mk("gpsimd"))


class Ctx:
    pass


def build_core(debug_taps=()):
    nc = bass.Bass("TRN2", target_bir_lowering=False)
    c = Ctx()
    c.nc = nc
    c.taps = {}
    c.debug_taps = debug_taps

    dd = {}
    for nm, shape in [("x", [TCORE, D]), ("xT", [D, TCORE]), ("Wcat", [D, U]),
                      ("bcat", [1, U]), ("lnw", [P, KD]), ("lnb", [P, KD]),
                      ("outWb", [33, D]), ("wgs", [P, OF]), ("wus", [P, OF]),
                      ("wds", [4 * P, D]), ("ident", [P, P]), ("ones1", [1, P]),
                      ("onescol", [P, 1]), ("cid", [1, 1])]:
        dd[nm] = nc.dram_tensor(nm, shape, F32, kind="ExternalInput").ap()
    dd["out"] = nc.dram_tensor("out", [TCORE, D], F32, kind="ExternalOutput").ap()
    dd["ybuf"] = nc.dram_tensor("ybuf", [TCORE, D], F32, kind="Internal").ap()
    dd["cc_in"] = nc.dram_tensor("cc_in", [1, 96], F32, kind="Internal",
                                 addr_space="Local").ap()
    dd["cc_out"] = nc.dram_tensor("cc_out", [NCORE, 96], F32, kind="Internal",
                                  addr_space="Shared").ap()
    dd["ccm_in"] = nc.dram_tensor("ccm_in", [1, 4], F32, kind="Internal",
                                  addr_space="Local").ap()
    dd["ccm_out"] = nc.dram_tensor("ccm_out", [NCORE, 4], F32, kind="Internal",
                                   addr_space="Shared").ap()
    dd["ccw_in"] = nc.dram_tensor("ccw_in", [1, SEGW], BF16, kind="Internal",
                                  addr_space="Local").ap()
    dd["ccw_out"] = nc.dram_tensor("ccw_out", [NCORE, SEGW], BF16,
                                   kind="Internal", addr_space="Shared").ap()
    c.d = dd

    def tapfn(name, shape, dtype=F32):
        if name in debug_taps:
            c.taps[name] = nc.dram_tensor("tap_" + name, shape, dtype,
                                          kind="ExternalOutput").ap()
        return c.taps.get(name)

    c.tap = tapfn

    T = {}
    c.T = T

    def al(name, shape, dtype=F32):
        T[name] = nc.alloc_sbuf_tensor("sb_" + name, shape, dtype)

    def alias(name, host, off, shape, dtype=F32):
        base = nc.lookup_mloc(T[host]).addr
        T[name] = nc.alloc_sbuf_tensor_at("sb_" + name, shape, dtype,
                                          offset=base + off)

    al("ident", [P, P], BF16)
    al("ones1", [1, P]); al("onescol", [P, 1], BF16); al("cid", [1, 1])
    al("iota8", [P, 8])
    al("lnw", [P, KD]); al("lnb", [P, KD])
    al("lnbh", [P, KD], BF16); al("lnbl", [P, KD], BF16)
    al("mu", [P, NT]); al("rstd", [P, NT]); al("nrstdmu", [P, NT])
    al("sa", [P, NT]); al("dsc", [P, NT])
    al("wsc", [P, 4]); al("whalf", [P, 4])
    al("stat1", [P, NT]); al("stat2", [P, NT]); al("st_s", [P, NT])
    al("s1", [P, 1]); al("s2", [P, 1]); al("s3", [P, 1])
    al("gm1", [P, NT * H]); al("gci", [P, NT * H])
    al("chn", [P, 1]); al("chp", [P, 1])
    al("identf", [P, P])
    al("onescf", [P, 1])
    al("wrow", [1, 4]); al("wrow8", [1, 32])
    al("mt", [P, 4]); al("rut", [P, 4]); al("sit", [P, 4])
    al("rsa", [P, NT]); al("rgq", [P, NT]); al("ruq", [P, NT])
    al("sitq", [P, NT])
    al("magicb", [P, 1]); al("nmagicb", [P, 1])
    al("sc_po", [P, 256], BF16); al("sc_sum", [P, 96]); al("onesL", [P, LCH * HN])
    al("l2_po", [8, 256], BF16)
    al("gsum", [1, NCORE * 96]); al("gcur", [1, 96]); al("gnx", [1, 96])
    al("goh", [1, H * 64]); al("gt0", [1, H * 64]); al("gmsk", [1, 1])
    al("h_in", [P, HN]); al("e_full", [P, HN]); al("e_ch", [P, HN])
    al("xstage", [P, D]); al("xstage2", [P, D])
    al("xTstage", [P, TCORE])
    al("wdst", [P, D], BF16); al("wdst2", [P, D], BF16)
    al("outWb_hi", [P, D], BF16); al("outWb_lo", [P, D], BF16)

    al("Wcat", [P, KD * U])
    al("Wp_hi", [P, KD * U], BF16)
    al("Wp_lo", [P, KD * U], BF16)
    al("K1R", [P, U]); al("B2R", [P, U])
    al("bcat", [1, U]); al("k1row", [1, U]); al("b2row", [1, U])
    al("wfst", [P, 2048])
    al("wist", [P, 2048], BF16)
    alias("x_intT", "Wcat", 0, [P, KD * TCORE], BF16)
    assert (nc.lookup_mloc(T["wfst"]).addr + 8192
            >= nc.lookup_mloc(T["Wcat"]).addr + KD * TCORE * 2), "x_intT span"

    al("Ubuf", [P, NT * U])
    alias("sc_oh", "Ubuf", 0, [P, LCH * H * 64], BF16)
    alias("sc_hh", "Ubuf", 8192, [P, LCH * HN])
    alias("ys_hi", "Ubuf", 12288, [P, LCH * HN], BF16)
    alias("ys_lo", "Ubuf", 13312, [P, LCH * HN], BF16)
    alias("sc_ys", "Ubuf", 10240, [P, LCH * HN])
    alias("ysT_hi", "Ubuf", 0, [P, TCORE], BF16)
    alias("ysT_lo", "Ubuf", 4096, [P, TCORE], BF16)
    alias("gsbA", "Ubuf", 0, [P, FT])
    alias("gsbB", "Ubuf", 2048, [P, FT])
    alias("Mq", "Ubuf", 4096, [P, 32])
    alias("ystg0", "Ubuf", 6144, [P, D])
    alias("ystg1", "Ubuf", 10240, [P, D])
    alias("ystg2", "Ubuf", 14336, [P, D])
    alias("ystg3", "Ubuf", 18432, [P, D])

    al("xT_hi", [P, KD * TCORE], BF16)
    alias("gtmp", "xT_hi", 0, [P, NT * 256])
    alias("sc_t0", "xT_hi", 16384, [P, LCH * 256])
    alias("l2_t0", "xT_hi", 16384, [8, LCH * 256])
    alias("zB", "xT_hi", 0, [P, 16 * FT])
    alias("iqB0", "xT_hi", 0, [P, 2 * OF], BF16)
    alias("iqB1", "xT_hi", 16384, [P, 2 * OF], BF16)

    al("xT_lo", [P, KD * TCORE], BF16)
    for i_, nm_ in enumerate(["l2_sig", "l2_rho", "l2_s", "l2_sg", "l2_dh",
                              "l2_ch", "l2_pi", "l2_sh", "l2_v"]):
        alias(nm_, "xT_lo", i_ * 2048, [8, LCH * HN])
    alias("l2_oh", "xT_lo", 9 * 2048, [8, LCH * H * 64], BF16)
    alias("zA", "xT_lo", 0, [P, 16 * FT])
    alias("iqA0", "xT_lo", 0, [P, 2 * OF], BF16)
    alias("iqA1", "xT_lo", 16384, [P, 2 * OF], BF16)

    al("S", [P, NT * 256])
    for i_, nm_ in enumerate(["sc_d", "sc_c", "sc_p", "sc_sig", "sc_dh",
                              "sc_ch", "sc_pi", "sc_sh"]):
        alias(nm_, "S", i_ * 2048, [P, LCH * HN])
    alias("wstage", "S", 0, [P, KD * FT], BF16)
    alias("wstage2", "S", 8192, [P, KD * FT], BF16)
    alias("wstageB", "xTstage", 0, [P, KD * FT], BF16)
    assert (nc.lookup_mloc(T["xstage2"]).addr
            == nc.lookup_mloc(T["xstage"]).addr + D * 4), "xstage adjacency"
    alias("wstage2B", "xstage", 0, [P, KD * FT], BF16)

    al("sgn", [P, NT * HN])
    for nm_ in ["gam", "dvec", "cvec", "rsum", "rrec", "Rmax", "AVm", "PIDX",
                "grk", "gsel", "gcs", "gsr", "gsm"]:
        al(nm_, [P, NT * HN])
    assert nc.lookup_mloc(T["gsm"]).addr - nc.lookup_mloc(T["sgn"]).addr \
        == 13 * 2048, "RE contiguity"
    alias("xq_t", "sgn", 0, [P, D], BF16)
    alias("qv_t", "sgn", 2048, [P, D])
    alias("yt1", "sgn", 6144, [P, D])
    alias("zt1", "sgn", 10240, [P, D])
    alias("yout", "sgn", 14336, [P, D])
    alias("qf_t", "sgn", 9 * 2048, [P, D])

    c.sbuf_left = nc.sbuf_bytes_remaining
    c.PSB = [nc.alloc_psum_tensor(f"ps{b}", [P, 512], F32) for b in range(8)]

    with contextlib.ExitStack() as ctx:
        sch = Sched(nc, ctx, ["ld", "wq", "mm", "ve", "se", "ge", "st", "cc",
                              "dq", "qa", "ph", "ev", "ls", "lm", "lq", "lx0",
                              "lx1", "lxT", "lay", "lg", "lw0", "lw1", "lwd0",
                              "lwd1", "ly0"])
        c.s = sch
        block = ctx.enter_context(nc.Block())
        plan(c)
        sch.emit(block)

    return nc, c.taps, c.sbuf_left



def v_op(c, fn, wait=(), tag=""):
    c.s.op("vector", fn, wait=wait, inc=[("ve", 1)], tag=tag)


def s_op(c, fn, wait=(), tag=""):
    c.s.op("scalar", fn, wait=wait, inc=[("se", 1)], tag=tag)


def g_op(c, fn, wait=(), tag=""):
    c.s.op("gpsimd", fn, wait=wait, inc=[("ge", 1)], tag=tag)


def t_op(c, fn, wait=(), tag=""):
    c.s.op("tensor", fn, wait=wait, inc=[("mm", 1)], tag=tag)


def dma(c, fn, wait=(), sem="ld", tag=""):
    c.s.op("sync", fn, wait=wait, inc=[(sem, 16)], tag=tag)


def gdma(c, fn, wait=(), sem="ld", tag=""):
    c.s.op("gpsimd", fn, wait=wait, inc=[(sem, 16)], tag=tag)



def plan(c):
    plan_setup(c)
    plan_attn_inputs(c)
    plan_wpartials(c)
    c.proj_hooks = mk_quant_hooks(c)
    plan_attn_front(c)
    c.s.barrier()
    plan_sinkhorn(c)
    plan_greedy(c)
    c.s.barrier()
    plan_scan_local(c)
    plan_scan_global(c)
    c.s.barrier()
    plan_out_proj(c)
    c.s.barrier()
    plan_ffn(c)
    plan_finish(c)


def plan_setup(c):
    T, d = c.T, c.d
    for nm in ["ident", "ones1", "onescol", "cid", "lnw", "lnb", "bcat"]:
        def ld(e, nm=nm):
            if nm in ("ident", "onescol"):
                return e.dma_start(out=c.T["wfst"][:, 0:P] if nm == "ident"
                                   else c.T["wfst"][:, P:P + 1],
                                   in_=c.d[nm][:, :])
            return e.dma_start(out=c.T[nm][:, :], in_=c.d[nm][:, :])
        dma(c, ld, sem="ls", tag=f"ld_{nm}")
    v_op(c, lambda e: e.tensor_copy(T["ident"][:, :], T["wfst"][:, 0:P]),
         wait=[("ls", c.s.val("ls"))])
    v_op(c, lambda e: e.tensor_copy(T["onescol"][:, :], T["wfst"][:, P:P + 1]))
    g_op(c, lambda e: e.iota(T["iota8"][:, :], pattern=[[1, 8]], base=0,
                             channel_multiplier=0,
                             allow_small_or_imprecise_dtypes=True))
    v_op(c, lambda e: e.memset(T["onesL"][:, :], 1.0))
    v_op(c, lambda e: e.memset(T["h_in"][:, :], 0.0))
    v_op(c, lambda e: e.tensor_copy(T["identf"][:, :], T["ident"][:, :]))
    v_op(c, lambda e: e.memset(T["onescf"][:, :], 1.0))
    v_op(c, lambda e: e.memset(T["chn"][:, :], -0.5))
    v_op(c, lambda e: e.memset(T["chp"][:, :], 0.5))
    v_op(c, lambda e: e.memset(T["magicb"][:, :], MAGIC))
    v_op(c, lambda e: e.memset(T["nmagicb"][:, :], -MAGIC))
    v_op(c, lambda e: e.memset(T["PIDX"][:, :], 0.0))
    v_op(c, lambda e: e.memset(T["AVm"][:, :], 0.0))
    v_op(c, lambda e: e.tensor_copy(T["lnbh"][:, :], T["lnb"][:, :]))
    v_op(c, lambda e: e.tensor_tensor(T["lnbl"][:, :], T["lnb"][:, :],
                                      T["lnbh"][:, :], ALU.subtract))
    dma(c, lambda e: e.dma_start(out=T["xstage"][0:33, :], in_=d["outWb"][:, :]),
        sem="ls")
    v_op(c, lambda e: e.tensor_copy(T["outWb_hi"][0:33, :], T["xstage"][0:33, :]),
         wait=[("ls", c.s.val("ls"))])
    v_op(c, lambda e: e.tensor_tensor(T["outWb_lo"][0:33, :], T["xstage"][0:33, :],
                                      T["outWb_hi"][0:33, :], ALU.subtract))
    dma(c, lambda e: e.dma_start(
        out=T["Wcat"][:, :].rearrange("p (c u) -> p c u", c=KD),
        in_=bass.AP(d["Wcat"].tensor, 0, [[U, P], [P * U, KD], [1, U]])),
        sem="ls")
    v_op(c, lambda e: e.tensor_copy(T["Wp_hi"][:, :], T["Wcat"][:, :]),
         wait=[("ls", c.s.val("ls"))])
    v_op(c, lambda e: e.tensor_tensor(T["Wp_lo"][:, :], T["Wcat"][:, :],
                                      T["Wp_hi"][:, :], ALU.subtract))
    c.ve_setup = c.s.val("ve")


def _wunits(c):
    d = c.d
    units = []
    for sb in range(2):
        units.append((lambda sb=sb: d["wgs"][:, sb * 2048:(sb + 1) * 2048],
                      2048, 0, sb * 2048, [[OF, P], [1, 2048]]))
    for sb in range(2):
        units.append((lambda sb=sb: d["wus"][:, sb * 2048:(sb + 1) * 2048],
                      2048, 1, P * OF + sb * 2048, [[OF, P], [1, 2048]]))
    for j in range(4):
        units.append((lambda j=j: d["wds"][j * P:(j + 1) * P, :],
                      1024, 2, 2 * P * OF + j * P * D, [[D, P], [1, D]]))
    return units


def plan_wpartials(c):
    T, d, s = c.T, c.d, c.s
    v_op(c, lambda e: e.memset(T["wsc"][:, :], 0.0))
    prev_ve = c.ve_setup
    for (src, width, col, _off, _dims) in _wunits(c):
        gdma(c, lambda e, src=src, width=width: e.dma_start(
            out=T["wfst"][:, 0:width], in_=src()),
            sem="lm", wait=[("ve", prev_ve)], tag="wpart_ld")
        v_op(c, lambda e, width=width: e.tensor_reduce(
            T["s1"][:, :], T["wfst"][:, 0:width], AXX, ALU.add,
            apply_absolute_value=True), wait=[("lm", s.val("lm"))])
        prev_ve = s.val("ve")
        v_op(c, lambda e, col=col: e.tensor_tensor(
            T["wsc"][:, col:col + 1], T["wsc"][:, col:col + 1],
            T["s1"][:, :], ALU.add))
    t_op(c, lambda e: e.matmul(c.PSB[7][0:1, 0:4], T["onescf"][:, :],
                               T["wsc"][:, 0:4], start=True, stop=True),
         wait=[("ve", s.val("ve"))], tag="meansum")
    v_op(c, lambda e: e.tensor_copy(T["wrow"][:, :], c.PSB[7][0:1, 0:4]),
         wait=[("mm", s.val("mm"))])
    gdma(c, lambda e: e.dma_start(out=d["ccm_in"][:, :], in_=T["wrow"][:, :]),
         sem="lm", wait=[("ve", s.val("ve"))], tag="means_st")
    c.s.op("gpsimd", lambda e: e.collective_compute(
        "AllGather", ALU.bypass, replica_groups=[list(range(NCORE))],
        ins=[d["ccm_in"][:, :]], outs=[d["ccm_out"][:, :]]),
        wait=[("lm", s.val("lm"))], inc=[("cc", 1)], tag="AGmeans")
    c.cc_m = s.val("cc")
    c.qprev_ve = s.val("ve")


def mk_quant_hooks(c):
    T, d, s = c.T, c.d, c.s
    units = _wunits(c)

    def combine():
        gdma(c, lambda e: e.dma_start(
            out=T["wrow8"][0:1, :],
            in_=bass.AP(d["ccm_out"].tensor, 0, [[1, 4], [4, NCORE]]),
            allow_slow_non_contiguous=True),
            sem="lm", wait=[("cc", c.cc_m)], tag="mcomb_ld")
        v_op(c, lambda e: e.tensor_reduce(
            T["wrow"][0:1, :],
            T["wrow8"][0:1, :].rearrange("p (c r) -> p c r", c=4),
            AXX, ALU.add), wait=[("lm", s.val("lm"))])
        t_op(c, lambda e: e.matmul(c.PSB[7][:, 0:4], T["ones1"][:, :],
                                   T["wrow"][:, :], start=True, stop=True),
             wait=[("ve", s.val("ve"))], tag="meanrep")
        v_op(c, lambda e: e.tensor_copy(T["wsc"][:, 0:4], c.PSB[7][:, 0:4]),
             wait=[("mm", s.val("mm"))])
        v_op(c, lambda e: e.tensor_scalar(T["wsc"][:, 0:4], T["wsc"][:, 0:4],
                                          1.0 / (D * OF), 1e-5, ALU.mult,
                                          ALU.max))
        v_op(c, lambda e: e.tensor_scalar(T["whalf"][:, 0:4], T["wsc"][:, 0:4],
                                          0.5, None, ALU.mult))
        v_op(c, lambda e: e.reciprocal(T["wsc"][:, 0:4], T["wsc"][:, 0:4]))
        c.ve_wsc = s.val("ve")

    def mk_qunit(u):
        def qunit():
            src, width, col, off, dims = units[u]
            gdma(c, lambda e: e.dma_start(out=T["wfst"][:, 0:width],
                                          in_=src()),
                 sem="lq", wait=[("ve", c.qprev_ve)], tag=f"q_ld{u}")
            lqv = s.val("lq")
            s_op(c, lambda e: e.activation(
                T["wist"][:, 0:width], T["wfst"][:, 0:width], AF.Sign,
                bias=T["chn"][:, :], scale=T["wsc"][:, col:col + 1]),
                wait=[("lq", lqv), ("ve", c.ve_wsc)])
            s_op(c, lambda e: e.activation(
                T["wfst"][:, 0:width], T["wfst"][:, 0:width], AF.Sign,
                bias=T["chp"][:, :], scale=T["wsc"][:, col:col + 1]))
            c.s.op("vector", lambda e: e.tensor_tensor(
                T["wist"][:, 0:width], T["wist"][:, 0:width],
                T["wfst"][:, 0:width], ALU.add),
                wait=[("se", s.val("se"))], inc=[("ve", 1)])
            c.qprev_ve = s.val("ve")
            dma(c, lambda e: e.dma_start(
                out=bass.AP(d["ccw_in"].tensor, off, dims),
                in_=T["wist"][:, 0:width]),
                sem="wq", wait=[("ve", s.val("ve"))], tag=f"q_st{u}")
        return qunit

    def ag2():
        c.s.op("gpsimd", lambda e: e.collective_compute(
            "AllGather", ALU.bypass, replica_groups=[list(range(NCORE))],
            ins=[d["ccw_in"][:, :]], outs=[d["ccw_out"][:, :]]),
            wait=[("wq", s.val("wq"))], inc=[("cc", 1)], tag="AGw")
        c.cc_w = s.val("cc")

    hooks = {0: [combine]}
    for u in range(8):
        hooks.setdefault(1 + u, []).append(mk_qunit(u))
    hooks.setdefault(9, []).append(ag2)
    return hooks



def plan_attn_inputs(c):
    T, d, s = c.T, c.d, c.s
    for k in range(KD):
        dma(c, lambda e, k=k: e.dma_start(
            out=T["xTstage"][:, :], in_=d["xT"][k * P:(k + 1) * P, :]),
            sem="lxT", wait=[("ve", s.val("ve"))], tag=f"xT{k}")
        s_op(c, lambda e, k=k: e.activation(
            T["xT_hi"][:, k * TCORE:(k + 1) * TCORE], T["xTstage"][:, :],
            AF.Identity), wait=[("lxT", s.val("lxT"))])
        v_op(c, lambda e, k=k: e.tensor_tensor(
            T["xT_lo"][:, k * TCORE:(k + 1) * TCORE], T["xTstage"][:, :],
            T["xT_hi"][:, k * TCORE:(k + 1) * TCORE], ALU.subtract),
            wait=[("se", s.val("se"))])
    last_use = {0: 0, 1: 0}
    for t in range(NT):
        buf = "xstage" if t % 2 == 0 else "xstage2"
        lsem = "lx0" if t % 2 == 0 else "lx1"
        dma(c, lambda e, t=t, buf=buf: e.dma_start(
            out=T[buf][:, :], in_=d["x"][t * P:(t + 1) * P, :]),
            sem=lsem, wait=[("ve", last_use[t % 2])], tag=f"x{t}")
        ldv = s.val(lsem)
        v_op(c, lambda e, t=t, buf=buf: e.tensor_reduce(
            T["stat1"][:, t:t + 1], T[buf][:, :], AXX, ALU.add),
            wait=[(lsem, ldv)])
        s_op(c, lambda e, t=t, buf=buf: e.activation(
            T["qf_t"][:, 0:D], T[buf][:, :], AF.Square),
            wait=[(lsem, ldv), ("ve", s.val("ve"))])
        v_op(c, lambda e, t=t: e.tensor_reduce(
            T["stat2"][:, t:t + 1], T["qf_t"][:, 0:D], AXX, ALU.add),
            wait=[("se", s.val("se"))])
        last_use[t % 2] = s.val("ve")
    c.ve_xsplit = s.val("ve")
    c.se_stats = s.val("se")
    c.ve_stats = s.val("ve")


def _rsqrt_inplace(c, ap_v, ap_r, ap_tmp):
    T, s = c.T, c.s
    s_op(c, lambda e: e.activation(ap_r, ap_v, AF.Sqrt), wait=[("ve", s.val("ve"))])
    v_op(c, lambda e: e.reciprocal(ap_r, ap_r), wait=[("se", s.val("se"))])
    for _ in range(2):
        v_op(c, lambda e: e.tensor_tensor(ap_tmp, ap_r, ap_r, ALU.mult))
        v_op(c, lambda e: e.tensor_tensor(ap_tmp, ap_tmp, ap_v, ALU.mult))
        v_op(c, lambda e: e.tensor_scalar(ap_tmp, ap_tmp, -0.5, 1.5,
                                          ALU.mult, ALU.add))
        v_op(c, lambda e: e.tensor_tensor(ap_r, ap_r, ap_tmp, ALU.mult))


def plan_attn_front(c):
    T, d, s, PSB = c.T, c.d, c.s, c.PSB

    def b2_mms(e):
        last = None
        for k in range(KD):
            for a_, b_ in (("lnbh", "Wp_hi"), ("lnbh", "Wp_lo"), ("lnbl", "Wp_hi")):
                last = e.matmul(PSB[7][0:1, 0:U], T[a_][:, k:k + 1],
                                T[b_][:, k * U:(k + 1) * U],
                                start=(k == 0 and a_ == "lnbh" and b_ == "Wp_hi"),
                                stop=(k == KD - 1 and a_ == "lnbl"))
        return last
    t_op(c, b2_mms, wait=[("ve", s.val("ve")), ("ls", s.val("ls"))], tag="b2")
    v_op(c, lambda e: e.tensor_tensor(T["b2row"][:, :], PSB[7][0:1, 0:U],
                                      T["bcat"][:, :], ALU.add),
         wait=[("mm", s.val("mm"))])

    v_op(c, lambda e: e.tensor_tensor(
        T["Wcat"][:, :].rearrange("p (c u) -> p c u", c=KD),
        T["Wcat"][:, :].rearrange("p (c u) -> p c u", c=KD),
        T["lnw"][:, :].unsqueeze(2).broadcast_to([P, KD, U]), ALU.mult))
    v_op(c, lambda e: e.tensor_copy(T["Wp_hi"][:, :], T["Wcat"][:, :]))
    v_op(c, lambda e: e.tensor_tensor(T["Wp_lo"][:, :], T["Wcat"][:, :],
                                      T["Wp_hi"][:, :], ALU.subtract))
    ve_wsplit = s.val("ve")

    def k1_mms(e):
        last = None
        for k in range(KD):
            for b_ in ("Wp_hi", "Wp_lo"):
                last = e.matmul(PSB[6][0:1, 0:U], T["onescol"][:, :],
                                T[b_][:, k * U:(k + 1) * U],
                                start=(k == 0 and b_ == "Wp_hi"),
                                stop=(k == KD - 1 and b_ == "Wp_lo"))
        return last
    t_op(c, k1_mms, wait=[("ve", s.val("ve"))], tag="k1")
    v_op(c, lambda e: e.tensor_copy(T["k1row"][:, :], PSB[6][0:1, 0:U]),
         wait=[("mm", s.val("mm"))])

    def rep_mms(e):
        e.matmul(PSB[5][:, 0:U], T["ones1"][:, :], T["k1row"][:, :],
                 start=True, stop=True)
        return e.matmul(PSB[4][:, 0:U], T["ones1"][:, :], T["b2row"][:, :],
                        start=True, stop=True)
    t_op(c, rep_mms, wait=[("ve", s.val("ve"))], tag="repk1b2")
    v_op(c, lambda e: e.tensor_copy(T["K1R"][:, :], PSB[5][:, 0:U]),
         wait=[("mm", s.val("mm"))])
    v_op(c, lambda e: e.tensor_copy(T["B2R"][:, :], PSB[4][:, 0:U]))

    v_op(c, lambda e: e.tensor_scalar(T["mu"][:, :], T["stat1"][:, :],
                                      1.0 / D, None, ALU.mult),
         wait=[("ve", c.ve_stats)])
    v_op(c, lambda e: e.tensor_tensor(T["stat1"][:, :], T["mu"][:, :],
                                      T["mu"][:, :], ALU.mult))
    v_op(c, lambda e: e.tensor_scalar(T["stat2"][:, :], T["stat2"][:, :],
                                      1.0 / D, LN_EPS, ALU.mult, ALU.add))
    v_op(c, lambda e: e.tensor_tensor(T["stat2"][:, :], T["stat2"][:, :],
                                      T["stat1"][:, :], ALU.subtract))
    _rsqrt_inplace(c, T["stat2"][:, :], T["rstd"][:, :], T["st_s"][:, :])
    v_op(c, lambda e: e.tensor_tensor(T["nrstdmu"][:, :], T["rstd"][:, :],
                                      T["mu"][:, :], ALU.mult))
    v_op(c, lambda e: e.tensor_scalar(T["nrstdmu"][:, :], T["nrstdmu"][:, :],
                                      -1.0, None, ALU.mult))
    ve_ready = s.val("ve")

    evac_se = {0: 0, 1: 0}
    for t in range(NT):
        for h in getattr(c, "proj_hooks", {}).get(t, []):
            h()
        bank = t % 2

        def proj_mms(e, t=t, bank=bank):
            last = None
            first = True
            for k in range(KD):
                xh = T["xT_hi"][:, k * TCORE + t * P:k * TCORE + (t + 1) * P]
                xl = T["xT_lo"][:, k * TCORE + t * P:k * TCORE + (t + 1) * P]
                wh = T["Wp_hi"][:, k * U:(k + 1) * U]
                wl = T["Wp_lo"][:, k * U:(k + 1) * U]
                last = e.matmul(PSB[bank][:, 0:U], xh, wh, start=first, stop=False)
                first = False
                last = e.matmul(PSB[bank][:, 0:U], xh, wl, start=False, stop=False)
                last = e.matmul(PSB[bank][:, 0:U], xl, wh, start=False,
                                stop=(k == KD - 1))
            return last
        t_op(c, proj_mms, wait=[("ve", ve_ready), ("se", evac_se[bank])],
             tag=f"proj{t}")
        mmv = s.val("mm")
        s_op(c, lambda e, t=t, bank=bank: e.activation(
            T["Ubuf"][:, t * U:(t + 1) * U], PSB[bank][:, 0:U], AF.Identity,
            scale=T["rstd"][:, t:t + 1]), wait=[("mm", mmv)])
        evac_se[bank] = s.val("se")
        c.s.op("vector", lambda e, t=t: e.scalar_tensor_tensor(
            T["Ubuf"][:, t * U:(t + 1) * U], T["K1R"][:, :],
            T["nrstdmu"][:, t:t + 1], T["Ubuf"][:, t * U:(t + 1) * U],
            ALU.mult, ALU.add), wait=[("se", s.val("se"))], inc=[("ve", 1)])
        v_op(c, lambda e, t=t: e.tensor_tensor(
            T["Ubuf"][:, t * U:(t + 1) * U], T["Ubuf"][:, t * U:(t + 1) * U],
            T["B2R"][:, :], ALU.add))

    c.wq_done = s.val("wq")
    if c.tap("Ubuf", [P, NT * U]):
        dma(c, lambda e: e.dma_start(out=c.taps["Ubuf"][:, :], in_=T["Ubuf"][:, :]),
            wait=[("ve", s.val("ve"))], sem="st")
    for nm_, sh_ in [("mu", [P, NT]), ("rstd", [P, NT]), ("K1R", [P, U]),
                     ("B2R", [P, U]), ("b2row", [1, U]), ("k1row", [1, U]),
                     ("Wcat", [P, KD * U]), ("stat2", [P, NT]),
                     ("bcat", [1, U])]:
        if c.tap(nm_, sh_):
            dma(c, lambda e, nm_=nm_: e.dma_start(out=c.taps[nm_][:, :],
                                                  in_=T[nm_][:, :]),
                wait=[("ve", s.val("ve")), ("se", s.val("se"))], sem="st")
    if c.tap("xh", [P, 2048], BF16):
        dma(c, lambda e: e.dma_start(
            out=c.taps["xh"][:, :],
            in_=T["xT_hi"][:, 0:2048]), wait=[("ve", s.val("ve"))], sem="st")

    uview = T["Ubuf"][:, :].rearrange("p (t u) -> p t u", t=NT)
    gate_in = uview[:, :, OFF_GATE:OFF_GATE + HN]
    sign_in = uview[:, :, OFF_SIGN:OFF_SIGN + HN]
    bin_in = uview[:, :, OFF_IN:OFF_IN + HN]
    perm_in = uview[:, :, OFF_PERM:OFF_PERM + 256]
    s_op(c, lambda e: e.activation(T["gam"][:, :], gate_in, AF.Sigmoid),
         wait=[("ve", s.val("ve"))])
    s_op(c, lambda e: e.activation(T["sgn"][:, :], sign_in, AF.Sigmoid))
    s_op(c, lambda e: e.activation(T["S"][:, :], perm_in, AF.Exp))
    v_op(c, lambda e: e.tensor_scalar(T["sgn"][:, :], T["sgn"][:, :], 2.0, -1.0,
                                      ALU.mult, ALU.add), wait=[("se", s.val("se"))])
    v_op(c, lambda e: e.tensor_tensor(T["dvec"][:, :], T["gam"][:, :],
                                      T["sgn"][:, :], ALU.mult))
    v_op(c, lambda e: e.tensor_tensor(T["cvec"][:, :], T["gam"][:, :], bin_in,
                                      ALU.mult))
    v_op(c, lambda e: e.tensor_tensor(T["cvec"][:, :], bin_in, T["cvec"][:, :],
                                      ALU.subtract))


def plan_sinkhorn(c):
    T, s = c.T, c.s
    NG = NT * H * 8
    Sv = T["S"][:, :].rearrange("p (g j) -> p g j", j=8)
    Sc = T["S"][:, :].rearrange("p (g i j) -> p g j i", i=8, j=8)
    rr_row = T["rrec"][:, :].rearrange("p (g o) -> p g o", o=1).broadcast_to([P, NG, 8])
    rr_col = (T["rrec"][:, :].rearrange("p (g j) -> p g j", j=8).unsqueeze(3)
              .broadcast_to([P, NT * H, 8, 8]))
    for it in range(5):
        v_op(c, lambda e: e.tensor_reduce(T["rsum"][:, :], Sv, AXX, ALU.add))
        v_op(c, lambda e: e.reciprocal(T["rrec"][:, :], T["rsum"][:, :]))
        v_op(c, lambda e: e.tensor_tensor(Sv, Sv, rr_row, ALU.mult))
        v_op(c, lambda e: e.tensor_reduce(T["rsum"][:, :], Sc, AXX, ALU.add))
        v_op(c, lambda e: e.reciprocal(T["rrec"][:, :], T["rsum"][:, :]))
        v_op(c, lambda e: e.tensor_tensor(Sc, Sc, rr_col, ALU.mult))
    if c.tap("S", [P, NT * 256]):
        dma(c, lambda e: e.dma_start(out=c.taps["S"][:, :], in_=T["S"][:, :]),
            wait=[("ve", s.val("ve"))], sem="st")


def plan_greedy(c):
    T, s = c.T, c.s
    NTH = NT * H
    v_op(c, lambda e: e.tensor_reduce(
        T["Rmax"][:, :], T["S"][:, :].rearrange("p (g j) -> p g j", j=8), AXX, ALU.max))
    Ri = T["Rmax"][:, :].rearrange("p (g i) -> p g i", i=8).unsqueeze(3).broadcast_to([P, NTH, 8, 8])
    Rp = T["Rmax"][:, :].rearrange("p (g i) -> p g i", i=8).unsqueeze(2).broadcast_to([P, NTH, 8, 8])
    gt_v = T["gtmp"][:, :].rearrange("p (g i j) -> p g i j", i=8, j=8)
    st_v = T["sc_t0"][:, :].rearrange("p (g i j) -> p g i j", i=8, j=8)[:, 0:NTH]
    ii = T["iota8"][:, :].unsqueeze(1).unsqueeze(3).broadcast_to([P, NTH, 8, 8])
    ip = T["iota8"][:, :].unsqueeze(1).unsqueeze(1).broadcast_to([P, NTH, 8, 8])
    v_op(c, lambda e: e.tensor_tensor(gt_v, Rp, Ri, ALU.is_gt))
    v_op(c, lambda e: e.tensor_reduce(T["grk"][:, :], gt_v, AXX, ALU.add))
    v_op(c, lambda e: e.tensor_tensor(gt_v, Rp, Ri, ALU.is_equal))
    v_op(c, lambda e: e.tensor_tensor(st_v, ii, ip, ALU.is_gt))
    v_op(c, lambda e: e.tensor_tensor(gt_v, gt_v, st_v, ALU.mult))
    v_op(c, lambda e: e.tensor_reduce(T["gsm"][:, :], gt_v, AXX, ALU.add))
    v_op(c, lambda e: e.tensor_tensor(T["grk"][:, :], T["grk"][:, :],
                                      T["gsm"][:, :], ALU.add))

    Sv2 = T["S"][:, :].rearrange("p (g i j) -> p g j i", i=8, j=8)
    gsel_bc = T["gsel"][:, :].rearrange("p (g i) -> p g i", i=8).unsqueeze(2).broadcast_to(
        [P, NTH, 8, 8])
    gtv2 = T["gtmp"][:, :].rearrange("p (g j i) -> p g j i", i=8, j=8)
    for r in range(8):
        v_op(c, lambda e, r=r: e.tensor_single_scalar(
            T["gsel"][:, :], T["grk"][:, :], float(r), ALU.is_equal))
        v_op(c, lambda e: e.tensor_tensor(gtv2, Sv2, gsel_bc, ALU.mult))
        v_op(c, lambda e: e.tensor_reduce(T["gsr"][:, :], gtv2, AXX, ALU.add))
        v_op(c, lambda e: e.tensor_tensor(T["gsm"][:, :], T["gsr"][:, :],
                                          T["AVm"][:, :], ALU.add))
        v_op(c, lambda e: e.tensor_reduce(
            T["gm1"][:, :], T["gsm"][:, :].rearrange("p (g j) -> p g j", j=8),
            AXX, ALU.max))
        v_op(c, lambda e: e.tensor_tensor(
            T["gcs"][:, :], T["gsm"][:, :],
            T["gm1"][:, :].unsqueeze(2).broadcast_to([P, NTH, 8]),
            ALU.is_ge))
        v_op(c, lambda e: e.scalar_tensor_tensor(
            T["gsm"][:, :], T["gcs"][:, :], -16.0,
            T["iota8"][:, :].unsqueeze(1).broadcast_to([P, NTH, 8]),
            ALU.mult, ALU.add))
        v_op(c, lambda e: e.tensor_reduce(
            T["gci"][:, :], T["gsm"][:, :].rearrange("p (g j) -> p g j", j=8),
            AXX, ALU.min))
        v_op(c, lambda e: e.tensor_scalar(T["gci"][:, :], T["gci"][:, :],
                                          1.0, 16.0, ALU.mult, ALU.add))
        v_op(c, lambda e: e.tensor_tensor(
            T["gcs"][:, :],
            T["iota8"][:, :].unsqueeze(1).broadcast_to([P, NTH, 8]),
            T["gci"][:, :].unsqueeze(2).broadcast_to([P, NTH, 8]),
            ALU.is_equal))
        v_op(c, lambda e: e.tensor_tensor(
            T["gsm"][:, :], T["gsel"][:, :],
            T["gci"][:, :].unsqueeze(2).broadcast_to([P, NTH, 8]),
            ALU.mult))
        v_op(c, lambda e: e.tensor_tensor(T["PIDX"][:, :], T["PIDX"][:, :],
                                          T["gsm"][:, :], ALU.add))
        v_op(c, lambda e: e.scalar_tensor_tensor(
            T["Rmax"][:, :], T["gsel"][:, :], -BIG, T["Rmax"][:, :],
            ALU.mult, ALU.add))
        v_op(c, lambda e: e.scalar_tensor_tensor(
            T["AVm"][:, :], T["gcs"][:, :], -BIG, T["AVm"][:, :],
            ALU.mult, ALU.add))
    if c.tap("PIDX", [P, NT * HN]):
        dma(c, lambda e: e.dma_start(out=c.taps["PIDX"][:, :], in_=T["PIDX"][:, :]),
            wait=[("ve", s.val("ve"))], sem="st")


def plan_scan_local(c):
    T, s, d = c.T, c.s, c.d
    for t in range(NT):
        for srcn, dstn in (("dvec", "sc_d"), ("cvec", "sc_c"), ("PIDX", "sc_p")):
            def mv(e, t=t, srcn=srcn, dstn=dstn):
                src = c.T[srcn][:, t * HN:(t + 1) * HN]
                dst = c.T[dstn][t * 8:(t + 1) * 8, :].rearrange(
                    "q (k f) -> q k f", f=HN)
                return e.dma_start(out=dst, in_=src)
            dma(c, mv, sem="lay", wait=[("ve", s.val("ve"))] if t == 0 else (),
                tag="lay")
    lay_ld = s.val("lay")

    pv = T["sc_p"][:, :].rearrange("p (k h i) -> p k h i", h=H, i=8)
    sv = T["sc_sig"][:, :].rearrange("p (k h i) -> p k h i", h=H, i=8)
    v_op(c, lambda e: e.tensor_copy(sv[:, 0], pv[:, 0]), wait=[("lay", lay_ld)])
    po = T["sc_po"][:, :].rearrange("p (h i m) -> p h i m", h=H, i=8, m=8)
    t0 = T["sc_t0"][:, :].rearrange("p (k h i m) -> p k h i m", k=LCH, h=H, i=8, m=8)
    for k in range(1, LCH):
        v_op(c, lambda e, k=k: e.tensor_tensor(
            po, pv[:, k].unsqueeze(3).broadcast_to([P, H, 8, 8]),
            T["iota8"][:, :].unsqueeze(1).unsqueeze(1).broadcast_to([P, H, 8, 8]),
            ALU.is_equal))
        v_op(c, lambda e, k=k: e.tensor_tensor(
            t0[:, 0], po,
            sv[:, k - 1].unsqueeze(2).broadcast_to([P, H, 8, 8]),
            ALU.mult))
        v_op(c, lambda e, k=k: e.tensor_reduce(sv[:, k], t0[:, 0], AXX, ALU.add))

    KH = LCH * H
    oh4 = T["sc_oh"][:, :].rearrange("p (q i m) -> p q i m", i=8, m=8)
    sv4 = T["sc_sig"][:, :].rearrange("p (q i) -> p q i", i=8)
    v_op(c, lambda e: e.tensor_tensor(
        oh4, sv4.unsqueeze(3).broadcast_to([P, KH, 8, 8]),
        T["iota8"][:, :].unsqueeze(1).unsqueeze(1).broadcast_to([P, KH, 8, 8]),
        ALU.is_equal))

    dv4 = T["sc_d"][:, :].rearrange("p (q i) -> p q i", i=8)
    cv4 = T["sc_c"][:, :].rearrange("p (q i) -> p q i", i=8)
    dh4 = T["sc_dh"][:, :].rearrange("p (q m) -> p q m", m=8)
    ch4 = T["sc_ch"][:, :].rearrange("p (q m) -> p q m", m=8)
    ohT4 = T["sc_oh"][:, :].rearrange("p (q i m) -> p q m i", i=8, m=8)
    t0m4 = T["sc_t0"][:, :].rearrange("p (q m i) -> p q m i", m=8, i=8)
    v_op(c, lambda e: e.tensor_tensor(
        t0m4, ohT4, dv4.unsqueeze(2).broadcast_to([P, KH, 8, 8]), ALU.mult))
    v_op(c, lambda e: e.tensor_reduce(dh4, t0m4, AXX, ALU.add))
    v_op(c, lambda e: e.tensor_tensor(
        t0m4, ohT4, cv4.unsqueeze(2).broadcast_to([P, KH, 8, 8]), ALU.mult))
    v_op(c, lambda e: e.tensor_reduce(ch4, t0m4, AXX, ALU.add))

    def hmslice(name, off):
        return (T[name][:, :].rearrange("p (k f) -> p k f", f=HN)
                [:, :, off:off + 1].squeeze(2))

    for h in range(H):
        for m in range(8):
            off = h * 8 + m
            v_op(c, lambda e, off=off: e.tensor_tensor_scan(
                hmslice("sc_pi", off), hmslice("sc_dh", off),
                hmslice("onesL", off), 1.0, ALU.mult, ALU.mult))
            v_op(c, lambda e, off=off: e.tensor_tensor_scan(
                hmslice("sc_sh", off), hmslice("sc_dh", off),
                hmslice("sc_ch", off), 0.0, ALU.mult, ALU.add))

    oh = T["sc_oh"][:, :].rearrange("p (k h i m) -> p k h i m", k=LCH, h=H,
                                    i=8, m=8)
    ohL = oh[:, LCH - 1]
    v_op(c, lambda e: e.tensor_copy(
        T["sc_sum"][:, 0:32].rearrange("p (h i) -> p h i", h=H), sv[:, LCH - 1]))
    piL = T["sc_pi"][:, :].rearrange("p (k h m) -> p k h m", h=H, m=8)[:, LCH - 1]
    shL = T["sc_sh"][:, :].rearrange("p (k h m) -> p k h m", h=H, m=8)[:, LCH - 1]
    t0g = T["sc_t0"][:, :].rearrange("p (k h i m) -> p k h i m", k=LCH, h=H, i=8, m=8)[:, 0]
    v_op(c, lambda e: e.tensor_tensor(
        t0g, ohL, piL.unsqueeze(2).broadcast_to([P, H, 8, 8]),
        ALU.mult))
    v_op(c, lambda e: e.tensor_reduce(
        T["sc_sum"][:, 32:64].rearrange("p (h i) -> p h i", h=H), t0g, AXX,
        ALU.add))
    v_op(c, lambda e: e.tensor_tensor(
        t0g, ohL, shL.unsqueeze(2).broadcast_to([P, H, 8, 8]),
        ALU.mult))
    v_op(c, lambda e: e.tensor_reduce(
        T["sc_sum"][:, 64:96].rearrange("p (h i) -> p h i", h=H), t0g, AXX,
        ALU.add))

    for srco, dstn in ((0, "l2_sg"), (32, "l2_rho"), (64, "l2_s")):
        def mv2(e, srco=srco, dstn=dstn):
            src = c.T["sc_sum"][:, srco:srco + 32]
            dst = c.T[dstn][:, :].rearrange("q (r f) -> q r f", f=HN)
            return e.dma_start(out=dst, in_=src)
        dma(c, mv2, sem="lay", wait=[("ve", s.val("ve"))], tag="lay2")
    lay2_ld = s.val("lay")

    pv2 = T["l2_sg"][:, :].rearrange("p (k h i) -> p k h i", h=H, i=8)
    sig2 = T["l2_sig"][:, :].rearrange("p (k h i) -> p k h i", h=H, i=8)
    po2 = T["l2_po"][:, :].rearrange("p (h i m) -> p h i m", h=H, i=8, m=8)
    t02 = T["l2_t0"][:, :].rearrange("p (k h i m) -> p k h i m", k=LCH, h=H, i=8, m=8)
    v_op(c, lambda e: e.tensor_copy(sig2[:, 0], pv2[:, 0]), wait=[("lay", lay2_ld)])
    for k in range(1, LCH):
        v_op(c, lambda e, k=k: e.tensor_tensor(
            po2, pv2[:, k].unsqueeze(3).broadcast_to([8, H, 8, 8]),
            T["iota8"][0:8, :].unsqueeze(1).unsqueeze(1).broadcast_to(
                [8, H, 8, 8]), ALU.is_equal))
        v_op(c, lambda e, k=k: e.tensor_tensor(
            t02[:, 0], po2,
            sig2[:, k - 1].unsqueeze(2).broadcast_to(
                [8, H, 8, 8]), ALU.mult))
        v_op(c, lambda e, k=k: e.tensor_reduce(sig2[:, k], t02[:, 0], AXX, ALU.add))
    KH = LCH * H
    oh24 = T["l2_oh"][:, :].rearrange("p (q i m) -> p q i m", i=8, m=8)
    sg24 = T["l2_sig"][:, :].rearrange("p (q i) -> p q i", i=8)
    v_op(c, lambda e: e.tensor_tensor(
        oh24, sg24.unsqueeze(3).broadcast_to([8, KH, 8, 8]),
        T["iota8"][0:8, :].unsqueeze(1).unsqueeze(1).broadcast_to([8, KH, 8, 8]),
        ALU.is_equal))
    dv24 = T["l2_rho"][:, :].rearrange("p (q i) -> p q i", i=8)
    cv24 = T["l2_s"][:, :].rearrange("p (q i) -> p q i", i=8)
    dh24 = T["l2_dh"][:, :].rearrange("p (q m) -> p q m", m=8)
    ch24 = T["l2_ch"][:, :].rearrange("p (q m) -> p q m", m=8)
    oh2T4 = T["l2_oh"][:, :].rearrange("p (q i m) -> p q m i", i=8, m=8)
    t02m4 = T["l2_t0"][:, :].rearrange("p (q m i) -> p q m i", m=8, i=8)
    v_op(c, lambda e: e.tensor_tensor(
        t02m4, oh2T4, dv24.unsqueeze(2).broadcast_to([8, KH, 8, 8]), ALU.mult))
    v_op(c, lambda e: e.tensor_reduce(dh24, t02m4, AXX, ALU.add))
    v_op(c, lambda e: e.tensor_tensor(
        t02m4, oh2T4, cv24.unsqueeze(2).broadcast_to([8, KH, 8, 8]), ALU.mult))
    v_op(c, lambda e: e.tensor_reduce(ch24, t02m4, AXX, ALU.add))
    def hmslice2(name, off):
        return (T[name][0:8, :].rearrange("p (k f) -> p k f", f=HN)
                [:, :, off:off + 1].squeeze(2))

    for h in range(H):
        for m in range(8):
            off = h * 8 + m
            v_op(c, lambda e, off=off: e.tensor_tensor_scan(
                hmslice2("l2_pi", off), hmslice2("l2_dh", off),
                hmslice2("onesL", off), 1.0, ALU.mult, ALU.mult))
            v_op(c, lambda e, off=off: e.tensor_tensor_scan(
                hmslice2("l2_sh", off), hmslice2("l2_dh", off),
                hmslice2("l2_ch", off), 0.0, ALU.mult, ALU.add))

    oh2L = (T["l2_oh"][:, :].rearrange("p (k h i m) -> p k h i m", k=LCH, h=H,
                                       i=8, m=8)[:, LCH - 1])
    pi2L = T["l2_pi"][:, :].rearrange("p (k h m) -> p k h m", h=H, m=8)[:, LCH - 1]
    sh2L = T["l2_sh"][:, :].rearrange("p (k h m) -> p k h m", h=H, m=8)[:, LCH - 1]
    t02g = (T["l2_t0"][:, :].rearrange("p (k h i m) -> p k h i m", k=LCH, h=H,
                                       i=8, m=8)[:, 0])
    v_op(c, lambda e: e.tensor_tensor(
        t02g, oh2L, pi2L.unsqueeze(2).broadcast_to([8, H, 8, 8]),
        ALU.mult))
    v_op(c, lambda e: e.tensor_reduce(
        T["l2_t0"][:, 1024:1056].rearrange("p (h i) -> p h i", h=H),
        t02g, AXX, ALU.add))
    v_op(c, lambda e: e.tensor_tensor(
        t02g, oh2L, sh2L.unsqueeze(2).broadcast_to([8, H, 8, 8]),
        ALU.mult))
    v_op(c, lambda e: e.tensor_reduce(
        T["l2_t0"][:, 1056:1088].rearrange("p (h i) -> p h i", h=H),
        t02g, AXX, ALU.add))
    ve_sum = s.val("ve")
    gdma(c, lambda e: e.dma_start(out=d["cc_in"][0:1, 0:32],
                                  in_=sig2[7:8, LCH - 1].rearrange(
                                      "p h i -> p (h i)")),
         wait=[("ve", ve_sum)], sem="st")
    gdma(c, lambda e: e.dma_start(out=d["cc_in"][0:1, 32:64],
                                  in_=T["l2_t0"][7:8, 1024:1056]), sem="st")
    gdma(c, lambda e: e.dma_start(out=d["cc_in"][0:1, 64:96],
                                  in_=T["l2_t0"][7:8, 1056:1088]), sem="st")


def plan_scan_global(c):
    T, s, d, PSB = c.T, c.s, c.d, c.PSB
    c.s.op("gpsimd", lambda e: e.collective_compute(
        "AllGather", ALU.bypass, replica_groups=[list(range(NCORE))],
        ins=[d["cc_in"][:, :]], outs=[d["cc_out"][:, :]]),
        wait=[("st", s.val("st"))], inc=[("cc", 1)], tag="AG")
    gdma(c, lambda e: e.dma_start(out=T["gsum"][0:1, :], in_=d["cc_out"][:, :]),
         wait=[("cc", s.val("cc"))], sem="lg")
    ag_ld = s.val("lg")

    v_op(c, lambda e: e.tensor_copy(
        T["gcur"][0:1, 0:32].rearrange("p (h i) -> p h i", h=H),
        T["iota8"][0:1, :].unsqueeze(1).broadcast_to([1, H, 8])))
    v_op(c, lambda e: e.memset(T["gcur"][0:1, 32:64], 1.0))
    v_op(c, lambda e: e.memset(T["gcur"][0:1, 64:96], 0.0))
    c.s.op("vector", lambda e: e.nop(), wait=[("lg", ag_ld)],
           inc=[("ve", 1)])
    gc_s = T["gcur"][0:1, 0:32].rearrange("p (h i) -> p h i", h=H)
    gc_r = T["gcur"][0:1, 32:64].rearrange("p (h i) -> p h i", h=H)
    gc_o = T["gcur"][0:1, 64:96].rearrange("p (h i) -> p h i", h=H)
    ohg = T["goh"][:, :].rearrange("p (h i m) -> p h i m", h=H, i=8, m=8)
    t0g = T["gt0"][:, :].rearrange("p (h i m) -> p h i m", h=H, i=8, m=8)
    for r in range(NCORE - 1):
        sr_s = (T["gsum"][0:1, r * 96:r * 96 + 32]
                .rearrange("p (h i) -> p h i", h=H))
        sr_r = (T["gsum"][0:1, r * 96 + 32:r * 96 + 64]
                .rearrange("p (h i) -> p h i", h=H))
        sr_o = (T["gsum"][0:1, r * 96 + 64:r * 96 + 96]
                .rearrange("p (h i) -> p h i", h=H))
        v_op(c, lambda e, sr_s=sr_s: e.tensor_tensor(
            ohg, sr_s.unsqueeze(3).broadcast_to([1, H, 8, 8]),
            T["iota8"][0:1, :].unsqueeze(1).unsqueeze(1).broadcast_to(
                [1, H, 8, 8]), ALU.is_equal))

        def gath(dst, srcv):
            v_op(c, lambda e, srcv=srcv: e.tensor_tensor(
                t0g, ohg, srcv.unsqueeze(2).broadcast_to(
                    [1, H, 8, 8]), ALU.mult))
            v_op(c, lambda e, dst=dst: e.tensor_reduce(dst, t0g, AXX, ALU.add))

        gn_s = T["gnx"][0:1, 0:32].rearrange("p (h i) -> p h i", h=H)
        gn_r = T["gnx"][0:1, 32:64].rearrange("p (h i) -> p h i", h=H)
        gn_o = T["gnx"][0:1, 64:96].rearrange("p (h i) -> p h i", h=H)
        gath(gn_s, gc_s)
        gath(gn_r, gc_r)
        gath(gn_o, gc_o)
        v_op(c, lambda e, sr_r=sr_r: e.tensor_tensor(gn_r, gn_r, sr_r, ALU.mult))
        v_op(c, lambda e, sr_r=sr_r: e.tensor_tensor(gn_o, gn_o, sr_r, ALU.mult))
        v_op(c, lambda e, sr_o=sr_o: e.tensor_tensor(gn_o, gn_o, sr_o, ALU.add))
        v_op(c, lambda e, r=r: e.tensor_single_scalar(
            T["gmsk"][:, :], T["cid"][:, :], float(r), ALU.is_gt))
        for a_, b_ in ((gn_s, gc_s), (gn_r, gc_r), (gn_o, gc_o)):
            v_op(c, lambda e, a_=a_, b_=b_: e.tensor_tensor(
                a_, a_, b_, ALU.subtract))
            v_op(c, lambda e, a_=a_: e.tensor_tensor(
                a_, a_,
                T["gmsk"][0:1, 0:1].unsqueeze(2)
                .broadcast_to([1, H, 8]), ALU.mult))
            v_op(c, lambda e, a_=a_, b_=b_: e.tensor_tensor(
                b_, b_, a_, ALU.add))

    t_op(c, lambda e: e.matmul(c.PSB[3][:, 0:HN], T["ones1"][:, :],
                               T["gcur"][0:1, 64:96], start=True, stop=True),
         wait=[("ve", s.val("ve"))], tag="rephin")
    v_op(c, lambda e: e.tensor_copy(T["h_in"][:, :], c.PSB[3][:, 0:HN]),
         wait=[("mm", s.val("mm"))])

    hv = T["h_in"][0:8, :].unsqueeze(1)
    pi2f = T["l2_pi"][:, :].rearrange("p (k f) -> p k f", f=HN)
    sh2f = T["l2_sh"][:, :].rearrange("p (k f) -> p k f", f=HN)
    dh2f = T["l2_dh"][:, :].rearrange("p (k f) -> p k f", f=HN)
    v_op(c, lambda e: e.tensor_tensor(
        dh2f, pi2f, hv.broadcast_to([8, LCH, HN]), ALU.mult))
    v_op(c, lambda e: e.tensor_tensor(dh2f, dh2f, sh2f, ALU.add))
    oh24b = T["l2_oh"][:, :].rearrange("p (q i m) -> p q i m", i=8, m=8)
    t024b = T["l2_t0"][:, :].rearrange("p (q i m) -> p q i m", i=8, m=8)
    dh24b = T["l2_dh"][:, :].rearrange("p (q m) -> p q m", m=8)
    v_op(c, lambda e: e.tensor_tensor(
        t024b, oh24b, dh24b.unsqueeze(2).broadcast_to([8, LCH * H, 8, 8]),
        ALU.mult))
    v_op(c, lambda e: e.tensor_reduce(
        T["l2_v"][:, :].rearrange("p (q i) -> p q i", i=8), t024b, AXX, ALU.add))
    ve_v = s.val("ve")

    dma(c, lambda e: e.dma_start(
        out=T["e_full"][:, :],
        in_=T["l2_v"][:, :].rearrange("q (r f) -> q r f", f=HN)),
        sem="lg", wait=[("ve", ve_v)], tag="efull")
    dma(c, lambda e: e.dma_start(out=T["e_ch"][1:P, :], in_=T["e_full"][0:P - 1, :]),
        sem="lg", wait=[("lg", s.val("lg"))])
    v_op(c, lambda e: e.tensor_copy(T["e_ch"][0:1, :], T["h_in"][0:1, :]),
         wait=[("lg", s.val("lg"))])

    ev = T["e_ch"][:, :].unsqueeze(1)
    pi1f = T["sc_pi"][:, :].rearrange("p (k f) -> p k f", f=HN)
    sh1f = T["sc_sh"][:, :].rearrange("p (k f) -> p k f", f=HN)
    hhf = T["sc_hh"][:, :].rearrange("p (k f) -> p k f", f=HN)
    v_op(c, lambda e: e.tensor_tensor(
        hhf, pi1f, ev.broadcast_to([P, LCH, HN]), ALU.mult))
    v_op(c, lambda e: e.tensor_tensor(hhf, hhf, sh1f, ALU.add))
    oh14 = T["sc_oh"][:, :].rearrange("p (q i m) -> p q i m", i=8, m=8)
    t014 = T["sc_t0"][:, :].rearrange("p (q i m) -> p q i m", i=8, m=8)
    hh14 = T["sc_hh"][:, :].rearrange("p (q m) -> p q m", m=8)
    v_op(c, lambda e: e.tensor_tensor(
        t014, oh14, hh14.unsqueeze(2).broadcast_to([P, LCH * H, 8, 8]),
        ALU.mult))
    v_op(c, lambda e: e.tensor_reduce(
        T["sc_ys"][:, :].rearrange("p (q i) -> p q i", i=8), t014, AXX, ALU.add))
    ve_ys = s.val("ve")
    if c.tap("ys", [P, LCH * HN]):
        dma(c, lambda e: e.dma_start(out=c.taps["ys"][:, :], in_=T["sc_ys"][:, :]),
            wait=[("ve", ve_ys)], sem="st")

    v_op(c, lambda e: e.tensor_copy(T["ys_hi"][:, :], T["sc_ys"][:, :]))
    v_op(c, lambda e: e.tensor_tensor(T["ys_lo"][:, :], T["sc_ys"][:, :],
                                      T["ys_hi"][:, :], ALU.subtract))
    ve_split = s.val("ve")
    for nm_src, nm_dst, bank in (("ys_hi", "ysT_hi", 4), ("ys_lo", "ysT_lo", 5)):
        for k4 in range(4):
            def trs(e, nm_src=nm_src, k4=k4, bank=bank):
                last = None
                for kk in range(4):
                    k = k4 * 4 + kk
                    inv = (T[nm_src][:, :].rearrange("p (k u) -> p k u", u=HN)
                           [:, k:k + 1, :].squeeze(1))
                    psv = PSB[bank][:, :].bitcast(BF16)
                    last = e.transpose(psv[0:HN, kk * P:(kk + 1) * P],
                                       inv, T["ident"][:, :])
                return last
            t_op(c, trs, wait=[("ve", max(ve_split, s.val("ve")))],
                 tag=f"ysT_{nm_src}_{k4}")
            mmt = s.val("mm")

            def evac(e, nm_dst=nm_dst, k4=k4, bank=bank):
                outv = (T[nm_dst][0:HN, :].rearrange("p (c k) -> p k c", k=LCH)
                        [:, k4 * 4:(k4 + 1) * 4, :])
                psv = PSB[bank][:, :].bitcast(BF16)[0:HN, 0:4 * P]
                return e.tensor_copy(outv, psv.rearrange(
                    "p (kk c) -> p kk c", kk=4))
            v_op(c, evac, wait=[("mm", mmt)])
    v_op(c, lambda e: e.memset(T["ysT_hi"][32:33, :], 1.0))
    v_op(c, lambda e: e.memset(T["ysT_lo"][32:33, :], 0.0))


def plan_out_proj(c):
    T, s, d, PSB = c.T, c.s, c.d, c.PSB
    c.s.op("vector", lambda e: e.nop(), wait=[("wq", c.wq_done)])

    def ldg0(e):
        inap = bass.AP(d["ccw_out"].tensor, 0, [[OF, P], [SEGW, KD], [1, FT]])
        return e.dma_start(out=T["wstage"].rearrange(
            "p (k f) -> p k f", k=KD), in_=inap)

    def ldu0(e):
        inap = bass.AP(d["ccw_out"].tensor, P * OF,
                       [[OF, P], [SEGW, KD], [1, FT]])
        return e.dma_start(out=T["wstage2"].rearrange(
            "p (k f) -> p k f", k=KD), in_=inap)
    dma(c, ldg0, sem="lw0", wait=[("ve", s.val("ve")), ("cc", c.cc_w)],
        tag="wg0pre")
    gdma(c, ldu0, sem="lw1", wait=[("ve", s.val("ve")), ("cc", c.cc_w)],
         tag="wu0pre")
    c.pre_lw0, c.pre_lw1 = s.val("lw0"), s.val("lw1")

    def zt(t):
        t_ = T["zA"] if t < 8 else T["zB"]
        return t_[:, (t % 8) * D:(t % 8 + 1) * D]

    evac_ve = {}
    xuse = {0: 0, 1: 0}
    for t in range(NT):
        buf = "xstage" if t % 2 == 0 else "xstage2"
        lsem = "lx0" if t % 2 == 0 else "lx1"
        dma(c, lambda e, t=t, buf=buf: e.dma_start(
            out=T[buf][:, :], in_=d["x"][t * P:(t + 1) * P, :]),
            sem=lsem, wait=[("ve", xuse[t % 2])], tag=f"xr{t}")
        ld_x = s.val(lsem)
        for dc in range(2):
            bank = (t % 2) * 2 + dc

            def mms(e, t=t, dc=dc, bank=bank):
                last = None
                first = True
                for a_, b_ in (("ysT_hi", "outWb_hi"), ("ysT_hi", "outWb_lo"),
                               ("ysT_lo", "outWb_hi")):
                    last = e.matmul(
                        PSB[bank][:, 0:512],
                        T[a_][0:33, t * P:(t + 1) * P],
                        T[b_][0:33, dc * 512:(dc + 1) * 512],
                        start=first, stop=(a_ == "ysT_lo"))
                    first = False
                return last
            t_op(c, mms, wait=[("ve", max(s.val("ve"), evac_ve.get(bank, 0)))],
                 tag=f"oproj{t}_{dc}")
            mmv = s.val("mm")
            c.s.op("vector", lambda e, t=t, dc=dc, bank=bank, buf=buf:
                   e.scalar_tensor_tensor(
                       zt(t)[:, dc * 512:(dc + 1) * 512],
                       T[buf][:, dc * 512:(dc + 1) * 512], 2.0,
                       PSB[bank][:, 0:512], ALU.mult, ALU.add),
                   wait=[("mm", mmv), (lsem, ld_x)], inc=[("ve", 1)])
            evac_ve[bank] = s.val("ve")
        xuse[t % 2] = s.val("ve")
        v_op(c, lambda e, t=t: e.tensor_reduce(T["stat1"][:, t:t + 1],
                                               zt(t), AXX, ALU.add))
        s_op(c, lambda e, t=t: e.activation(T["qf_t"][:, 0:D], zt(t),
                                            AF.Square),
             wait=[("ve", s.val("ve"))])
        v_op(c, lambda e, t=t: e.tensor_reduce(T["stat2"][:, t:t + 1],
                                               T["qf_t"][:, 0:D], AXX, ALU.add),
             wait=[("se", s.val("se"))])

    v_op(c, lambda e: e.tensor_scalar(T["mu"][:, :], T["stat1"][:, :],
                                      1.0 / D, None, ALU.mult))
    v_op(c, lambda e: e.tensor_tensor(T["stat1"][:, :], T["mu"][:, :],
                                      T["mu"][:, :], ALU.mult))
    v_op(c, lambda e: e.tensor_scalar(T["stat2"][:, :], T["stat2"][:, :],
                                      1.0 / D, LN_EPS, ALU.mult, ALU.add))
    v_op(c, lambda e: e.tensor_tensor(T["stat2"][:, :], T["stat2"][:, :],
                                      T["stat1"][:, :], ALU.subtract))
    _rsqrt_inplace(c, T["stat2"][:, :], T["rstd"][:, :], T["st_s"][:, :])

    ob_se = {"yout": 0, "yt1": 0}
    for t in range(NT):
        ob = "yout" if t % 2 == 0 else "yt1"
        v_op(c, lambda e, t=t, ob=ob: e.tensor_scalar(
            T[ob][:, :], zt(t), T["mu"][:, t:t + 1],
            T["rstd"][:, t:t + 1], ALU.subtract, ALU.mult),
            wait=[("qa", 16 * max(0, t - 1)), ("se", ob_se[ob])])
        ve_y = s.val("ve")
        dma(c, lambda e, t=t, ob=ob: e.dma_start(
            out=d["ybuf"][t * P:(t + 1) * P, :], in_=T[ob][:, :]),
            wait=[("ve", ve_y)], sem="qa", tag=f"ystore{t}")
        v_op(c, lambda e, t=t, ob=ob: e.tensor_reduce(
            T["sa"][:, t:t + 1], T[ob][:, :], AXX, ALU.max,
            apply_absolute_value=True))
        v_op(c, lambda e, t=t: e.tensor_single_scalar(
            T["sa"][:, t:t + 1], T["sa"][:, t:t + 1], 1e-5, ALU.max))
        v_op(c, lambda e, t=t: e.reciprocal(T["sa"][:, t:t + 1],
                                            T["sa"][:, t:t + 1]))
        v_op(c, lambda e, t=t: e.tensor_scalar(T["sa"][:, t:t + 1],
                                               T["sa"][:, t:t + 1], 127.0,
                                               None, ALU.mult))
        s_op(c, lambda e, t=t, ob=ob: e.activation(
            T["qv_t"][:, :], T[ob][:, :], AF.Identity,
            bias=T["magicb"][:, :], scale=T["sa"][:, t:t + 1]),
            wait=[("ve", s.val("ve"))])
        ob_se[ob] = s.val("se")
        s_op(c, lambda e: e.activation(T["xq_t"][:, :], T["qv_t"][:, :],
                                       AF.Identity, bias=T["nmagicb"][:, :]),
             wait=[("mm", getattr(c, "op_tr_mm", 0))])
        se_q = s.val("se")
        for half in range(2):
            bank = 4 + half

            def trs(e, t=t, half=half, bank=bank):
                last = None
                psv = PSB[bank][:, :].bitcast(BF16)
                for j in range(4):
                    cch = half * 4 + j
                    last = e.transpose(
                        psv[:, j * P:(j + 1) * P],
                        T["xq_t"][:, cch * P:(cch + 1) * P], T["ident"][:, :])
                return last
            t_op(c, trs, wait=[("se", se_q),
                               ("ve", getattr(c, "op_tr_ve", {}).get(half, 0))],
                 tag=f"xtr{t}_{half}")
            mmv2 = s.val("mm")
            c.op_tr_mm = mmv2

            def evac(e, t=t, half=half, bank=bank):
                outv = (T["x_intT"][:, :].rearrange("p (c s) -> p c s", c=KD)
                        [:, half * 4:(half + 1) * 4, t * P:(t + 1) * P])
                psv = PSB[bank][:, :].bitcast(BF16)[:, 0:4 * P]
                return e.tensor_copy(outv, psv.rearrange(
                    "p (j f) -> p j f", j=4))
            v_op(c, evac, wait=[("mm", mmv2)])
            if not hasattr(c, "op_tr_ve"):
                c.op_tr_ve = {}
            c.op_tr_ve[half] = s.val("ve")


def plan_ffn(c):
    T, s, d, PSB = c.T, c.s, c.d, c.PSB

    def z_t(ti):
        t_ = T["zA"] if ti < 2 else T["zB"]
        return t_[:, (ti % 2) * OF:(ti % 2) * OF + OF]

    def iqreg(ti):
        return T[("iqA0", "iqA1", "iqB0", "iqB1")[ti]]

    GST = ["wstage", "wstageB"]
    UST = ["wstage2", "wstage2B"]
    WDB = ["wdst", "wdst2", "outWb_hi", "outWb_lo"]

    v_op(c, lambda e: e.reciprocal(T["rsa"][:, :], T["sa"][:, :]))
    v_op(c, lambda e: e.tensor_scalar(T["rgq"][:, :], T["rsa"][:, :],
                                      T["whalf"][:, 0:1], None, ALU.mult))
    v_op(c, lambda e: e.tensor_scalar(T["ruq"][:, :], T["rsa"][:, :],
                                      T["whalf"][:, 1:2], None, ALU.mult))

    ldg_v, ldu_v = {}, {}
    mmQ = {}
    silu_se, evac_ve = {}, {}
    stt_ve = {g_: {} for g_ in range(-1, NFT)}
    mm_dn = {}

    def sched_wload(Q):
        w_ = [("mm", mmQ[Q - 2])] if Q - 2 in mmQ else []
        if Q <= 1:
            w_.append(("cc", c.cc_w))

        def ldg(e, Q=Q):
            inap = bass.AP(d["ccw_out"].tensor, (Q % 8) * FT,
                           [[OF, P], [SEGW, KD], [1, FT]])
            return e.dma_start(out=T[GST[Q % 2]].rearrange(
                "p (k f) -> p k f", k=KD), in_=inap)

        def ldu(e, Q=Q):
            inap = bass.AP(d["ccw_out"].tensor, P * OF + (Q % 8) * FT,
                           [[OF, P], [SEGW, KD], [1, FT]])
            return e.dma_start(out=T[UST[Q % 2]].rearrange(
                "p (k f) -> p k f", k=KD), in_=inap)
        dma(c, ldg, sem="lw0", wait=w_, tag=f"wg{Q}")
        ldg_v[Q] = s.val("lw0")
        gdma(c, ldu, sem="lw1", wait=w_, tag=f"wu{Q}")
        ldu_v[Q] = s.val("lw1")

    if hasattr(c, "pre_lw0"):
        ldg_v[0], ldu_v[0] = c.pre_lw0, c.pre_lw1
        sched_wload(1)
    else:
        sched_wload(0)
        sched_wload(1)

    for g in range(NFT):
        for q8 in range(8):
            Q = g * 8 + q8
            for ti in range(4):
                t_ = g * 4 + ti
                bg, bu = ti * 2, ti * 2 + 1

                def gumm(e, Q=Q, t_=t_, bg=bg, bu=bu):
                    last = None
                    gst, ust = T[GST[Q % 2]], T[UST[Q % 2]]
                    for k in range(KD):
                        xs = T["x_intT"][:, k * TCORE + t_ * P:
                                         k * TCORE + (t_ + 1) * P]
                        last = e.matmul(PSB[bg][:, 0:FT], xs,
                                        gst[:, k * FT:(k + 1) * FT],
                                        start=(k == 0), stop=(k == KD - 1))
                        last = e.matmul(PSB[bu][:, 0:FT], xs,
                                        ust[:, k * FT:(k + 1) * FT],
                                        start=(k == 0), stop=(k == KD - 1))
                    return last
                w_ = [("lw0", ldg_v[Q]), ("lw1", ldu_v[Q])]
                if q8 == 0:
                    if ti in stt_ve[g - 1]:
                        w_.append(("ve", stt_ve[g - 1][ti]))
                else:
                    w_ += [("se", silu_se[(Q - 1, ti)]),
                           ("ve", evac_ve[(Q - 1, ti)])]
                t_op(c, gumm, wait=w_, tag=f"gu{Q}_{ti}")
                mmv = s.val("mm")
                mmQ[Q] = mmv
                gsb = "gsbA" if ti % 2 == 0 else "gsbB"
                sw = [("mm", mmv)]
                if (Q, ti - 2) in evac_ve:
                    sw.append(("ve", evac_ve[(Q, ti - 2)]))
                elif (Q - 1, ti + 2) in evac_ve:
                    sw.append(("ve", evac_ve[(Q - 1, ti + 2)]))
                s_op(c, lambda e, t_=t_, bg=bg, gsb=gsb: e.activation(
                    T[gsb][:, :], PSB[bg][:, 0:FT], AF.Silu,
                    scale=T["rgq"][:, t_:t_ + 1]), wait=sw, tag=f"si{Q}_{ti}")
                silu_se[(Q, ti)] = s.val("se")
                v_op(c, lambda e, q8=q8, ti=ti, t_=t_, bu=bu: e.tensor_scalar(
                    z_t(ti)[:, q8 * FT:(q8 + 1) * FT], PSB[bu][:, 0:FT],
                    T["ruq"][:, t_:t_ + 1], None, ALU.mult),
                    wait=[("mm", mmv)])
                c.s.op("vector", lambda e, q8=q8, ti=ti, gsb=gsb:
                       e.tensor_tensor(
                           z_t(ti)[:, q8 * FT:(q8 + 1) * FT],
                           z_t(ti)[:, q8 * FT:(q8 + 1) * FT],
                           T[gsb][:, :], ALU.mult),
                       wait=[("se", silu_se[(Q, ti)])], inc=[("ve", 1)])
                evac_ve[(Q, ti)] = s.val("ve")
                v_op(c, lambda e, q8=q8, ti=ti: e.tensor_reduce(
                    T["Mq"][:, ti * 8 + q8:ti * 8 + q8 + 1],
                    z_t(ti)[:, q8 * FT:(q8 + 1) * FT], AXX, ALU.max,
                    apply_absolute_value=True))
            if Q + 2 < NFT * 8:
                sched_wload(Q + 2)

        v_op(c, lambda e: e.tensor_reduce(
            T["mt"][:, :], T["Mq"][:, :].rearrange("p (i q) -> p i q", i=4),
            AXX, ALU.max))
        v_op(c, lambda e: e.tensor_single_scalar(T["mt"][:, :], T["mt"][:, :],
                                                 1e-5, ALU.max))
        v_op(c, lambda e, g=g: e.tensor_scalar(
            T["dsc"][:, g * 4:(g + 1) * 4], T["mt"][:, :],
            T["whalf"][:, 2:3], 1.0 / 127.0, ALU.mult, ALU.mult))
        v_op(c, lambda e, g=g: e.reciprocal(T["sitq"][:, g * 4:(g + 1) * 4],
                                            T["mt"][:, :]))
        v_op(c, lambda e, g=g: e.tensor_scalar(
            T["sitq"][:, g * 4:(g + 1) * 4], T["sitq"][:, g * 4:(g + 1) * 4],
            127.0, None, ALU.mult))
        ve_scales = s.val("ve")

        if g == 0 and c.tap("z0", [P, 4 * OF]):
            dma(c, lambda e: e.dma_start(out=c.taps["z0"][:, 0:2 * OF],
                                         in_=T["zA"][:, :]),
                wait=[("ve", ve_scales)], sem="st")
            dma(c, lambda e: e.dma_start(out=c.taps["z0"][:, 2 * OF:4 * OF],
                                         in_=T["zB"][:, :]),
                wait=[("ve", ve_scales)], sem="st")
        if g == 0 and c.tap("Mq0", [P, 32]):
            dma(c, lambda e: e.dma_start(out=c.taps["Mq0"][:, :],
                                         in_=T["Mq"][:, :]),
                wait=[("ve", ve_scales)], sem="st")
        tap_hold = s.val("st")

        quant_done = {}
        for ti in range(4):
            t_ = g * 4 + ti
            if ti < 2:
                s_op(c, lambda e, ti=ti, t_=t_: e.activation(
                    z_t(ti)[:, :], z_t(ti)[:, :], AF.Identity,
                    bias=T["magicb"][:, :], scale=T["sitq"][:, t_:t_ + 1]),
                    wait=[("ve", ve_scales), ("st", tap_hold)],
                    tag=f"qa{g}_{ti}")
                s_op(c, lambda e, ti=ti: e.activation(
                    iqreg(ti)[:, 0:OF], z_t(ti)[:, :], AF.Identity,
                    bias=T["nmagicb"][:, :]), tag=f"qb{g}_{ti}")
                quant_done[ti] = ("se", s.val("se"))
            else:
                v_op(c, lambda e, ti=ti, t_=t_: e.tensor_scalar(
                    z_t(ti)[:, :], z_t(ti)[:, :], T["sitq"][:, t_:t_ + 1],
                    MAGIC, ALU.mult, ALU.add),
                    wait=[("st", tap_hold)], tag=f"qa{g}_{ti}")
                s_op(c, lambda e, ti=ti: e.activation(
                    iqreg(ti)[:, 0:OF], z_t(ti)[:, :], AF.Identity,
                    bias=T["nmagicb"][:, :]),
                    wait=[("ve", s.val("ve"))], tag=f"qb{g}_{ti}")
                quant_done[ti] = ("se", s.val("se"))

        fill_done = {}
        cum = {"ve": 0, "se": 0}
        tr_prev = {}
        for f in range(4):
            for ti in range(4):
                bank = (f * 4 + ti) % 8

                def trf(e, f=f, ti=ti, bank=bank):
                    psv = PSB[bank][:, :].bitcast(BF16)
                    last = None
                    for j in range(8):
                        oc = f * 8 + j
                        last = e.transpose(
                            psv[:, j * P:(j + 1) * P],
                            iqreg(ti)[:, oc * P:(oc + 1) * P],
                            T["ident"][:, :])
                    return last
                w_ = [quant_done[ti]]
                if f == 0:
                    w_.append(("ve", ve_scales))
                if bank in tr_prev:
                    w_.append(tr_prev[bank])
                t_op(c, trf, wait=w_, tag=f"tr{g}_{f}_{ti}")
                mmv = s.val("mm")

                def evf(e, f=f, ti=ti, bank=bank, cp=None):
                    src = PSB[bank][:, :].bitcast(BF16)[:, 0:1024]
                    dst = iqreg(ti)[:, OF + f * 1024:OF + (f + 1) * 1024]
                    if cp == "s":
                        return e.activation(dst, src, AF.Identity)
                    return e.tensor_copy(dst, src)
                if (f + ti) % 2 == 0:
                    v_op(c, lambda e, evf=evf: evf(e), wait=[("mm", mmv)])
                    mk = ("ve", s.val("ve"))
                else:
                    s_op(c, lambda e, evf=evf: evf(e, cp="s"),
                         wait=[("mm", mmv)])
                    mk = ("se", s.val("se"))
                tr_prev[bank] = mk
                cum[mk[0]] = mk[1]
            fill_done[f] = [("ve", cum["ve"]), ("se", cum["se"])]

        if g == 0 and c.tap("iqt0", [P, 2 * OF], BF16):
            dma(c, lambda e: e.dma_start(out=c.taps["iqt0"][:, :],
                                         in_=iqreg(0)[:, :]),
                wait=fill_done[3], sem="st")
        if g == 0 and c.tap("iqt3", [P, 2 * OF], BF16):
            dma(c, lambda e: e.dma_start(out=c.taps["iqt3"][:, :],
                                         in_=iqreg(3)[:, :]),
                wait=fill_done[3], sem="st")
        if g == 0 and c.tap("scl0", [P, 3 * NT]):
            dma(c, lambda e: e.dma_start(out=c.taps["scl0"][:, 0:NT],
                                         in_=T["sitq"][:, :]),
                wait=[("ve", ve_scales)], sem="st")
            dma(c, lambda e: e.dma_start(out=c.taps["scl0"][:, NT:2 * NT],
                                         in_=T["dsc"][:, :]), sem="st")
            dma(c, lambda e: e.dma_start(out=c.taps["scl0"][:, 2 * NT:3 * NT],
                                         in_=T["rgq"][:, :]), sem="st")
        tap_hold2 = s.val("st")

        yld_v = {}
        for ti in range(4):
            t_ = g * 4 + ti
            w_ = ([("ve", stt_ve[g - 1][ti])] if ti in stt_ve[g - 1] else [])
            dma(c, lambda e, ti=ti, t_=t_: e.dma_start(
                out=T["ystg%d" % ti][:, :],
                in_=d["ybuf"][t_ * P:(t_ + 1) * P, :]),
                sem="ly0", wait=w_, tag=f"yld{t_}")
            yld_v[ti] = s.val("ly0")

        for oc in range(NOC):
            buf = WDB[oc % 4]
            wsem = "lwd0" if oc % 2 == 0 else "lwd1"
            w_ = []
            if (g, oc - 4) in mm_dn:
                w_.append(("mm", mm_dn[(g, oc - 4)]))
            elif (g - 1, oc + 28) in mm_dn:
                w_.append(("mm", mm_dn[(g - 1, oc + 28)]))
            if g == 0 and oc < 4:
                w_.append(("cc", c.cc_w))
            ldfn = dma if oc % 2 == 0 else gdma
            wd_ap = bass.AP(d["ccw_out"].tensor,
                            (oc // 4) * SEGW + 2 * P * OF + (oc % 4) * P * D,
                            [[D, P], [1, D]])
            ldfn(c, lambda e, buf=buf, wd_ap=wd_ap: e.dma_start(
                out=T[buf][:, :], in_=wd_ap),
                sem=wsem, wait=w_, tag=f"wd{g}_{oc}")
            ld_wd = s.val(wsem)

            def dmm(e, oc=oc, buf=buf):
                last = None
                for ti in range(4):
                    st_ap = iqreg(ti)[:, OF + oc * P:OF + (oc + 1) * P]
                    for dc in range(2):
                        last = e.matmul(
                            PSB[ti * 2 + dc][:, 0:512], st_ap,
                            T[buf][:, dc * 512:(dc + 1) * 512],
                            start=(oc == 0), stop=(oc == NOC - 1))
                return last
            t_op(c, dmm, wait=[(wsem, ld_wd)]
                 + fill_done[3 if oc == 0 else oc // 8],
                 tag=f"dn{g}_{oc}")
            mm_dn[(g, oc)] = s.val("mm")
        mmd = mm_dn[(g, NOC - 1)]

        for ti in range(4):
            t_ = g * 4 + ti
            for dc in range(2):
                c.s.op("vector", lambda e, ti=ti, t_=t_, dc=dc:
                       e.scalar_tensor_tensor(
                           z_t(ti)[:, dc * 512:(dc + 1) * 512],
                           PSB[ti * 2 + dc][:, 0:512],
                           T["dsc"][:, t_:t_ + 1],
                           T["ystg%d" % ti][:, dc * 512:(dc + 1) * 512],
                           ALU.mult, ALU.add),
                       wait=[("mm", mmd), ("ly0", yld_v[ti]),
                             ("st", tap_hold2)],
                       inc=[("ve", 1)])
            stt_ve[g][ti] = s.val("ve")
        g4 = slice(g * 4, (g + 1) * 4)
        for ti in range(4):
            t_ = g * 4 + ti
            zst = z_t(ti)[:, 0:D]
            v_op(c, lambda e, t_=t_, zst=zst: e.tensor_reduce(
                T["stat1"][:, t_:t_ + 1], zst, AXX, ALU.add))
            s_op(c, lambda e, zst=zst: e.activation(
                T["qf_t"][:, 0:D], zst, AF.Square),
                wait=[("ve", s.val("ve"))])
            v_op(c, lambda e, t_=t_: e.tensor_reduce(
                T["stat2"][:, t_:t_ + 1], T["qf_t"][:, 0:D], AXX, ALU.add),
                wait=[("se", s.val("se"))])
        v_op(c, lambda e, g4=g4: e.tensor_scalar(
            T["mu"][:, g4], T["stat1"][:, g4], 1.0 / D, None, ALU.mult))
        v_op(c, lambda e, g4=g4: e.tensor_tensor(
            T["stat1"][:, g4], T["mu"][:, g4], T["mu"][:, g4], ALU.mult))
        v_op(c, lambda e, g4=g4: e.tensor_scalar(
            T["stat2"][:, g4], T["stat2"][:, g4], 1.0 / D, LN_EPS,
            ALU.mult, ALU.add))
        v_op(c, lambda e, g4=g4: e.tensor_tensor(
            T["stat2"][:, g4], T["stat2"][:, g4], T["stat1"][:, g4],
            ALU.subtract))
        _rsqrt_inplace(c, T["stat2"][:, g4], T["rstd"][:, g4],
                       T["st_s"][:, g4])
        dq_prev = s.val("dq")
        for ti in range(4):
            t_ = g * 4 + ti
            ob = ("yout", "qv_t", "yt1", "zt1")[ti]
            v_op(c, lambda e, t_=t_, ti=ti, ob=ob: e.tensor_scalar(
                T[ob][:, :], z_t(ti)[:, 0:D], T["mu"][:, t_:t_ + 1],
                T["rstd"][:, t_:t_ + 1], ALU.subtract, ALU.mult),
                wait=[("dq", dq_prev)])
            dma(c, lambda e, t_=t_, ob=ob: e.dma_start(
                out=d["out"][t_ * P:(t_ + 1) * P, :], in_=T[ob][:, :]),
                wait=[("ve", s.val("ve"))], sem="dq", tag=f"out{t_}")


def plan_finish(c):
    s = c.s
    allsems = [(n, s.val(n)) for n in s.cnt if n != "ph"]
    s.op("gpsimd", lambda e: e.nop(), wait=allsems)



_BUILT = None


def _get_built():
    global _BUILT
    if _BUILT is None:
        _BUILT = build_core()
    return _BUILT


def prep_in_maps(inputs):
    xfull = np.ascontiguousarray(np.asarray(inputs["L0_attn_x"],
                                            dtype=np.float32)[0])
    Wcat = np.ascontiguousarray(np.concatenate(
        [np.asarray(inputs["L0_attn_sign_W"], np.float32),
         np.asarray(inputs["L0_attn_perm_W"], np.float32),
         np.asarray(inputs["L0_attn_gate_W"], np.float32),
         np.asarray(inputs["L0_attn_in_W"], np.float32)], axis=1))
    bcat = np.ascontiguousarray(np.concatenate(
        [np.asarray(inputs["L0_attn_sign_b"], np.float32),
         np.asarray(inputs["L0_attn_perm_b"], np.float32),
         np.asarray(inputs["L0_attn_gate_b"], np.float32),
         np.asarray(inputs["L0_attn_in_b"], np.float32)])[None, :])
    lnw = np.ascontiguousarray(
        np.asarray(inputs["L0_attn_ln_w"], np.float32).reshape(KD, P).T)
    lnb = np.ascontiguousarray(
        np.asarray(inputs["L0_attn_ln_b"], np.float32).reshape(KD, P).T)
    outWb = np.ascontiguousarray(np.concatenate(
        [np.asarray(inputs["L0_attn_out_W"], np.float32),
         np.asarray(inputs["L0_attn_out_b"], np.float32)[None, :]], axis=0))
    wgT = np.ascontiguousarray(np.asarray(inputs["L0_ffn_w_gate"], np.float32).T)
    wuT = np.ascontiguousarray(np.asarray(inputs["L0_ffn_w_up"], np.float32).T)
    wdT = np.ascontiguousarray(np.asarray(inputs["L0_ffn_w_down"], np.float32).T)
    ident = np.eye(P, dtype=np.float32)
    ones1 = np.ones((1, P), np.float32)
    onescol = np.ones((P, 1), np.float32)

    in_maps = []
    for core in range(NCORE):
        xs = np.ascontiguousarray(xfull[core * TCORE:(core + 1) * TCORE])
        in_maps.append(dict(
            x=xs, xT=np.ascontiguousarray(xs.T), Wcat=Wcat, bcat=bcat,
            lnw=lnw, lnb=lnb, outWb=outWb,
            wgs=np.ascontiguousarray(wgT[core * P:(core + 1) * P]),
            wus=np.ascontiguousarray(wuT[core * P:(core + 1) * P]),
            wds=np.ascontiguousarray(wdT[core * 4 * P:(core + 1) * 4 * P]),
            ident=ident, ones1=ones1, onescol=onescol,
            cid=np.full((1, 1), float(core), np.float32)))
    return in_maps


def kernel(**inputs):
    nc, taps, _ = _get_built()
    in_maps = prep_in_maps(inputs)
    res = run_bass_kernel_spmd(nc, in_maps, core_ids=list(range(NCORE)))
    out = np.concatenate([r["out"] for r in res.results], axis=0)
    return out[None, :, :]


# version 10 (final)
# speedup vs baseline: 1.0439x; 1.0439x over previous
import contextlib
import numpy as np
import ml_dtypes
import concourse.bass as bass
import concourse.mybir as mybir
from concourse.bass_isa_rust import ReduceOp
from concourse.bass_utils import run_bass_kernel_spmd

F32 = mybir.dt.float32
BF16 = mybir.dt.bfloat16
AF = mybir.ActivationFunctionType
ALU = mybir.AluOpType
AXX = mybir.AxisListType.X

P = 128
NCORE = 8
TCORE = 2048
NT = 16
D = 1024
KD = 8
H, N8 = 4, 8
HN = 32
U = 352
OFF_SIGN, OFF_PERM, OFF_GATE, OFF_IN = 0, 32, 288, 320
LCH = 16
OF = 4096
NOC = 32
FT = 512
NFT = 4
LN_EPS = 1e-5
BIG = 1e30
MAGIC = 12582912.0
SEGW = 2 * P * OF + 4 * P * D
ENGS = ("sync", "vector", "scalar", "tensor", "gpsimd")


class Sched:

    LOCKSTEP = False

    def __init__(self, nc, ctx, sem_names):
        self.nc = nc
        self.ops = {e: [] for e in ENGS}
        self.sem = {}
        self.cnt = {}
        for s in sem_names:
            self.sem[s] = ctx.enter_context(nc.semaphore(s))
            self.cnt[s] = 0
        self.nbar = 0

    def op(self, eng, fn, wait=(), inc=(), tag=""):
        if self.LOCKSTEP:
            sel = self.LOCKSTEP if isinstance(self.LOCKSTEP, (tuple, list, set)) \
                else list(self.cnt)
            sel2 = []
            for n in sel:
                if n.startswith("vrange:"):
                    _, a_, b_ = n.split(":")
                    if eng == "vector" and int(a_) <= len(self.ops["vector"]) < int(b_):
                        sel2.append("ve")
                elif "@" in n:
                    nm, e_ = n.split("@")
                    if e_ == eng:
                        sel2.append(nm)
                else:
                    sel2.append(n)
            wait = list(wait) + [(n, self.cnt[n]) for n in sel2]
        waits = []
        for w in wait:
            if isinstance(w, tuple):
                waits.append((self.sem[w[0]], w[1]))
            else:
                waits.append((self.sem[w], self.cnt[w]))
        incs = []
        for i in inc:
            if isinstance(i, tuple):
                name, n = i
            else:
                name, n = i, 1
            self.cnt[name] += n
            incs.append((self.sem[name], n))
        self.ops[eng].append((fn, waits, incs, tag))

    def val(self, name):
        return self.cnt[name]

    def barrier(self):
        self.nbar += 1
        tgt = 5 * self.nbar
        dma_sems = ("ld", "st", "wq", "ls", "lm", "lq", "lx0", "lx1", "lxT",
                    "lay", "lg", "lw0", "lw1", "lwd0", "lwd1", "ly0", "qa",
                    "dq")
        for e in ENGS:
            wl = [(n, self.cnt[n]) for n in dma_sems] if e == "sync" else []
            self.op(e, lambda eng: eng.nop(), wait=wl, inc=[("ph", 1)])
        for e in ENGS:
            self.op(e, lambda eng: eng.nop(), wait=[("ph", tgt)])

    def emit(self, block):
        sched = self

        def mk(engname):
            serial_sem = {"vector": "ve", "gpsimd": "ge"}.get(engname)

            def emitter(eng):
                prior = 0
                for fn, waits, incs, tag in sched.ops[engname]:
                    if serial_sem and prior:
                        eng.wait_ge(sched.sem[serial_sem], prior)
                    for h, v in waits:
                        eng.wait_ge(h, v)
                    instr = fn(eng)
                    if incs:
                        assert instr is not None, f"op {tag} returned no instr"
                        instr.then_inc(*incs[0])
                        for h, n in incs[1:]:
                            eng.nop().then_inc(h, n)
                    if serial_sem:
                        for hn, n in incs:
                            if hn is sched.sem[serial_sem]:
                                prior += n
            return emitter

        block.sync(mk("sync"))
        block.vector(mk("vector"))
        block.scalar(mk("scalar"))
        block.tensor(mk("tensor"))
        block.gpsimd(mk("gpsimd"))


class Ctx:
    pass


def build_core(debug_taps=()):
    nc = bass.Bass("TRN2", target_bir_lowering=False)
    c = Ctx()
    c.nc = nc
    c.taps = {}
    c.debug_taps = debug_taps

    dd = {}
    for nm, shape in [("x", [TCORE, D]), ("xT", [D, TCORE]), ("Wcat", [D, U]),
                      ("bcat", [1, U]), ("lnw", [P, KD]), ("lnb", [P, KD]),
                      ("outWb", [33, D]), ("wgs", [P, OF]), ("wus", [P, OF]),
                      ("wds", [4 * P, D]), ("ident", [P, P]), ("ones1", [1, P]),
                      ("onescol", [P, 1]), ("cid", [1, 1])]:
        dd[nm] = nc.dram_tensor(nm, shape, F32, kind="ExternalInput").ap()
    dd["out"] = nc.dram_tensor("out", [TCORE, D], F32, kind="ExternalOutput").ap()
    dd["ybuf"] = nc.dram_tensor("ybuf", [TCORE, D], F32, kind="Internal").ap()
    dd["cc_in"] = nc.dram_tensor("cc_in", [1, 96], F32, kind="Internal",
                                 addr_space="Local").ap()
    dd["cc_out"] = nc.dram_tensor("cc_out", [NCORE, 96], F32, kind="Internal",
                                  addr_space="Shared").ap()
    dd["ccm_in"] = nc.dram_tensor("ccm_in", [1, 4], F32, kind="Internal",
                                  addr_space="Local").ap()
    dd["ccm_out"] = nc.dram_tensor("ccm_out", [NCORE, 4], F32, kind="Internal",
                                   addr_space="Shared").ap()
    dd["ccw_in"] = nc.dram_tensor("ccw_in", [1, SEGW], BF16, kind="Internal",
                                  addr_space="Local").ap()
    dd["ccw_out"] = nc.dram_tensor("ccw_out", [NCORE, SEGW], BF16,
                                   kind="Internal", addr_space="Shared").ap()
    c.d = dd

    def tapfn(name, shape, dtype=F32):
        if name in debug_taps:
            c.taps[name] = nc.dram_tensor("tap_" + name, shape, dtype,
                                          kind="ExternalOutput").ap()
        return c.taps.get(name)

    c.tap = tapfn

    T = {}
    c.T = T

    def al(name, shape, dtype=F32):
        T[name] = nc.alloc_sbuf_tensor("sb_" + name, shape, dtype)

    def alias(name, host, off, shape, dtype=F32):
        base = nc.lookup_mloc(T[host]).addr
        T[name] = nc.alloc_sbuf_tensor_at("sb_" + name, shape, dtype,
                                          offset=base + off)

    al("ident", [P, P], BF16)
    al("ones1", [1, P]); al("onescol", [P, 1], BF16); al("cid", [1, 1])
    al("iota8", [P, 8])
    al("lnw", [P, KD]); al("lnb", [P, KD])
    al("lnbh", [P, KD], BF16); al("lnbl", [P, KD], BF16)
    al("mu", [P, NT]); al("rstd", [P, NT]); al("nrstdmu", [P, NT])
    al("sa", [P, NT]); al("dsc", [P, NT])
    al("wsc", [P, 4]); al("whalf", [P, 4])
    al("stat1", [P, NT]); al("stat2", [P, NT]); al("st_s", [P, NT])
    al("s1", [P, 1]); al("s2", [P, 1]); al("s3", [P, 1])
    al("gm1", [P, NT * H]); al("gci", [P, NT * H])
    al("chn", [P, 1]); al("chp", [P, 1])
    al("identf", [P, P])
    al("onescf", [P, 1])
    al("wrow", [1, 4]); al("wrow8", [1, 32])
    al("mt", [P, 4]); al("rut", [P, 4]); al("sit", [P, 4])
    al("rsa", [P, NT]); al("rgq", [P, NT]); al("ruq", [P, NT])
    al("sitq", [P, NT])
    al("magicb", [P, 1]); al("nmagicb", [P, 1])
    al("sc_po", [P, 256], BF16); al("sc_sum", [P, 96]); al("onesL", [P, LCH * HN])
    al("l2_po", [8, 256], BF16)
    al("gsum", [1, NCORE * 96]); al("gcur", [1, 96]); al("gnx", [1, 96])
    al("goh", [1, H * 64]); al("gt0", [1, H * 64]); al("gmsk", [1, 1])
    al("h_in", [P, HN]); al("e_full", [P, HN]); al("e_ch", [P, HN])
    al("xstage", [P, D]); al("xstage2", [P, D])
    al("xTstage", [P, TCORE])
    al("wdst", [P, D], BF16); al("wdst2", [P, D], BF16)
    al("outWb_hi", [P, D], BF16); al("outWb_lo", [P, D], BF16)

    al("Wcat", [P, KD * U])
    al("Wp_hi", [P, KD * U], BF16)
    al("Wp_lo", [P, KD * U], BF16)
    al("K1R", [P, U]); al("B2R", [P, U])
    al("bcat", [1, U]); al("k1row", [1, U]); al("b2row", [1, U])
    al("wfst", [P, 2048])
    al("wist", [P, 2048], BF16)
    alias("x_intT", "Wcat", 0, [P, KD * TCORE], BF16)
    assert (nc.lookup_mloc(T["wfst"]).addr + 8192
            >= nc.lookup_mloc(T["Wcat"]).addr + KD * TCORE * 2), "x_intT span"

    al("Ubuf", [P, NT * U])
    alias("sc_oh", "Ubuf", 0, [P, LCH * H * 64], BF16)
    alias("sc_hh", "Ubuf", 8192, [P, LCH * HN])
    alias("ys_hi", "Ubuf", 12288, [P, LCH * HN], BF16)
    alias("ys_lo", "Ubuf", 13312, [P, LCH * HN], BF16)
    alias("sc_ys", "Ubuf", 10240, [P, LCH * HN])
    alias("ysT_hi", "Ubuf", 0, [P, TCORE], BF16)
    alias("ysT_lo", "Ubuf", 4096, [P, TCORE], BF16)
    alias("gsbA", "Ubuf", 0, [P, FT])
    alias("gsbB", "Ubuf", 2048, [P, FT])
    alias("Mq", "Ubuf", 4096, [P, 32])
    alias("ystg0", "Ubuf", 6144, [P, D])
    alias("ystg1", "Ubuf", 10240, [P, D])
    alias("ystg2", "Ubuf", 14336, [P, D])
    alias("ystg3", "Ubuf", 18432, [P, D])

    al("xT_hi", [P, KD * TCORE], BF16)
    alias("gtmp", "xT_hi", 0, [P, NT * 256])
    alias("sc_t0", "xT_hi", 16384, [P, LCH * 256])
    alias("l2_t0", "xT_hi", 16384, [8, LCH * 256])
    alias("zB", "xT_hi", 0, [P, 16 * FT])
    alias("iqB0", "xT_hi", 0, [P, 2 * OF], BF16)
    alias("iqB1", "xT_hi", 16384, [P, 2 * OF], BF16)

    al("xT_lo", [P, KD * TCORE], BF16)
    for i_, nm_ in enumerate(["l2_sig", "l2_rho", "l2_s", "l2_sg", "l2_dh",
                              "l2_ch", "l2_pi", "l2_sh", "l2_v"]):
        alias(nm_, "xT_lo", i_ * 2048, [8, LCH * HN])
    alias("l2_oh", "xT_lo", 9 * 2048, [8, LCH * H * 64], BF16)
    alias("zA", "xT_lo", 0, [P, 16 * FT])
    alias("iqA0", "xT_lo", 0, [P, 2 * OF], BF16)
    alias("iqA1", "xT_lo", 16384, [P, 2 * OF], BF16)

    al("S", [P, NT * 256])
    for i_, nm_ in enumerate(["sc_d", "sc_c", "sc_p", "sc_sig", "sc_dh",
                              "sc_ch", "sc_pi", "sc_sh"]):
        alias(nm_, "S", i_ * 2048, [P, LCH * HN])
    alias("wstage", "S", 0, [P, KD * FT], BF16)
    alias("wstage2", "S", 8192, [P, KD * FT], BF16)
    alias("xTstage2", "S", 0, [P, TCORE])
    alias("wstageB", "xTstage", 0, [P, KD * FT], BF16)
    assert (nc.lookup_mloc(T["xstage2"]).addr
            == nc.lookup_mloc(T["xstage"]).addr + D * 4), "xstage adjacency"
    alias("wstage2B", "xstage", 0, [P, KD * FT], BF16)

    al("sgn", [P, NT * HN])
    for nm_ in ["gam", "dvec", "cvec", "rsum", "rrec", "Rmax", "AVm", "PIDX",
                "grk", "gsel", "gcs", "gsr", "gsm"]:
        al(nm_, [P, NT * HN])
    assert nc.lookup_mloc(T["gsm"]).addr - nc.lookup_mloc(T["sgn"]).addr \
        == 13 * 2048, "RE contiguity"
    alias("xq_t", "sgn", 0, [P, D], BF16)
    alias("qv_t", "sgn", 2048, [P, D])
    alias("yt1", "sgn", 6144, [P, D])
    alias("zt1", "sgn", 10240, [P, D])
    alias("yout", "sgn", 14336, [P, D])
    alias("qf_t", "sgn", 9 * 2048, [P, D])

    c.sbuf_left = nc.sbuf_bytes_remaining
    c.PSB = [nc.alloc_psum_tensor(f"ps{b}", [P, 512], F32) for b in range(8)]

    with contextlib.ExitStack() as ctx:
        sch = Sched(nc, ctx, ["ld", "wq", "mm", "ve", "se", "ge", "st", "cc",
                              "dq", "qa", "ph", "ev", "ls", "lm", "lq", "lx0",
                              "lx1", "lxT", "lay", "lg", "lw0", "lw1", "lwd0",
                              "lwd1", "ly0"])
        c.s = sch
        block = ctx.enter_context(nc.Block())
        plan(c)
        sch.emit(block)

    return nc, c.taps, c.sbuf_left



def v_op(c, fn, wait=(), tag=""):
    c.s.op("vector", fn, wait=wait, inc=[("ve", 1)], tag=tag)


def s_op(c, fn, wait=(), tag=""):
    c.s.op("scalar", fn, wait=wait, inc=[("se", 1)], tag=tag)


def g_op(c, fn, wait=(), tag=""):
    c.s.op("gpsimd", fn, wait=wait, inc=[("ge", 1)], tag=tag)


def t_op(c, fn, wait=(), tag=""):
    c.s.op("tensor", fn, wait=wait, inc=[("mm", 1)], tag=tag)


def dma(c, fn, wait=(), sem="ld", tag=""):
    c.s.op("sync", fn, wait=wait, inc=[(sem, 16)], tag=tag)


def gdma(c, fn, wait=(), sem="ld", tag=""):
    c.s.op("gpsimd", fn, wait=wait, inc=[(sem, 16)], tag=tag)



def plan(c):
    plan_setup(c)
    plan_attn_inputs(c)
    plan_wpartials(c)
    c.proj_hooks = mk_quant_hooks(c)
    plan_attn_front(c)
    c.s.barrier()
    plan_sinkhorn(c)
    plan_greedy(c)
    c.s.barrier()
    plan_scan_local(c)
    plan_scan_global(c)
    c.s.barrier()
    plan_out_proj(c)
    c.s.barrier()
    plan_ffn(c)
    plan_finish(c)


def plan_setup(c):
    T, d = c.T, c.d
    for nm in ["ident", "ones1", "onescol", "cid", "lnw", "lnb", "bcat"]:
        def ld(e, nm=nm):
            if nm in ("ident", "onescol"):
                return e.dma_start(out=c.T["wfst"][:, 0:P] if nm == "ident"
                                   else c.T["wfst"][:, P:P + 1],
                                   in_=c.d[nm][:, :])
            return e.dma_start(out=c.T[nm][:, :], in_=c.d[nm][:, :])
        dma(c, ld, sem="ls", tag=f"ld_{nm}")
    v_op(c, lambda e: e.tensor_copy(T["ident"][:, :], T["wfst"][:, 0:P]),
         wait=[("ls", c.s.val("ls"))])
    v_op(c, lambda e: e.tensor_copy(T["onescol"][:, :], T["wfst"][:, P:P + 1]))
    g_op(c, lambda e: e.iota(T["iota8"][:, :], pattern=[[1, 8]], base=0,
                             channel_multiplier=0,
                             allow_small_or_imprecise_dtypes=True))
    v_op(c, lambda e: e.memset(T["onesL"][:, :], 1.0))
    v_op(c, lambda e: e.memset(T["h_in"][:, :], 0.0))
    v_op(c, lambda e: e.tensor_copy(T["identf"][:, :], T["ident"][:, :]))
    v_op(c, lambda e: e.memset(T["onescf"][:, :], 1.0))
    v_op(c, lambda e: e.memset(T["chn"][:, :], -0.5))
    v_op(c, lambda e: e.memset(T["chp"][:, :], 0.5))
    v_op(c, lambda e: e.memset(T["magicb"][:, :], MAGIC))
    v_op(c, lambda e: e.memset(T["nmagicb"][:, :], -MAGIC))
    v_op(c, lambda e: e.memset(T["PIDX"][:, :], 0.0))
    v_op(c, lambda e: e.memset(T["AVm"][:, :], 0.0))
    v_op(c, lambda e: e.tensor_copy(T["lnbh"][:, :], T["lnb"][:, :]))
    v_op(c, lambda e: e.tensor_tensor(T["lnbl"][:, :], T["lnb"][:, :],
                                      T["lnbh"][:, :], ALU.subtract))
    dma(c, lambda e: e.dma_start(out=T["xstage"][0:33, :], in_=d["outWb"][:, :]),
        sem="ls")
    v_op(c, lambda e: e.tensor_copy(T["outWb_hi"][0:33, :], T["xstage"][0:33, :]),
         wait=[("ls", c.s.val("ls"))])
    v_op(c, lambda e: e.tensor_tensor(T["outWb_lo"][0:33, :], T["xstage"][0:33, :],
                                      T["outWb_hi"][0:33, :], ALU.subtract))
    dma(c, lambda e: e.dma_start(
        out=T["Wcat"][:, :].rearrange("p (c u) -> p c u", c=KD),
        in_=bass.AP(d["Wcat"].tensor, 0, [[U, P], [P * U, KD], [1, U]])),
        sem="ls")
    v_op(c, lambda e: e.tensor_copy(T["Wp_hi"][:, :], T["Wcat"][:, :]),
         wait=[("ls", c.s.val("ls"))])
    v_op(c, lambda e: e.tensor_tensor(T["Wp_lo"][:, :], T["Wcat"][:, :],
                                      T["Wp_hi"][:, :], ALU.subtract))
    c.ve_setup = c.s.val("ve")


def _wunits(c):
    d = c.d
    units = []
    for sb in range(2):
        units.append((lambda sb=sb: d["wgs"][:, sb * 2048:(sb + 1) * 2048],
                      2048, 0, sb * 2048, [[OF, P], [1, 2048]]))
    for sb in range(2):
        units.append((lambda sb=sb: d["wus"][:, sb * 2048:(sb + 1) * 2048],
                      2048, 1, P * OF + sb * 2048, [[OF, P], [1, 2048]]))
    for j in range(4):
        units.append((lambda j=j: d["wds"][j * P:(j + 1) * P, :],
                      1024, 2, 2 * P * OF + j * P * D, [[D, P], [1, D]]))
    return units


def plan_wpartials(c):
    T, d, s = c.T, c.d, c.s
    v_op(c, lambda e: e.memset(T["wsc"][:, :], 0.0))
    prev_ve = c.ve_setup
    for (src, width, col, _off, _dims) in _wunits(c):
        gdma(c, lambda e, src=src, width=width: e.dma_start(
            out=T["wfst"][:, 0:width], in_=src()),
            sem="lm", wait=[("ve", prev_ve)], tag="wpart_ld")
        v_op(c, lambda e, width=width: e.tensor_reduce(
            T["s1"][:, :], T["wfst"][:, 0:width], AXX, ALU.add,
            apply_absolute_value=True), wait=[("lm", s.val("lm"))])
        prev_ve = s.val("ve")
        v_op(c, lambda e, col=col: e.tensor_tensor(
            T["wsc"][:, col:col + 1], T["wsc"][:, col:col + 1],
            T["s1"][:, :], ALU.add))
    t_op(c, lambda e: e.matmul(c.PSB[7][0:1, 0:4], T["onescf"][:, :],
                               T["wsc"][:, 0:4], start=True, stop=True),
         wait=[("ve", s.val("ve"))], tag="meansum")
    v_op(c, lambda e: e.tensor_copy(T["wrow"][:, :], c.PSB[7][0:1, 0:4]),
         wait=[("mm", s.val("mm"))])
    gdma(c, lambda e: e.dma_start(out=d["ccm_in"][:, :], in_=T["wrow"][:, :]),
         sem="lm", wait=[("ve", s.val("ve"))], tag="means_st")
    c.s.op("gpsimd", lambda e: e.collective_compute(
        "AllGather", ALU.bypass, replica_groups=[list(range(NCORE))],
        ins=[d["ccm_in"][:, :]], outs=[d["ccm_out"][:, :]]),
        wait=[("lm", s.val("lm"))], inc=[("cc", 1)], tag="AGmeans")
    c.cc_m = s.val("cc")
    c.qprev_ve = s.val("ve")


def mk_quant_hooks(c):
    T, d, s = c.T, c.d, c.s
    units = _wunits(c)

    def combine():
        gdma(c, lambda e: e.dma_start(
            out=T["wrow8"][0:1, :],
            in_=bass.AP(d["ccm_out"].tensor, 0, [[1, 4], [4, NCORE]]),
            allow_slow_non_contiguous=True),
            sem="lm", wait=[("cc", c.cc_m)], tag="mcomb_ld")
        v_op(c, lambda e: e.tensor_reduce(
            T["wrow"][0:1, :],
            T["wrow8"][0:1, :].rearrange("p (c r) -> p c r", c=4),
            AXX, ALU.add), wait=[("lm", s.val("lm"))])
        t_op(c, lambda e: e.matmul(c.PSB[7][:, 0:4], T["ones1"][:, :],
                                   T["wrow"][:, :], start=True, stop=True),
             wait=[("ve", s.val("ve"))], tag="meanrep")
        v_op(c, lambda e: e.tensor_copy(T["wsc"][:, 0:4], c.PSB[7][:, 0:4]),
             wait=[("mm", s.val("mm"))])
        v_op(c, lambda e: e.tensor_scalar(T["wsc"][:, 0:4], T["wsc"][:, 0:4],
                                          1.0 / (D * OF), 1e-5, ALU.mult,
                                          ALU.max))
        v_op(c, lambda e: e.tensor_scalar(T["whalf"][:, 0:4], T["wsc"][:, 0:4],
                                          0.5, None, ALU.mult))
        v_op(c, lambda e: e.reciprocal(T["wsc"][:, 0:4], T["wsc"][:, 0:4]))
        c.ve_wsc = s.val("ve")

    def mk_qunit(u):
        def qunit():
            src, width, col, off, dims = units[u]
            gdma(c, lambda e: e.dma_start(out=T["wfst"][:, 0:width],
                                          in_=src()),
                 sem="lq", wait=[("ve", c.qprev_ve)], tag=f"q_ld{u}")
            lqv = s.val("lq")
            s_op(c, lambda e: e.activation(
                T["wist"][:, 0:width], T["wfst"][:, 0:width], AF.Sign,
                bias=T["chn"][:, :], scale=T["wsc"][:, col:col + 1]),
                wait=[("lq", lqv), ("ve", c.ve_wsc)])
            s_op(c, lambda e: e.activation(
                T["wfst"][:, 0:width], T["wfst"][:, 0:width], AF.Sign,
                bias=T["chp"][:, :], scale=T["wsc"][:, col:col + 1]))
            c.s.op("vector", lambda e: e.tensor_tensor(
                T["wist"][:, 0:width], T["wist"][:, 0:width],
                T["wfst"][:, 0:width], ALU.add),
                wait=[("se", s.val("se"))], inc=[("ve", 1)])
            c.qprev_ve = s.val("ve")
            dma(c, lambda e: e.dma_start(
                out=bass.AP(d["ccw_in"].tensor, off, dims),
                in_=T["wist"][:, 0:width]),
                sem="wq", wait=[("ve", s.val("ve"))], tag=f"q_st{u}")
        return qunit

    def ag2():
        c.s.op("gpsimd", lambda e: e.collective_compute(
            "AllGather", ALU.bypass, replica_groups=[list(range(NCORE))],
            ins=[d["ccw_in"][:, :]], outs=[d["ccw_out"][:, :]]),
            wait=[("wq", s.val("wq"))], inc=[("cc", 1)], tag="AGw")
        c.cc_w = s.val("cc")

    hooks = {0: [combine]}
    for u in range(8):
        hooks.setdefault(1 + u, []).append(mk_qunit(u))
    hooks.setdefault(9, []).append(ag2)
    return hooks



def plan_attn_inputs(c):
    T, d, s = c.T, c.d, c.s
    xt_use = {0: c.s.val("ve"), 1: c.s.val("ve")}
    for k in range(KD):
        buf = "xTstage" if k % 2 == 0 else "xTstage2"
        dma(c, lambda e, k=k, buf=buf: e.dma_start(
            out=T[buf][:, :], in_=d["xT"][k * P:(k + 1) * P, :]),
            sem="lxT", wait=[("ve", xt_use[k % 2])], tag=f"xT{k}")
        s_op(c, lambda e, k=k, buf=buf: e.activation(
            T["xT_hi"][:, k * TCORE:(k + 1) * TCORE], T[buf][:, :],
            AF.Identity), wait=[("lxT", s.val("lxT"))])
        v_op(c, lambda e, k=k, buf=buf: e.tensor_tensor(
            T["xT_lo"][:, k * TCORE:(k + 1) * TCORE], T[buf][:, :],
            T["xT_hi"][:, k * TCORE:(k + 1) * TCORE], ALU.subtract),
            wait=[("se", s.val("se"))])
        xt_use[k % 2] = s.val("ve")
    last_use = {0: 0, 1: 0}
    for t in range(NT):
        buf = "xstage" if t % 2 == 0 else "xstage2"
        lsem = "lx0" if t % 2 == 0 else "lx1"
        dma(c, lambda e, t=t, buf=buf: e.dma_start(
            out=T[buf][:, :], in_=d["x"][t * P:(t + 1) * P, :]),
            sem=lsem, wait=[("ve", last_use[t % 2])], tag=f"x{t}")
        ldv = s.val(lsem)
        v_op(c, lambda e, t=t, buf=buf: e.tensor_reduce(
            T["stat1"][:, t:t + 1], T[buf][:, :], AXX, ALU.add),
            wait=[(lsem, ldv)])
        s_op(c, lambda e, t=t, buf=buf: e.activation(
            T["qf_t"][:, 0:D], T[buf][:, :], AF.Square),
            wait=[(lsem, ldv), ("ve", s.val("ve"))])
        v_op(c, lambda e, t=t: e.tensor_reduce(
            T["stat2"][:, t:t + 1], T["qf_t"][:, 0:D], AXX, ALU.add),
            wait=[("se", s.val("se"))])
        last_use[t % 2] = s.val("ve")
    c.ve_xsplit = s.val("ve")
    c.se_stats = s.val("se")
    c.ve_stats = s.val("ve")


def _rsqrt_inplace(c, ap_v, ap_r, ap_tmp):
    T, s = c.T, c.s
    s_op(c, lambda e: e.activation(ap_r, ap_v, AF.Sqrt), wait=[("ve", s.val("ve"))])
    v_op(c, lambda e: e.reciprocal(ap_r, ap_r), wait=[("se", s.val("se"))])
    for _ in range(2):
        v_op(c, lambda e: e.tensor_tensor(ap_tmp, ap_r, ap_r, ALU.mult))
        v_op(c, lambda e: e.tensor_tensor(ap_tmp, ap_tmp, ap_v, ALU.mult))
        v_op(c, lambda e: e.tensor_scalar(ap_tmp, ap_tmp, -0.5, 1.5,
                                          ALU.mult, ALU.add))
        v_op(c, lambda e: e.tensor_tensor(ap_r, ap_r, ap_tmp, ALU.mult))


def plan_attn_front(c):
    T, d, s, PSB = c.T, c.d, c.s, c.PSB

    def b2_mms(e):
        last = None
        for k in range(KD):
            for a_, b_ in (("lnbh", "Wp_hi"), ("lnbh", "Wp_lo"), ("lnbl", "Wp_hi")):
                last = e.matmul(PSB[7][0:1, 0:U], T[a_][:, k:k + 1],
                                T[b_][:, k * U:(k + 1) * U],
                                start=(k == 0 and a_ == "lnbh" and b_ == "Wp_hi"),
                                stop=(k == KD - 1 and a_ == "lnbl"))
        return last
    t_op(c, b2_mms, wait=[("ve", s.val("ve")), ("ls", s.val("ls"))], tag="b2")
    v_op(c, lambda e: e.tensor_tensor(T["b2row"][:, :], PSB[7][0:1, 0:U],
                                      T["bcat"][:, :], ALU.add),
         wait=[("mm", s.val("mm"))])

    v_op(c, lambda e: e.tensor_tensor(
        T["Wcat"][:, :].rearrange("p (c u) -> p c u", c=KD),
        T["Wcat"][:, :].rearrange("p (c u) -> p c u", c=KD),
        T["lnw"][:, :].unsqueeze(2).broadcast_to([P, KD, U]), ALU.mult))
    v_op(c, lambda e: e.tensor_copy(T["Wp_hi"][:, :], T["Wcat"][:, :]))
    v_op(c, lambda e: e.tensor_tensor(T["Wp_lo"][:, :], T["Wcat"][:, :],
                                      T["Wp_hi"][:, :], ALU.subtract))
    ve_wsplit = s.val("ve")

    def k1_mms(e):
        last = None
        for k in range(KD):
            for b_ in ("Wp_hi", "Wp_lo"):
                last = e.matmul(PSB[6][0:1, 0:U], T["onescol"][:, :],
                                T[b_][:, k * U:(k + 1) * U],
                                start=(k == 0 and b_ == "Wp_hi"),
                                stop=(k == KD - 1 and b_ == "Wp_lo"))
        return last
    t_op(c, k1_mms, wait=[("ve", s.val("ve"))], tag="k1")
    v_op(c, lambda e: e.tensor_copy(T["k1row"][:, :], PSB[6][0:1, 0:U]),
         wait=[("mm", s.val("mm"))])

    def rep_mms(e):
        e.matmul(PSB[5][:, 0:U], T["ones1"][:, :], T["k1row"][:, :],
                 start=True, stop=True)
        return e.matmul(PSB[4][:, 0:U], T["ones1"][:, :], T["b2row"][:, :],
                        start=True, stop=True)
    t_op(c, rep_mms, wait=[("ve", s.val("ve"))], tag="repk1b2")
    v_op(c, lambda e: e.tensor_copy(T["K1R"][:, :], PSB[5][:, 0:U]),
         wait=[("mm", s.val("mm"))])
    v_op(c, lambda e: e.tensor_copy(T["B2R"][:, :], PSB[4][:, 0:U]))

    v_op(c, lambda e: e.tensor_scalar(T["mu"][:, :], T["stat1"][:, :],
                                      1.0 / D, None, ALU.mult),
         wait=[("ve", c.ve_stats)])
    v_op(c, lambda e: e.tensor_tensor(T["stat1"][:, :], T["mu"][:, :],
                                      T["mu"][:, :], ALU.mult))
    v_op(c, lambda e: e.tensor_scalar(T["stat2"][:, :], T["stat2"][:, :],
                                      1.0 / D, LN_EPS, ALU.mult, ALU.add))
    v_op(c, lambda e: e.tensor_tensor(T["stat2"][:, :], T["stat2"][:, :],
                                      T["stat1"][:, :], ALU.subtract))
    _rsqrt_inplace(c, T["stat2"][:, :], T["rstd"][:, :], T["st_s"][:, :])
    v_op(c, lambda e: e.tensor_tensor(T["nrstdmu"][:, :], T["rstd"][:, :],
                                      T["mu"][:, :], ALU.mult))
    v_op(c, lambda e: e.tensor_scalar(T["nrstdmu"][:, :], T["nrstdmu"][:, :],
                                      -1.0, None, ALU.mult))
    ve_ready = s.val("ve")

    evac_se = {0: 0, 1: 0}
    for t in range(NT):
        for h in getattr(c, "proj_hooks", {}).get(t, []):
            h()
        bank = t % 2

        def proj_mms(e, t=t, bank=bank):
            last = None
            first = True
            for k in range(KD):
                xh = T["xT_hi"][:, k * TCORE + t * P:k * TCORE + (t + 1) * P]
                xl = T["xT_lo"][:, k * TCORE + t * P:k * TCORE + (t + 1) * P]
                wh = T["Wp_hi"][:, k * U:(k + 1) * U]
                wl = T["Wp_lo"][:, k * U:(k + 1) * U]
                last = e.matmul(PSB[bank][:, 0:U], xh, wh, start=first, stop=False)
                first = False
                last = e.matmul(PSB[bank][:, 0:U], xh, wl, start=False, stop=False)
                last = e.matmul(PSB[bank][:, 0:U], xl, wh, start=False,
                                stop=(k == KD - 1))
            return last
        t_op(c, proj_mms, wait=[("ve", ve_ready), ("se", evac_se[bank])],
             tag=f"proj{t}")
        mmv = s.val("mm")
        s_op(c, lambda e, t=t, bank=bank: e.activation(
            T["Ubuf"][:, t * U:(t + 1) * U], PSB[bank][:, 0:U], AF.Identity,
            scale=T["rstd"][:, t:t + 1]), wait=[("mm", mmv)])
        evac_se[bank] = s.val("se")
        c.s.op("vector", lambda e, t=t: e.scalar_tensor_tensor(
            T["Ubuf"][:, t * U:(t + 1) * U], T["K1R"][:, :],
            T["nrstdmu"][:, t:t + 1], T["Ubuf"][:, t * U:(t + 1) * U],
            ALU.mult, ALU.add), wait=[("se", s.val("se"))], inc=[("ve", 1)])
        v_op(c, lambda e, t=t: e.tensor_tensor(
            T["Ubuf"][:, t * U:(t + 1) * U], T["Ubuf"][:, t * U:(t + 1) * U],
            T["B2R"][:, :], ALU.add))

    c.wq_done = s.val("wq")
    if c.tap("Ubuf", [P, NT * U]):
        dma(c, lambda e: e.dma_start(out=c.taps["Ubuf"][:, :], in_=T["Ubuf"][:, :]),
            wait=[("ve", s.val("ve"))], sem="st")
    for nm_, sh_ in [("mu", [P, NT]), ("rstd", [P, NT]), ("K1R", [P, U]),
                     ("B2R", [P, U]), ("b2row", [1, U]), ("k1row", [1, U]),
                     ("Wcat", [P, KD * U]), ("stat2", [P, NT]),
                     ("bcat", [1, U])]:
        if c.tap(nm_, sh_):
            dma(c, lambda e, nm_=nm_: e.dma_start(out=c.taps[nm_][:, :],
                                                  in_=T[nm_][:, :]),
                wait=[("ve", s.val("ve")), ("se", s.val("se"))], sem="st")
    if c.tap("xh", [P, 2048], BF16):
        dma(c, lambda e: e.dma_start(
            out=c.taps["xh"][:, :],
            in_=T["xT_hi"][:, 0:2048]), wait=[("ve", s.val("ve"))], sem="st")

    uview = T["Ubuf"][:, :].rearrange("p (t u) -> p t u", t=NT)
    gate_in = uview[:, :, OFF_GATE:OFF_GATE + HN]
    sign_in = uview[:, :, OFF_SIGN:OFF_SIGN + HN]
    bin_in = uview[:, :, OFF_IN:OFF_IN + HN]
    perm_in = uview[:, :, OFF_PERM:OFF_PERM + 256]
    s_op(c, lambda e: e.activation(T["gam"][:, :], gate_in, AF.Sigmoid),
         wait=[("ve", s.val("ve"))])
    s_op(c, lambda e: e.activation(T["sgn"][:, :], sign_in, AF.Sigmoid))
    s_op(c, lambda e: e.activation(T["S"][:, :], perm_in, AF.Exp))
    v_op(c, lambda e: e.tensor_scalar(T["sgn"][:, :], T["sgn"][:, :], 2.0, -1.0,
                                      ALU.mult, ALU.add), wait=[("se", s.val("se"))])
    v_op(c, lambda e: e.tensor_tensor(T["dvec"][:, :], T["gam"][:, :],
                                      T["sgn"][:, :], ALU.mult))
    v_op(c, lambda e: e.tensor_tensor(T["cvec"][:, :], T["gam"][:, :], bin_in,
                                      ALU.mult))
    v_op(c, lambda e: e.tensor_tensor(T["cvec"][:, :], bin_in, T["cvec"][:, :],
                                      ALU.subtract))


def plan_sinkhorn(c):
    T, s = c.T, c.s
    NG = NT * H * 8
    Sv = T["S"][:, :].rearrange("p (g j) -> p g j", j=8)
    Sc = T["S"][:, :].rearrange("p (g i j) -> p g j i", i=8, j=8)
    rr_row = T["rrec"][:, :].rearrange("p (g o) -> p g o", o=1).broadcast_to([P, NG, 8])
    rr_col = (T["rrec"][:, :].rearrange("p (g j) -> p g j", j=8).unsqueeze(3)
              .broadcast_to([P, NT * H, 8, 8]))
    for it in range(5):
        v_op(c, lambda e: e.tensor_reduce(T["rsum"][:, :], Sv, AXX, ALU.add))
        v_op(c, lambda e: e.reciprocal(T["rrec"][:, :], T["rsum"][:, :]))
        v_op(c, lambda e: e.tensor_tensor(Sv, Sv, rr_row, ALU.mult))
        v_op(c, lambda e: e.tensor_reduce(T["rsum"][:, :], Sc, AXX, ALU.add))
        v_op(c, lambda e: e.reciprocal(T["rrec"][:, :], T["rsum"][:, :]))
        v_op(c, lambda e: e.tensor_tensor(Sc, Sc, rr_col, ALU.mult))
    if c.tap("S", [P, NT * 256]):
        dma(c, lambda e: e.dma_start(out=c.taps["S"][:, :], in_=T["S"][:, :]),
            wait=[("ve", s.val("ve"))], sem="st")


def plan_greedy(c):
    T, s = c.T, c.s
    NTH = NT * H
    v_op(c, lambda e: e.tensor_reduce(
        T["Rmax"][:, :], T["S"][:, :].rearrange("p (g j) -> p g j", j=8), AXX, ALU.max))
    Ri = T["Rmax"][:, :].rearrange("p (g i) -> p g i", i=8).unsqueeze(3).broadcast_to([P, NTH, 8, 8])
    Rp = T["Rmax"][:, :].rearrange("p (g i) -> p g i", i=8).unsqueeze(2).broadcast_to([P, NTH, 8, 8])
    gt_v = T["gtmp"][:, :].rearrange("p (g i j) -> p g i j", i=8, j=8)
    st_v = T["sc_t0"][:, :].rearrange("p (g i j) -> p g i j", i=8, j=8)[:, 0:NTH]
    ii = T["iota8"][:, :].unsqueeze(1).unsqueeze(3).broadcast_to([P, NTH, 8, 8])
    ip = T["iota8"][:, :].unsqueeze(1).unsqueeze(1).broadcast_to([P, NTH, 8, 8])
    v_op(c, lambda e: e.tensor_tensor(gt_v, Rp, Ri, ALU.is_gt))
    v_op(c, lambda e: e.tensor_reduce(T["grk"][:, :], gt_v, AXX, ALU.add))
    v_op(c, lambda e: e.tensor_tensor(gt_v, Rp, Ri, ALU.is_equal))
    v_op(c, lambda e: e.tensor_tensor(st_v, ii, ip, ALU.is_gt))
    v_op(c, lambda e: e.tensor_tensor(gt_v, gt_v, st_v, ALU.mult))
    v_op(c, lambda e: e.tensor_reduce(T["gsm"][:, :], gt_v, AXX, ALU.add))
    v_op(c, lambda e: e.tensor_tensor(T["grk"][:, :], T["grk"][:, :],
                                      T["gsm"][:, :], ALU.add))

    Sv2 = T["S"][:, :].rearrange("p (g i j) -> p g j i", i=8, j=8)
    gsel_bc = T["gsel"][:, :].rearrange("p (g i) -> p g i", i=8).unsqueeze(2).broadcast_to(
        [P, NTH, 8, 8])
    gtv2 = T["gtmp"][:, :].rearrange("p (g j i) -> p g j i", i=8, j=8)
    for r in range(8):
        v_op(c, lambda e, r=r: e.tensor_single_scalar(
            T["gsel"][:, :], T["grk"][:, :], float(r), ALU.is_equal))
        v_op(c, lambda e: e.tensor_tensor(gtv2, Sv2, gsel_bc, ALU.mult))
        v_op(c, lambda e: e.tensor_reduce(T["gsr"][:, :], gtv2, AXX, ALU.add))
        v_op(c, lambda e: e.tensor_tensor(T["gsm"][:, :], T["gsr"][:, :],
                                          T["AVm"][:, :], ALU.add))
        v_op(c, lambda e: e.tensor_reduce(
            T["gm1"][:, :], T["gsm"][:, :].rearrange("p (g j) -> p g j", j=8),
            AXX, ALU.max))
        v_op(c, lambda e: e.tensor_tensor(
            T["gcs"][:, :], T["gsm"][:, :],
            T["gm1"][:, :].unsqueeze(2).broadcast_to([P, NTH, 8]),
            ALU.is_ge))
        v_op(c, lambda e: e.scalar_tensor_tensor(
            T["gsm"][:, :], T["gcs"][:, :], -16.0,
            T["iota8"][:, :].unsqueeze(1).broadcast_to([P, NTH, 8]),
            ALU.mult, ALU.add))
        v_op(c, lambda e: e.tensor_reduce(
            T["gci"][:, :], T["gsm"][:, :].rearrange("p (g j) -> p g j", j=8),
            AXX, ALU.min))
        v_op(c, lambda e: e.tensor_scalar(T["gci"][:, :], T["gci"][:, :],
                                          1.0, 16.0, ALU.mult, ALU.add))
        v_op(c, lambda e: e.tensor_tensor(
            T["gcs"][:, :],
            T["iota8"][:, :].unsqueeze(1).broadcast_to([P, NTH, 8]),
            T["gci"][:, :].unsqueeze(2).broadcast_to([P, NTH, 8]),
            ALU.is_equal))
        v_op(c, lambda e: e.tensor_tensor(
            T["gsm"][:, :], T["gsel"][:, :],
            T["gci"][:, :].unsqueeze(2).broadcast_to([P, NTH, 8]),
            ALU.mult))
        v_op(c, lambda e: e.tensor_tensor(T["PIDX"][:, :], T["PIDX"][:, :],
                                          T["gsm"][:, :], ALU.add))
        v_op(c, lambda e: e.scalar_tensor_tensor(
            T["Rmax"][:, :], T["gsel"][:, :], -BIG, T["Rmax"][:, :],
            ALU.mult, ALU.add))
        v_op(c, lambda e: e.scalar_tensor_tensor(
            T["AVm"][:, :], T["gcs"][:, :], -BIG, T["AVm"][:, :],
            ALU.mult, ALU.add))
    if c.tap("PIDX", [P, NT * HN]):
        dma(c, lambda e: e.dma_start(out=c.taps["PIDX"][:, :], in_=T["PIDX"][:, :]),
            wait=[("ve", s.val("ve"))], sem="st")


def plan_scan_local(c):
    T, s, d = c.T, c.s, c.d
    for t in range(NT):
        for srcn, dstn in (("dvec", "sc_d"), ("cvec", "sc_c"), ("PIDX", "sc_p")):
            def mv(e, t=t, srcn=srcn, dstn=dstn):
                src = c.T[srcn][:, t * HN:(t + 1) * HN]
                dst = c.T[dstn][t * 8:(t + 1) * 8, :].rearrange(
                    "q (k f) -> q k f", f=HN)
                return e.dma_start(out=dst, in_=src)
            dma(c, mv, sem="lay", wait=[("ve", s.val("ve"))] if t == 0 else (),
                tag="lay")
    lay_ld = s.val("lay")

    pv = T["sc_p"][:, :].rearrange("p (k h i) -> p k h i", h=H, i=8)
    sv = T["sc_sig"][:, :].rearrange("p (k h i) -> p k h i", h=H, i=8)
    v_op(c, lambda e: e.tensor_copy(sv[:, 0], pv[:, 0]), wait=[("lay", lay_ld)])
    po = T["sc_po"][:, :].rearrange("p (h i m) -> p h i m", h=H, i=8, m=8)
    t0 = T["sc_t0"][:, :].rearrange("p (k h i m) -> p k h i m", k=LCH, h=H, i=8, m=8)
    for k in range(1, LCH):
        v_op(c, lambda e, k=k: e.tensor_tensor(
            po, pv[:, k].unsqueeze(3).broadcast_to([P, H, 8, 8]),
            T["iota8"][:, :].unsqueeze(1).unsqueeze(1).broadcast_to([P, H, 8, 8]),
            ALU.is_equal))
        v_op(c, lambda e, k=k: e.tensor_tensor(
            t0[:, 0], po,
            sv[:, k - 1].unsqueeze(2).broadcast_to([P, H, 8, 8]),
            ALU.mult))
        v_op(c, lambda e, k=k: e.tensor_reduce(sv[:, k], t0[:, 0], AXX, ALU.add))

    KH = LCH * H
    oh4 = T["sc_oh"][:, :].rearrange("p (q i m) -> p q i m", i=8, m=8)
    sv4 = T["sc_sig"][:, :].rearrange("p (q i) -> p q i", i=8)
    v_op(c, lambda e: e.tensor_tensor(
        oh4, sv4.unsqueeze(3).broadcast_to([P, KH, 8, 8]),
        T["iota8"][:, :].unsqueeze(1).unsqueeze(1).broadcast_to([P, KH, 8, 8]),
        ALU.is_equal))

    dv4 = T["sc_d"][:, :].rearrange("p (q i) -> p q i", i=8)
    cv4 = T["sc_c"][:, :].rearrange("p (q i) -> p q i", i=8)
    dh4 = T["sc_dh"][:, :].rearrange("p (q m) -> p q m", m=8)
    ch4 = T["sc_ch"][:, :].rearrange("p (q m) -> p q m", m=8)
    ohT4 = T["sc_oh"][:, :].rearrange("p (q i m) -> p q m i", i=8, m=8)
    t0m4 = T["sc_t0"][:, :].rearrange("p (q m i) -> p q m i", m=8, i=8)
    v_op(c, lambda e: e.tensor_tensor(
        t0m4, ohT4, dv4.unsqueeze(2).broadcast_to([P, KH, 8, 8]), ALU.mult))
    v_op(c, lambda e: e.tensor_reduce(dh4, t0m4, AXX, ALU.add))
    v_op(c, lambda e: e.tensor_tensor(
        t0m4, ohT4, cv4.unsqueeze(2).broadcast_to([P, KH, 8, 8]), ALU.mult))
    v_op(c, lambda e: e.tensor_reduce(ch4, t0m4, AXX, ALU.add))

    def hmslice(name, off):
        return (T[name][:, :].rearrange("p (k f) -> p k f", f=HN)
                [:, :, off:off + 1].squeeze(2))

    for h in range(H):
        for m in range(8):
            off = h * 8 + m
            v_op(c, lambda e, off=off: e.tensor_tensor_scan(
                hmslice("sc_pi", off), hmslice("sc_dh", off),
                hmslice("onesL", off), 1.0, ALU.mult, ALU.mult))
            v_op(c, lambda e, off=off: e.tensor_tensor_scan(
                hmslice("sc_sh", off), hmslice("sc_dh", off),
                hmslice("sc_ch", off), 0.0, ALU.mult, ALU.add))

    oh = T["sc_oh"][:, :].rearrange("p (k h i m) -> p k h i m", k=LCH, h=H,
                                    i=8, m=8)
    ohL = oh[:, LCH - 1]
    v_op(c, lambda e: e.tensor_copy(
        T["sc_sum"][:, 0:32].rearrange("p (h i) -> p h i", h=H), sv[:, LCH - 1]))
    piL = T["sc_pi"][:, :].rearrange("p (k h m) -> p k h m", h=H, m=8)[:, LCH - 1]
    shL = T["sc_sh"][:, :].rearrange("p (k h m) -> p k h m", h=H, m=8)[:, LCH - 1]
    t0g = T["sc_t0"][:, :].rearrange("p (k h i m) -> p k h i m", k=LCH, h=H, i=8, m=8)[:, 0]
    v_op(c, lambda e: e.tensor_tensor(
        t0g, ohL, piL.unsqueeze(2).broadcast_to([P, H, 8, 8]),
        ALU.mult))
    v_op(c, lambda e: e.tensor_reduce(
        T["sc_sum"][:, 32:64].rearrange("p (h i) -> p h i", h=H), t0g, AXX,
        ALU.add))
    v_op(c, lambda e: e.tensor_tensor(
        t0g, ohL, shL.unsqueeze(2).broadcast_to([P, H, 8, 8]),
        ALU.mult))
    v_op(c, lambda e: e.tensor_reduce(
        T["sc_sum"][:, 64:96].rearrange("p (h i) -> p h i", h=H), t0g, AXX,
        ALU.add))

    for srco, dstn in ((0, "l2_sg"), (32, "l2_rho"), (64, "l2_s")):
        def mv2(e, srco=srco, dstn=dstn):
            src = c.T["sc_sum"][:, srco:srco + 32]
            dst = c.T[dstn][:, :].rearrange("q (r f) -> q r f", f=HN)
            return e.dma_start(out=dst, in_=src)
        dma(c, mv2, sem="lay", wait=[("ve", s.val("ve"))], tag="lay2")
    lay2_ld = s.val("lay")

    pv2 = T["l2_sg"][:, :].rearrange("p (k h i) -> p k h i", h=H, i=8)
    sig2 = T["l2_sig"][:, :].rearrange("p (k h i) -> p k h i", h=H, i=8)
    po2 = T["l2_po"][:, :].rearrange("p (h i m) -> p h i m", h=H, i=8, m=8)
    t02 = T["l2_t0"][:, :].rearrange("p (k h i m) -> p k h i m", k=LCH, h=H, i=8, m=8)
    v_op(c, lambda e: e.tensor_copy(sig2[:, 0], pv2[:, 0]), wait=[("lay", lay2_ld)])
    for k in range(1, LCH):
        v_op(c, lambda e, k=k: e.tensor_tensor(
            po2, pv2[:, k].unsqueeze(3).broadcast_to([8, H, 8, 8]),
            T["iota8"][0:8, :].unsqueeze(1).unsqueeze(1).broadcast_to(
                [8, H, 8, 8]), ALU.is_equal))
        v_op(c, lambda e, k=k: e.tensor_tensor(
            t02[:, 0], po2,
            sig2[:, k - 1].unsqueeze(2).broadcast_to(
                [8, H, 8, 8]), ALU.mult))
        v_op(c, lambda e, k=k: e.tensor_reduce(sig2[:, k], t02[:, 0], AXX, ALU.add))
    KH = LCH * H
    oh24 = T["l2_oh"][:, :].rearrange("p (q i m) -> p q i m", i=8, m=8)
    sg24 = T["l2_sig"][:, :].rearrange("p (q i) -> p q i", i=8)
    v_op(c, lambda e: e.tensor_tensor(
        oh24, sg24.unsqueeze(3).broadcast_to([8, KH, 8, 8]),
        T["iota8"][0:8, :].unsqueeze(1).unsqueeze(1).broadcast_to([8, KH, 8, 8]),
        ALU.is_equal))
    dv24 = T["l2_rho"][:, :].rearrange("p (q i) -> p q i", i=8)
    cv24 = T["l2_s"][:, :].rearrange("p (q i) -> p q i", i=8)
    dh24 = T["l2_dh"][:, :].rearrange("p (q m) -> p q m", m=8)
    ch24 = T["l2_ch"][:, :].rearrange("p (q m) -> p q m", m=8)
    oh2T4 = T["l2_oh"][:, :].rearrange("p (q i m) -> p q m i", i=8, m=8)
    t02m4 = T["l2_t0"][:, :].rearrange("p (q m i) -> p q m i", m=8, i=8)
    v_op(c, lambda e: e.tensor_tensor(
        t02m4, oh2T4, dv24.unsqueeze(2).broadcast_to([8, KH, 8, 8]), ALU.mult))
    v_op(c, lambda e: e.tensor_reduce(dh24, t02m4, AXX, ALU.add))
    v_op(c, lambda e: e.tensor_tensor(
        t02m4, oh2T4, cv24.unsqueeze(2).broadcast_to([8, KH, 8, 8]), ALU.mult))
    v_op(c, lambda e: e.tensor_reduce(ch24, t02m4, AXX, ALU.add))
    def hmslice2(name, off):
        return (T[name][0:8, :].rearrange("p (k f) -> p k f", f=HN)
                [:, :, off:off + 1].squeeze(2))

    for h in range(H):
        for m in range(8):
            off = h * 8 + m
            v_op(c, lambda e, off=off: e.tensor_tensor_scan(
                hmslice2("l2_pi", off), hmslice2("l2_dh", off),
                hmslice2("onesL", off), 1.0, ALU.mult, ALU.mult))
            v_op(c, lambda e, off=off: e.tensor_tensor_scan(
                hmslice2("l2_sh", off), hmslice2("l2_dh", off),
                hmslice2("l2_ch", off), 0.0, ALU.mult, ALU.add))

    oh2L = (T["l2_oh"][:, :].rearrange("p (k h i m) -> p k h i m", k=LCH, h=H,
                                       i=8, m=8)[:, LCH - 1])
    pi2L = T["l2_pi"][:, :].rearrange("p (k h m) -> p k h m", h=H, m=8)[:, LCH - 1]
    sh2L = T["l2_sh"][:, :].rearrange("p (k h m) -> p k h m", h=H, m=8)[:, LCH - 1]
    t02g = (T["l2_t0"][:, :].rearrange("p (k h i m) -> p k h i m", k=LCH, h=H,
                                       i=8, m=8)[:, 0])
    v_op(c, lambda e: e.tensor_tensor(
        t02g, oh2L, pi2L.unsqueeze(2).broadcast_to([8, H, 8, 8]),
        ALU.mult))
    v_op(c, lambda e: e.tensor_reduce(
        T["l2_t0"][:, 1024:1056].rearrange("p (h i) -> p h i", h=H),
        t02g, AXX, ALU.add))
    v_op(c, lambda e: e.tensor_tensor(
        t02g, oh2L, sh2L.unsqueeze(2).broadcast_to([8, H, 8, 8]),
        ALU.mult))
    v_op(c, lambda e: e.tensor_reduce(
        T["l2_t0"][:, 1056:1088].rearrange("p (h i) -> p h i", h=H),
        t02g, AXX, ALU.add))
    ve_sum = s.val("ve")
    gdma(c, lambda e: e.dma_start(out=d["cc_in"][0:1, 0:32],
                                  in_=sig2[7:8, LCH - 1].rearrange(
                                      "p h i -> p (h i)")),
         wait=[("ve", ve_sum)], sem="st")
    gdma(c, lambda e: e.dma_start(out=d["cc_in"][0:1, 32:64],
                                  in_=T["l2_t0"][7:8, 1024:1056]), sem="st")
    gdma(c, lambda e: e.dma_start(out=d["cc_in"][0:1, 64:96],
                                  in_=T["l2_t0"][7:8, 1056:1088]), sem="st")


def plan_scan_global(c):
    T, s, d, PSB = c.T, c.s, c.d, c.PSB
    c.s.op("gpsimd", lambda e: e.collective_compute(
        "AllGather", ALU.bypass, replica_groups=[list(range(NCORE))],
        ins=[d["cc_in"][:, :]], outs=[d["cc_out"][:, :]]),
        wait=[("st", s.val("st"))], inc=[("cc", 1)], tag="AG")
    gdma(c, lambda e: e.dma_start(out=T["gsum"][0:1, :], in_=d["cc_out"][:, :]),
         wait=[("cc", s.val("cc"))], sem="lg")
    ag_ld = s.val("lg")

    v_op(c, lambda e: e.tensor_copy(
        T["gcur"][0:1, 0:32].rearrange("p (h i) -> p h i", h=H),
        T["iota8"][0:1, :].unsqueeze(1).broadcast_to([1, H, 8])))
    v_op(c, lambda e: e.memset(T["gcur"][0:1, 32:64], 1.0))
    v_op(c, lambda e: e.memset(T["gcur"][0:1, 64:96], 0.0))
    c.s.op("vector", lambda e: e.nop(), wait=[("lg", ag_ld)],
           inc=[("ve", 1)])
    gc_s = T["gcur"][0:1, 0:32].rearrange("p (h i) -> p h i", h=H)
    gc_r = T["gcur"][0:1, 32:64].rearrange("p (h i) -> p h i", h=H)
    gc_o = T["gcur"][0:1, 64:96].rearrange("p (h i) -> p h i", h=H)
    ohg = T["goh"][:, :].rearrange("p (h i m) -> p h i m", h=H, i=8, m=8)
    t0g = T["gt0"][:, :].rearrange("p (h i m) -> p h i m", h=H, i=8, m=8)
    for r in range(NCORE - 1):
        sr_s = (T["gsum"][0:1, r * 96:r * 96 + 32]
                .rearrange("p (h i) -> p h i", h=H))
        sr_r = (T["gsum"][0:1, r * 96 + 32:r * 96 + 64]
                .rearrange("p (h i) -> p h i", h=H))
        sr_o = (T["gsum"][0:1, r * 96 + 64:r * 96 + 96]
                .rearrange("p (h i) -> p h i", h=H))
        v_op(c, lambda e, sr_s=sr_s: e.tensor_tensor(
            ohg, sr_s.unsqueeze(3).broadcast_to([1, H, 8, 8]),
            T["iota8"][0:1, :].unsqueeze(1).unsqueeze(1).broadcast_to(
                [1, H, 8, 8]), ALU.is_equal))

        def gath(dst, srcv):
            v_op(c, lambda e, srcv=srcv: e.tensor_tensor(
                t0g, ohg, srcv.unsqueeze(2).broadcast_to(
                    [1, H, 8, 8]), ALU.mult))
            v_op(c, lambda e, dst=dst: e.tensor_reduce(dst, t0g, AXX, ALU.add))

        gn_s = T["gnx"][0:1, 0:32].rearrange("p (h i) -> p h i", h=H)
        gn_r = T["gnx"][0:1, 32:64].rearrange("p (h i) -> p h i", h=H)
        gn_o = T["gnx"][0:1, 64:96].rearrange("p (h i) -> p h i", h=H)
        gath(gn_s, gc_s)
        gath(gn_r, gc_r)
        gath(gn_o, gc_o)
        v_op(c, lambda e, sr_r=sr_r: e.tensor_tensor(gn_r, gn_r, sr_r, ALU.mult))
        v_op(c, lambda e, sr_r=sr_r: e.tensor_tensor(gn_o, gn_o, sr_r, ALU.mult))
        v_op(c, lambda e, sr_o=sr_o: e.tensor_tensor(gn_o, gn_o, sr_o, ALU.add))
        v_op(c, lambda e, r=r: e.tensor_single_scalar(
            T["gmsk"][:, :], T["cid"][:, :], float(r), ALU.is_gt))
        for a_, b_ in ((gn_s, gc_s), (gn_r, gc_r), (gn_o, gc_o)):
            v_op(c, lambda e, a_=a_, b_=b_: e.tensor_tensor(
                a_, a_, b_, ALU.subtract))
            v_op(c, lambda e, a_=a_: e.tensor_tensor(
                a_, a_,
                T["gmsk"][0:1, 0:1].unsqueeze(2)
                .broadcast_to([1, H, 8]), ALU.mult))
            v_op(c, lambda e, a_=a_, b_=b_: e.tensor_tensor(
                b_, b_, a_, ALU.add))

    t_op(c, lambda e: e.matmul(c.PSB[3][:, 0:HN], T["ones1"][:, :],
                               T["gcur"][0:1, 64:96], start=True, stop=True),
         wait=[("ve", s.val("ve"))], tag="rephin")
    v_op(c, lambda e: e.tensor_copy(T["h_in"][:, :], c.PSB[3][:, 0:HN]),
         wait=[("mm", s.val("mm"))])

    hv = T["h_in"][0:8, :].unsqueeze(1)
    pi2f = T["l2_pi"][:, :].rearrange("p (k f) -> p k f", f=HN)
    sh2f = T["l2_sh"][:, :].rearrange("p (k f) -> p k f", f=HN)
    dh2f = T["l2_dh"][:, :].rearrange("p (k f) -> p k f", f=HN)
    v_op(c, lambda e: e.tensor_tensor(
        dh2f, pi2f, hv.broadcast_to([8, LCH, HN]), ALU.mult))
    v_op(c, lambda e: e.tensor_tensor(dh2f, dh2f, sh2f, ALU.add))
    oh24b = T["l2_oh"][:, :].rearrange("p (q i m) -> p q i m", i=8, m=8)
    t024b = T["l2_t0"][:, :].rearrange("p (q i m) -> p q i m", i=8, m=8)
    dh24b = T["l2_dh"][:, :].rearrange("p (q m) -> p q m", m=8)
    v_op(c, lambda e: e.tensor_tensor(
        t024b, oh24b, dh24b.unsqueeze(2).broadcast_to([8, LCH * H, 8, 8]),
        ALU.mult))
    v_op(c, lambda e: e.tensor_reduce(
        T["l2_v"][:, :].rearrange("p (q i) -> p q i", i=8), t024b, AXX, ALU.add))
    ve_v = s.val("ve")

    dma(c, lambda e: e.dma_start(
        out=T["e_full"][:, :],
        in_=T["l2_v"][:, :].rearrange("q (r f) -> q r f", f=HN)),
        sem="lg", wait=[("ve", ve_v)], tag="efull")
    dma(c, lambda e: e.dma_start(out=T["e_ch"][1:P, :], in_=T["e_full"][0:P - 1, :]),
        sem="lg", wait=[("lg", s.val("lg"))])
    v_op(c, lambda e: e.tensor_copy(T["e_ch"][0:1, :], T["h_in"][0:1, :]),
         wait=[("lg", s.val("lg"))])

    ev = T["e_ch"][:, :].unsqueeze(1)
    pi1f = T["sc_pi"][:, :].rearrange("p (k f) -> p k f", f=HN)
    sh1f = T["sc_sh"][:, :].rearrange("p (k f) -> p k f", f=HN)
    hhf = T["sc_hh"][:, :].rearrange("p (k f) -> p k f", f=HN)
    v_op(c, lambda e: e.tensor_tensor(
        hhf, pi1f, ev.broadcast_to([P, LCH, HN]), ALU.mult))
    v_op(c, lambda e: e.tensor_tensor(hhf, hhf, sh1f, ALU.add))
    oh14 = T["sc_oh"][:, :].rearrange("p (q i m) -> p q i m", i=8, m=8)
    t014 = T["sc_t0"][:, :].rearrange("p (q i m) -> p q i m", i=8, m=8)
    hh14 = T["sc_hh"][:, :].rearrange("p (q m) -> p q m", m=8)
    v_op(c, lambda e: e.tensor_tensor(
        t014, oh14, hh14.unsqueeze(2).broadcast_to([P, LCH * H, 8, 8]),
        ALU.mult))
    v_op(c, lambda e: e.tensor_reduce(
        T["sc_ys"][:, :].rearrange("p (q i) -> p q i", i=8), t014, AXX, ALU.add))
    ve_ys = s.val("ve")
    if c.tap("ys", [P, LCH * HN]):
        dma(c, lambda e: e.dma_start(out=c.taps["ys"][:, :], in_=T["sc_ys"][:, :]),
            wait=[("ve", ve_ys)], sem="st")

    v_op(c, lambda e: e.tensor_copy(T["ys_hi"][:, :], T["sc_ys"][:, :]))
    v_op(c, lambda e: e.tensor_tensor(T["ys_lo"][:, :], T["sc_ys"][:, :],
                                      T["ys_hi"][:, :], ALU.subtract))
    ve_split = s.val("ve")
    for nm_src, nm_dst, bank in (("ys_hi", "ysT_hi", 4), ("ys_lo", "ysT_lo", 5)):
        for k4 in range(4):
            def trs(e, nm_src=nm_src, k4=k4, bank=bank):
                last = None
                for kk in range(4):
                    k = k4 * 4 + kk
                    inv = (T[nm_src][:, :].rearrange("p (k u) -> p k u", u=HN)
                           [:, k:k + 1, :].squeeze(1))
                    psv = PSB[bank][:, :].bitcast(BF16)
                    last = e.transpose(psv[0:HN, kk * P:(kk + 1) * P],
                                       inv, T["ident"][:, :])
                return last
            t_op(c, trs, wait=[("ve", max(ve_split, s.val("ve")))],
                 tag=f"ysT_{nm_src}_{k4}")
            mmt = s.val("mm")

            def evac(e, nm_dst=nm_dst, k4=k4, bank=bank):
                outv = (T[nm_dst][0:HN, :].rearrange("p (c k) -> p k c", k=LCH)
                        [:, k4 * 4:(k4 + 1) * 4, :])
                psv = PSB[bank][:, :].bitcast(BF16)[0:HN, 0:4 * P]
                return e.tensor_copy(outv, psv.rearrange(
                    "p (kk c) -> p kk c", kk=4))
            v_op(c, evac, wait=[("mm", mmt)])
    v_op(c, lambda e: e.memset(T["ysT_hi"][32:33, :], 1.0))
    v_op(c, lambda e: e.memset(T["ysT_lo"][32:33, :], 0.0))


def plan_out_proj(c):
    T, s, d, PSB = c.T, c.s, c.d, c.PSB
    c.s.op("vector", lambda e: e.nop(), wait=[("wq", c.wq_done)])

    def ldg0(e):
        inap = bass.AP(d["ccw_out"].tensor, 0, [[OF, P], [SEGW, KD], [1, FT]])
        return e.dma_start(out=T["wstage"].rearrange(
            "p (k f) -> p k f", k=KD), in_=inap)

    def ldu0(e):
        inap = bass.AP(d["ccw_out"].tensor, P * OF,
                       [[OF, P], [SEGW, KD], [1, FT]])
        return e.dma_start(out=T["wstage2"].rearrange(
            "p (k f) -> p k f", k=KD), in_=inap)
    dma(c, ldg0, sem="lw0", wait=[("ve", s.val("ve")), ("cc", c.cc_w)],
        tag="wg0pre")
    gdma(c, ldu0, sem="lw1", wait=[("ve", s.val("ve")), ("cc", c.cc_w)],
         tag="wu0pre")
    c.pre_lw0, c.pre_lw1 = s.val("lw0"), s.val("lw1")

    def zt(t):
        t_ = T["zA"] if t < 8 else T["zB"]
        return t_[:, (t % 8) * D:(t % 8 + 1) * D]

    evac_ve = {}
    xuse = {0: 0, 1: 0}
    for t in range(NT):
        buf = "xstage" if t % 2 == 0 else "xstage2"
        lsem = "lx0" if t % 2 == 0 else "lx1"
        dma(c, lambda e, t=t, buf=buf: e.dma_start(
            out=T[buf][:, :], in_=d["x"][t * P:(t + 1) * P, :]),
            sem=lsem, wait=[("ve", xuse[t % 2])], tag=f"xr{t}")
        ld_x = s.val(lsem)
        for dc in range(2):
            bank = (t % 2) * 2 + dc

            def mms(e, t=t, dc=dc, bank=bank):
                last = None
                first = True
                for a_, b_ in (("ysT_hi", "outWb_hi"), ("ysT_hi", "outWb_lo"),
                               ("ysT_lo", "outWb_hi")):
                    last = e.matmul(
                        PSB[bank][:, 0:512],
                        T[a_][0:33, t * P:(t + 1) * P],
                        T[b_][0:33, dc * 512:(dc + 1) * 512],
                        start=first, stop=(a_ == "ysT_lo"))
                    first = False
                return last
            t_op(c, mms, wait=[("ve", max(s.val("ve"), evac_ve.get(bank, 0)))],
                 tag=f"oproj{t}_{dc}")
            mmv = s.val("mm")
            c.s.op("vector", lambda e, t=t, dc=dc, bank=bank, buf=buf:
                   e.scalar_tensor_tensor(
                       zt(t)[:, dc * 512:(dc + 1) * 512],
                       T[buf][:, dc * 512:(dc + 1) * 512], 2.0,
                       PSB[bank][:, 0:512], ALU.mult, ALU.add),
                   wait=[("mm", mmv), (lsem, ld_x)], inc=[("ve", 1)])
            evac_ve[bank] = s.val("ve")
        xuse[t % 2] = s.val("ve")
        v_op(c, lambda e, t=t: e.tensor_reduce(T["stat1"][:, t:t + 1],
                                               zt(t), AXX, ALU.add))
        s_op(c, lambda e, t=t: e.activation(T["qf_t"][:, 0:D], zt(t),
                                            AF.Square),
             wait=[("ve", s.val("ve"))])
        v_op(c, lambda e, t=t: e.tensor_reduce(T["stat2"][:, t:t + 1],
                                               T["qf_t"][:, 0:D], AXX, ALU.add),
             wait=[("se", s.val("se"))])

    v_op(c, lambda e: e.tensor_scalar(T["mu"][:, :], T["stat1"][:, :],
                                      1.0 / D, None, ALU.mult))
    v_op(c, lambda e: e.tensor_tensor(T["stat1"][:, :], T["mu"][:, :],
                                      T["mu"][:, :], ALU.mult))
    v_op(c, lambda e: e.tensor_scalar(T["stat2"][:, :], T["stat2"][:, :],
                                      1.0 / D, LN_EPS, ALU.mult, ALU.add))
    v_op(c, lambda e: e.tensor_tensor(T["stat2"][:, :], T["stat2"][:, :],
                                      T["stat1"][:, :], ALU.subtract))
    _rsqrt_inplace(c, T["stat2"][:, :], T["rstd"][:, :], T["st_s"][:, :])

    ob_se = {"yout": 0, "yt1": 0}
    for t in range(NT):
        ob = "yout" if t % 2 == 0 else "yt1"
        v_op(c, lambda e, t=t, ob=ob: e.tensor_scalar(
            T[ob][:, :], zt(t), T["mu"][:, t:t + 1],
            T["rstd"][:, t:t + 1], ALU.subtract, ALU.mult),
            wait=[("qa", 16 * max(0, t - 1)), ("se", ob_se[ob])])
        ve_y = s.val("ve")
        dma(c, lambda e, t=t, ob=ob: e.dma_start(
            out=d["ybuf"][t * P:(t + 1) * P, :], in_=T[ob][:, :]),
            wait=[("ve", ve_y)], sem="qa", tag=f"ystore{t}")
        v_op(c, lambda e, t=t, ob=ob: e.tensor_reduce(
            T["sa"][:, t:t + 1], T[ob][:, :], AXX, ALU.max,
            apply_absolute_value=True))
        v_op(c, lambda e, t=t: e.tensor_single_scalar(
            T["sa"][:, t:t + 1], T["sa"][:, t:t + 1], 1e-5, ALU.max))
        v_op(c, lambda e, t=t: e.reciprocal(T["sa"][:, t:t + 1],
                                            T["sa"][:, t:t + 1]))
        v_op(c, lambda e, t=t: e.tensor_scalar(T["sa"][:, t:t + 1],
                                               T["sa"][:, t:t + 1], 127.0,
                                               None, ALU.mult))
        s_op(c, lambda e, t=t, ob=ob: e.activation(
            T["qv_t"][:, :], T[ob][:, :], AF.Identity,
            bias=T["magicb"][:, :], scale=T["sa"][:, t:t + 1]),
            wait=[("ve", s.val("ve"))])
        ob_se[ob] = s.val("se")
        s_op(c, lambda e: e.activation(T["xq_t"][:, :], T["qv_t"][:, :],
                                       AF.Identity, bias=T["nmagicb"][:, :]),
             wait=[("mm", getattr(c, "op_tr_mm", 0))])
        se_q = s.val("se")
        for half in range(2):
            bank = 4 + half

            def trs(e, t=t, half=half, bank=bank):
                last = None
                psv = PSB[bank][:, :].bitcast(BF16)
                for j in range(4):
                    cch = half * 4 + j
                    last = e.transpose(
                        psv[:, j * P:(j + 1) * P],
                        T["xq_t"][:, cch * P:(cch + 1) * P], T["ident"][:, :])
                return last
            t_op(c, trs, wait=[("se", se_q),
                               ("ve", getattr(c, "op_tr_ve", {}).get(half, 0))],
                 tag=f"xtr{t}_{half}")
            mmv2 = s.val("mm")
            c.op_tr_mm = mmv2

            def evac(e, t=t, half=half, bank=bank):
                outv = (T["x_intT"][:, :].rearrange("p (c s) -> p c s", c=KD)
                        [:, half * 4:(half + 1) * 4, t * P:(t + 1) * P])
                psv = PSB[bank][:, :].bitcast(BF16)[:, 0:4 * P]
                return e.tensor_copy(outv, psv.rearrange(
                    "p (j f) -> p j f", j=4))
            v_op(c, evac, wait=[("mm", mmv2)])
            if not hasattr(c, "op_tr_ve"):
                c.op_tr_ve = {}
            c.op_tr_ve[half] = s.val("ve")


def plan_ffn(c):
    T, s, d, PSB = c.T, c.s, c.d, c.PSB

    def z_t(ti):
        t_ = T["zA"] if ti < 2 else T["zB"]
        return t_[:, (ti % 2) * OF:(ti % 2) * OF + OF]

    def iqreg(ti):
        return T[("iqA0", "iqA1", "iqB0", "iqB1")[ti]]

    GST = ["wstage", "wstageB"]
    UST = ["wstage2", "wstage2B"]
    WDB = ["wdst", "wdst2", "outWb_hi", "outWb_lo"]

    v_op(c, lambda e: e.reciprocal(T["rsa"][:, :], T["sa"][:, :]))
    v_op(c, lambda e: e.tensor_scalar(T["rgq"][:, :], T["rsa"][:, :],
                                      T["whalf"][:, 0:1], None, ALU.mult))
    v_op(c, lambda e: e.tensor_scalar(T["ruq"][:, :], T["rsa"][:, :],
                                      T["whalf"][:, 1:2], None, ALU.mult))

    ldg_v, ldu_v = {}, {}
    mmQ = {}
    silu_se, evac_ve = {}, {}
    stt_ve = {g_: {} for g_ in range(-1, NFT)}
    mm_dn = {}

    def sched_wload(Q):
        w_ = [("mm", mmQ[Q - 2])] if Q - 2 in mmQ else []
        if Q <= 1:
            w_.append(("cc", c.cc_w))

        def ldg(e, Q=Q):
            inap = bass.AP(d["ccw_out"].tensor, (Q % 8) * FT,
                           [[OF, P], [SEGW, KD], [1, FT]])
            return e.dma_start(out=T[GST[Q % 2]].rearrange(
                "p (k f) -> p k f", k=KD), in_=inap)

        def ldu(e, Q=Q):
            inap = bass.AP(d["ccw_out"].tensor, P * OF + (Q % 8) * FT,
                           [[OF, P], [SEGW, KD], [1, FT]])
            return e.dma_start(out=T[UST[Q % 2]].rearrange(
                "p (k f) -> p k f", k=KD), in_=inap)
        dma(c, ldg, sem="lw0", wait=w_, tag=f"wg{Q}")
        ldg_v[Q] = s.val("lw0")
        gdma(c, ldu, sem="lw1", wait=w_, tag=f"wu{Q}")
        ldu_v[Q] = s.val("lw1")

    if hasattr(c, "pre_lw0"):
        ldg_v[0], ldu_v[0] = c.pre_lw0, c.pre_lw1
        sched_wload(1)
    else:
        sched_wload(0)
        sched_wload(1)

    for g in range(NFT):
        for q8 in range(8):
            Q = g * 8 + q8
            for ti in range(4):
                t_ = g * 4 + ti
                bg, bu = ti * 2, ti * 2 + 1

                def gumm(e, Q=Q, t_=t_, bg=bg, bu=bu):
                    last = None
                    gst, ust = T[GST[Q % 2]], T[UST[Q % 2]]
                    for k in range(KD):
                        xs = T["x_intT"][:, k * TCORE + t_ * P:
                                         k * TCORE + (t_ + 1) * P]
                        last = e.matmul(PSB[bg][:, 0:FT], xs,
                                        gst[:, k * FT:(k + 1) * FT],
                                        start=(k == 0), stop=(k == KD - 1))
                        last = e.matmul(PSB[bu][:, 0:FT], xs,
                                        ust[:, k * FT:(k + 1) * FT],
                                        start=(k == 0), stop=(k == KD - 1))
                    return last
                w_ = [("lw0", ldg_v[Q]), ("lw1", ldu_v[Q])]
                if q8 == 0:
                    if ti in stt_ve[g - 1]:
                        w_.append(("ve", stt_ve[g - 1][ti]))
                else:
                    w_ += [("se", silu_se[(Q - 1, ti)]),
                           ("ve", evac_ve[(Q - 1, ti)])]
                t_op(c, gumm, wait=w_, tag=f"gu{Q}_{ti}")
                mmv = s.val("mm")
                mmQ[Q] = mmv
                gsb = "gsbA" if ti % 2 == 0 else "gsbB"
                sw = [("mm", mmv)]
                if (Q, ti - 2) in evac_ve:
                    sw.append(("ve", evac_ve[(Q, ti - 2)]))
                elif (Q - 1, ti + 2) in evac_ve:
                    sw.append(("ve", evac_ve[(Q - 1, ti + 2)]))
                s_op(c, lambda e, t_=t_, bg=bg, gsb=gsb: e.activation(
                    T[gsb][:, :], PSB[bg][:, 0:FT], AF.Silu,
                    scale=T["rgq"][:, t_:t_ + 1]), wait=sw, tag=f"si{Q}_{ti}")
                silu_se[(Q, ti)] = s.val("se")
                v_op(c, lambda e, q8=q8, ti=ti, t_=t_, bu=bu: e.tensor_scalar(
                    z_t(ti)[:, q8 * FT:(q8 + 1) * FT], PSB[bu][:, 0:FT],
                    T["ruq"][:, t_:t_ + 1], None, ALU.mult),
                    wait=[("mm", mmv)])
                c.s.op("vector", lambda e, q8=q8, ti=ti, gsb=gsb:
                       e.tensor_tensor(
                           z_t(ti)[:, q8 * FT:(q8 + 1) * FT],
                           z_t(ti)[:, q8 * FT:(q8 + 1) * FT],
                           T[gsb][:, :], ALU.mult),
                       wait=[("se", silu_se[(Q, ti)])], inc=[("ve", 1)])
                evac_ve[(Q, ti)] = s.val("ve")
                v_op(c, lambda e, q8=q8, ti=ti: e.tensor_reduce(
                    T["Mq"][:, ti * 8 + q8:ti * 8 + q8 + 1],
                    z_t(ti)[:, q8 * FT:(q8 + 1) * FT], AXX, ALU.max,
                    apply_absolute_value=True))
            if Q + 2 < NFT * 8:
                sched_wload(Q + 2)

        v_op(c, lambda e: e.tensor_reduce(
            T["mt"][:, :], T["Mq"][:, :].rearrange("p (i q) -> p i q", i=4),
            AXX, ALU.max))
        v_op(c, lambda e: e.tensor_single_scalar(T["mt"][:, :], T["mt"][:, :],
                                                 1e-5, ALU.max))
        v_op(c, lambda e, g=g: e.tensor_scalar(
            T["dsc"][:, g * 4:(g + 1) * 4], T["mt"][:, :],
            T["whalf"][:, 2:3], 1.0 / 127.0, ALU.mult, ALU.mult))
        v_op(c, lambda e, g=g: e.reciprocal(T["sitq"][:, g * 4:(g + 1) * 4],
                                            T["mt"][:, :]))
        v_op(c, lambda e, g=g: e.tensor_scalar(
            T["sitq"][:, g * 4:(g + 1) * 4], T["sitq"][:, g * 4:(g + 1) * 4],
            127.0, None, ALU.mult))
        ve_scales = s.val("ve")

        if g == 0 and c.tap("z0", [P, 4 * OF]):
            dma(c, lambda e: e.dma_start(out=c.taps["z0"][:, 0:2 * OF],
                                         in_=T["zA"][:, :]),
                wait=[("ve", ve_scales)], sem="st")
            dma(c, lambda e: e.dma_start(out=c.taps["z0"][:, 2 * OF:4 * OF],
                                         in_=T["zB"][:, :]),
                wait=[("ve", ve_scales)], sem="st")
        if g == 0 and c.tap("Mq0", [P, 32]):
            dma(c, lambda e: e.dma_start(out=c.taps["Mq0"][:, :],
                                         in_=T["Mq"][:, :]),
                wait=[("ve", ve_scales)], sem="st")
        tap_hold = s.val("st")

        quant_done = {}
        for ti in range(4):
            t_ = g * 4 + ti
            if ti < 1:
                s_op(c, lambda e, ti=ti, t_=t_: e.activation(
                    z_t(ti)[:, :], z_t(ti)[:, :], AF.Identity,
                    bias=T["magicb"][:, :], scale=T["sitq"][:, t_:t_ + 1]),
                    wait=[("ve", ve_scales), ("st", tap_hold)],
                    tag=f"qa{g}_{ti}")
                s_op(c, lambda e, ti=ti: e.activation(
                    iqreg(ti)[:, 0:OF], z_t(ti)[:, :], AF.Identity,
                    bias=T["nmagicb"][:, :]), tag=f"qb{g}_{ti}")
                quant_done[ti] = ("se", s.val("se"))
            else:
                v_op(c, lambda e, ti=ti, t_=t_: e.tensor_scalar(
                    z_t(ti)[:, :], z_t(ti)[:, :], T["sitq"][:, t_:t_ + 1],
                    MAGIC, ALU.mult, ALU.add),
                    wait=[("st", tap_hold)], tag=f"qa{g}_{ti}")
                s_op(c, lambda e, ti=ti: e.activation(
                    iqreg(ti)[:, 0:OF], z_t(ti)[:, :], AF.Identity,
                    bias=T["nmagicb"][:, :]),
                    wait=[("ve", s.val("ve"))], tag=f"qb{g}_{ti}")
                quant_done[ti] = ("se", s.val("se"))

        fill_done = {}
        cum = {"ve": 0, "se": 0}
        tr_prev = {}
        for f in range(4):
            for ti in range(4):
                bank = (f * 4 + ti) % 8

                def trf(e, f=f, ti=ti, bank=bank):
                    psv = PSB[bank][:, :].bitcast(BF16)
                    last = None
                    for j in range(8):
                        oc = f * 8 + j
                        last = e.transpose(
                            psv[:, j * P:(j + 1) * P],
                            iqreg(ti)[:, oc * P:(oc + 1) * P],
                            T["ident"][:, :])
                    return last
                w_ = [quant_done[ti]]
                if f == 0:
                    w_.append(("ve", ve_scales))
                if bank in tr_prev:
                    w_.append(tr_prev[bank])
                t_op(c, trf, wait=w_, tag=f"tr{g}_{f}_{ti}")
                mmv = s.val("mm")

                def evf(e, f=f, ti=ti, bank=bank, cp=None):
                    src = PSB[bank][:, :].bitcast(BF16)[:, 0:1024]
                    dst = iqreg(ti)[:, OF + f * 1024:OF + (f + 1) * 1024]
                    if cp == "s":
                        return e.activation(dst, src, AF.Identity)
                    return e.tensor_copy(dst, src)
                if (f + ti) % 2 == 0:
                    v_op(c, lambda e, evf=evf: evf(e), wait=[("mm", mmv)])
                    mk = ("ve", s.val("ve"))
                else:
                    s_op(c, lambda e, evf=evf: evf(e, cp="s"),
                         wait=[("mm", mmv)])
                    mk = ("se", s.val("se"))
                tr_prev[bank] = mk
                cum[mk[0]] = mk[1]
            fill_done[f] = [("ve", cum["ve"]), ("se", cum["se"])]

        if g == 0 and c.tap("iqt0", [P, 2 * OF], BF16):
            dma(c, lambda e: e.dma_start(out=c.taps["iqt0"][:, :],
                                         in_=iqreg(0)[:, :]),
                wait=fill_done[3], sem="st")
        if g == 0 and c.tap("iqt3", [P, 2 * OF], BF16):
            dma(c, lambda e: e.dma_start(out=c.taps["iqt3"][:, :],
                                         in_=iqreg(3)[:, :]),
                wait=fill_done[3], sem="st")
        if g == 0 and c.tap("scl0", [P, 3 * NT]):
            dma(c, lambda e: e.dma_start(out=c.taps["scl0"][:, 0:NT],
                                         in_=T["sitq"][:, :]),
                wait=[("ve", ve_scales)], sem="st")
            dma(c, lambda e: e.dma_start(out=c.taps["scl0"][:, NT:2 * NT],
                                         in_=T["dsc"][:, :]), sem="st")
            dma(c, lambda e: e.dma_start(out=c.taps["scl0"][:, 2 * NT:3 * NT],
                                         in_=T["rgq"][:, :]), sem="st")
        tap_hold2 = s.val("st")

        yld_v = {}
        for ti in range(4):
            t_ = g * 4 + ti
            w_ = ([("ve", stt_ve[g - 1][ti])] if ti in stt_ve[g - 1] else [])
            dma(c, lambda e, ti=ti, t_=t_: e.dma_start(
                out=T["ystg%d" % ti][:, :],
                in_=d["ybuf"][t_ * P:(t_ + 1) * P, :]),
                sem="ly0", wait=w_, tag=f"yld{t_}")
            yld_v[ti] = s.val("ly0")

        for oc in range(NOC):
            buf = WDB[oc % 4]
            wsem = "lwd0" if oc % 2 == 0 else "lwd1"
            w_ = []
            if (g, oc - 4) in mm_dn:
                w_.append(("mm", mm_dn[(g, oc - 4)]))
            elif (g - 1, oc + 28) in mm_dn:
                w_.append(("mm", mm_dn[(g - 1, oc + 28)]))
            if g == 0 and oc < 4:
                w_.append(("cc", c.cc_w))
            ldfn = dma if oc % 2 == 0 else gdma
            wd_ap = bass.AP(d["ccw_out"].tensor,
                            (oc // 4) * SEGW + 2 * P * OF + (oc % 4) * P * D,
                            [[D, P], [1, D]])
            ldfn(c, lambda e, buf=buf, wd_ap=wd_ap: e.dma_start(
                out=T[buf][:, :], in_=wd_ap),
                sem=wsem, wait=w_, tag=f"wd{g}_{oc}")
            ld_wd = s.val(wsem)

            def dmm(e, oc=oc, buf=buf):
                last = None
                for ti in range(4):
                    st_ap = iqreg(ti)[:, OF + oc * P:OF + (oc + 1) * P]
                    for dc in range(2):
                        last = e.matmul(
                            PSB[ti * 2 + dc][:, 0:512], st_ap,
                            T[buf][:, dc * 512:(dc + 1) * 512],
                            start=(oc == 0), stop=(oc == NOC - 1))
                return last
            t_op(c, dmm, wait=[(wsem, ld_wd)]
                 + fill_done[3 if oc == 0 else oc // 8],
                 tag=f"dn{g}_{oc}")
            mm_dn[(g, oc)] = s.val("mm")
        mmd = mm_dn[(g, NOC - 1)]

        for ti in range(4):
            t_ = g * 4 + ti
            for dc in range(2):
                c.s.op("vector", lambda e, ti=ti, t_=t_, dc=dc:
                       e.scalar_tensor_tensor(
                           z_t(ti)[:, dc * 512:(dc + 1) * 512],
                           PSB[ti * 2 + dc][:, 0:512],
                           T["dsc"][:, t_:t_ + 1],
                           T["ystg%d" % ti][:, dc * 512:(dc + 1) * 512],
                           ALU.mult, ALU.add),
                       wait=[("mm", mmd), ("ly0", yld_v[ti]),
                             ("st", tap_hold2)],
                       inc=[("ve", 1)])
            stt_ve[g][ti] = s.val("ve")
        g4 = slice(g * 4, (g + 1) * 4)
        for ti in range(4):
            t_ = g * 4 + ti
            zst = z_t(ti)[:, 0:D]
            v_op(c, lambda e, t_=t_, zst=zst: e.tensor_reduce(
                T["stat1"][:, t_:t_ + 1], zst, AXX, ALU.add))
            s_op(c, lambda e, zst=zst: e.activation(
                T["qf_t"][:, 0:D], zst, AF.Square),
                wait=[("ve", s.val("ve"))])
            v_op(c, lambda e, t_=t_: e.tensor_reduce(
                T["stat2"][:, t_:t_ + 1], T["qf_t"][:, 0:D], AXX, ALU.add),
                wait=[("se", s.val("se"))])
        v_op(c, lambda e, g4=g4: e.tensor_scalar(
            T["mu"][:, g4], T["stat1"][:, g4], 1.0 / D, None, ALU.mult))
        v_op(c, lambda e, g4=g4: e.tensor_tensor(
            T["stat1"][:, g4], T["mu"][:, g4], T["mu"][:, g4], ALU.mult))
        v_op(c, lambda e, g4=g4: e.tensor_scalar(
            T["stat2"][:, g4], T["stat2"][:, g4], 1.0 / D, LN_EPS,
            ALU.mult, ALU.add))
        v_op(c, lambda e, g4=g4: e.tensor_tensor(
            T["stat2"][:, g4], T["stat2"][:, g4], T["stat1"][:, g4],
            ALU.subtract))
        _rsqrt_inplace(c, T["stat2"][:, g4], T["rstd"][:, g4],
                       T["st_s"][:, g4])
        dq_prev = s.val("dq")
        for ti in range(4):
            t_ = g * 4 + ti
            ob = ("yout", "qv_t", "yt1", "zt1")[ti]
            v_op(c, lambda e, t_=t_, ti=ti, ob=ob: e.tensor_scalar(
                T[ob][:, :], z_t(ti)[:, 0:D], T["mu"][:, t_:t_ + 1],
                T["rstd"][:, t_:t_ + 1], ALU.subtract, ALU.mult),
                wait=[("dq", dq_prev)])
            dma(c, lambda e, t_=t_, ob=ob: e.dma_start(
                out=d["out"][t_ * P:(t_ + 1) * P, :], in_=T[ob][:, :]),
                wait=[("ve", s.val("ve"))], sem="dq", tag=f"out{t_}")


def plan_finish(c):
    s = c.s
    allsems = [(n, s.val(n)) for n in s.cnt if n != "ph"]
    s.op("gpsimd", lambda e: e.nop(), wait=allsems)



_BUILT = None


def _get_built():
    global _BUILT
    if _BUILT is None:
        _BUILT = build_core()
    return _BUILT


def prep_in_maps(inputs):
    xfull = np.ascontiguousarray(np.asarray(inputs["L0_attn_x"],
                                            dtype=np.float32)[0])
    Wcat = np.ascontiguousarray(np.concatenate(
        [np.asarray(inputs["L0_attn_sign_W"], np.float32),
         np.asarray(inputs["L0_attn_perm_W"], np.float32),
         np.asarray(inputs["L0_attn_gate_W"], np.float32),
         np.asarray(inputs["L0_attn_in_W"], np.float32)], axis=1))
    bcat = np.ascontiguousarray(np.concatenate(
        [np.asarray(inputs["L0_attn_sign_b"], np.float32),
         np.asarray(inputs["L0_attn_perm_b"], np.float32),
         np.asarray(inputs["L0_attn_gate_b"], np.float32),
         np.asarray(inputs["L0_attn_in_b"], np.float32)])[None, :])
    lnw = np.ascontiguousarray(
        np.asarray(inputs["L0_attn_ln_w"], np.float32).reshape(KD, P).T)
    lnb = np.ascontiguousarray(
        np.asarray(inputs["L0_attn_ln_b"], np.float32).reshape(KD, P).T)
    outWb = np.ascontiguousarray(np.concatenate(
        [np.asarray(inputs["L0_attn_out_W"], np.float32),
         np.asarray(inputs["L0_attn_out_b"], np.float32)[None, :]], axis=0))
    wgT = np.ascontiguousarray(np.asarray(inputs["L0_ffn_w_gate"], np.float32).T)
    wuT = np.ascontiguousarray(np.asarray(inputs["L0_ffn_w_up"], np.float32).T)
    wdT = np.ascontiguousarray(np.asarray(inputs["L0_ffn_w_down"], np.float32).T)
    ident = np.eye(P, dtype=np.float32)
    ones1 = np.ones((1, P), np.float32)
    onescol = np.ones((P, 1), np.float32)

    in_maps = []
    for core in range(NCORE):
        xs = np.ascontiguousarray(xfull[core * TCORE:(core + 1) * TCORE])
        in_maps.append(dict(
            x=xs, xT=np.ascontiguousarray(xs.T), Wcat=Wcat, bcat=bcat,
            lnw=lnw, lnb=lnb, outWb=outWb,
            wgs=np.ascontiguousarray(wgT[core * P:(core + 1) * P]),
            wus=np.ascontiguousarray(wuT[core * P:(core + 1) * P]),
            wds=np.ascontiguousarray(wdT[core * 4 * P:(core + 1) * 4 * P]),
            ident=ident, ones1=ones1, onescol=onescol,
            cid=np.full((1, 1), float(core), np.float32)))
    return in_maps


def kernel(**inputs):
    nc, taps, _ = _get_built()
    in_maps = prep_in_maps(inputs)
    res = run_bass_kernel_spmd(nc, in_maps, core_ids=list(range(NCORE)))
    out = np.concatenate([r["out"] for r in res.results], axis=0)
    return out[None, :, :]


# version 11
# speedup vs baseline: 1.0576x; 1.0131x over previous
import contextlib
import numpy as np
import ml_dtypes
import concourse.bass as bass
import concourse.mybir as mybir
from concourse.bass_isa_rust import ReduceOp
from concourse.bass_utils import run_bass_kernel_spmd

F32 = mybir.dt.float32
BF16 = mybir.dt.bfloat16
AF = mybir.ActivationFunctionType
ALU = mybir.AluOpType
AXX = mybir.AxisListType.X

P = 128
NCORE = 8
TCORE = 2048
NT = 16
D = 1024
KD = 8
H, N8 = 4, 8
HN = 32
U = 352
OFF_SIGN, OFF_PERM, OFF_GATE, OFF_IN = 0, 32, 288, 320
LCH = 16
OF = 4096
NOC = 32
FT = 512
NFT = 4
LN_EPS = 1e-5
BIG = 1e30
MAGIC = 12582912.0
SEGW = 2 * P * OF + 4 * P * D
ENGS = ("sync", "vector", "scalar", "tensor", "gpsimd")


class Sched:

    LOCKSTEP = False

    def __init__(self, nc, ctx, sem_names):
        self.nc = nc
        self.ops = {e: [] for e in ENGS}
        self.sem = {}
        self.cnt = {}
        for s in sem_names:
            self.sem[s] = ctx.enter_context(nc.semaphore(s))
            self.cnt[s] = 0
        self.nbar = 0

    def op(self, eng, fn, wait=(), inc=(), tag=""):
        if self.LOCKSTEP:
            sel = self.LOCKSTEP if isinstance(self.LOCKSTEP, (tuple, list, set)) \
                else list(self.cnt)
            sel2 = []
            for n in sel:
                if n.startswith("vrange:"):
                    _, a_, b_ = n.split(":")
                    if eng == "vector" and int(a_) <= len(self.ops["vector"]) < int(b_):
                        sel2.append("ve")
                elif "@" in n:
                    nm, e_ = n.split("@")
                    if e_ == eng:
                        sel2.append(nm)
                else:
                    sel2.append(n)
            wait = list(wait) + [(n, self.cnt[n]) for n in sel2]
        waits = []
        for w in wait:
            if isinstance(w, tuple):
                waits.append((self.sem[w[0]], w[1]))
            else:
                waits.append((self.sem[w], self.cnt[w]))
        incs = []
        for i in inc:
            if isinstance(i, tuple):
                name, n = i
            else:
                name, n = i, 1
            self.cnt[name] += n
            incs.append((self.sem[name], n))
        self.ops[eng].append((fn, waits, incs, tag))

    def val(self, name):
        return self.cnt[name]

    def barrier(self):
        self.nbar += 1
        tgt = 5 * self.nbar
        dma_sems = ("ld", "st", "wq", "ls", "lm", "lq", "lx0", "lx1", "lxT",
                    "lay", "lg", "lw0", "lw1", "lwd0", "lwd1", "ly0", "qa",
                    "dq")
        for e in ENGS:
            wl = [(n, self.cnt[n]) for n in dma_sems] if e == "sync" else []
            self.op(e, lambda eng: eng.nop(), wait=wl, inc=[("ph", 1)])
        for e in ENGS:
            self.op(e, lambda eng: eng.nop(), wait=[("ph", tgt)])

    def emit(self, block):
        sched = self

        def mk(engname):
            serial_sem = {"vector": "ve", "gpsimd": "ge"}.get(engname)

            def emitter(eng):
                prior = 0
                for fn, waits, incs, tag in sched.ops[engname]:
                    if serial_sem and prior:
                        eng.wait_ge(sched.sem[serial_sem], prior)
                    for h, v in waits:
                        eng.wait_ge(h, v)
                    instr = fn(eng)
                    if incs:
                        assert instr is not None, f"op {tag} returned no instr"
                        instr.then_inc(*incs[0])
                        for h, n in incs[1:]:
                            eng.nop().then_inc(h, n)
                    if serial_sem:
                        for hn, n in incs:
                            if hn is sched.sem[serial_sem]:
                                prior += n
            return emitter

        block.sync(mk("sync"))
        block.vector(mk("vector"))
        block.scalar(mk("scalar"))
        block.tensor(mk("tensor"))
        block.gpsimd(mk("gpsimd"))


class Ctx:
    pass


def build_core(debug_taps=()):
    nc = bass.Bass("TRN2", target_bir_lowering=False)
    c = Ctx()
    c.nc = nc
    c.taps = {}
    c.debug_taps = debug_taps

    dd = {}
    for nm, shape in [("x", [TCORE, D]), ("xT", [D, TCORE]), ("Wcat", [D, U]),
                      ("bcat", [1, U]), ("lnw", [P, KD]), ("lnb", [P, KD]),
                      ("outWb", [33, D]), ("wgs", [P, OF]), ("wus", [P, OF]),
                      ("wds", [4 * P, D]), ("ident", [P, P]), ("ones1", [1, P]),
                      ("onescol", [P, 1]), ("cid", [1, 1])]:
        dd[nm] = nc.dram_tensor(nm, shape, F32, kind="ExternalInput").ap()
    dd["out"] = nc.dram_tensor("out", [TCORE, D], F32, kind="ExternalOutput").ap()
    dd["ybuf"] = nc.dram_tensor("ybuf", [TCORE, D], F32, kind="Internal").ap()
    dd["cc_in"] = nc.dram_tensor("cc_in", [1, 96], F32, kind="Internal",
                                 addr_space="Local").ap()
    dd["cc_out"] = nc.dram_tensor("cc_out", [NCORE, 96], F32, kind="Internal",
                                  addr_space="Shared").ap()
    dd["ccm_in"] = nc.dram_tensor("ccm_in", [1, 4], F32, kind="Internal",
                                  addr_space="Local").ap()
    dd["ccm_out"] = nc.dram_tensor("ccm_out", [NCORE, 4], F32, kind="Internal",
                                   addr_space="Shared").ap()
    dd["ccw_in"] = nc.dram_tensor("ccw_in", [1, SEGW], BF16, kind="Internal",
                                  addr_space="Local").ap()
    dd["ccw_out"] = nc.dram_tensor("ccw_out", [NCORE, SEGW], BF16,
                                   kind="Internal", addr_space="Shared").ap()
    c.d = dd

    def tapfn(name, shape, dtype=F32):
        if name in debug_taps:
            c.taps[name] = nc.dram_tensor("tap_" + name, shape, dtype,
                                          kind="ExternalOutput").ap()
        return c.taps.get(name)

    c.tap = tapfn

    T = {}
    c.T = T

    def al(name, shape, dtype=F32):
        T[name] = nc.alloc_sbuf_tensor("sb_" + name, shape, dtype)

    def alias(name, host, off, shape, dtype=F32):
        base = nc.lookup_mloc(T[host]).addr
        T[name] = nc.alloc_sbuf_tensor_at("sb_" + name, shape, dtype,
                                          offset=base + off)

    al("ident", [P, P], BF16)
    al("ones1", [1, P]); al("onescol", [P, 1], BF16); al("cid", [1, 1])
    al("iota8", [P, 8])
    al("lnw", [P, KD]); al("lnb", [P, KD])
    al("lnbh", [P, KD], BF16); al("lnbl", [P, KD], BF16)
    al("mu", [P, NT]); al("rstd", [P, NT]); al("nrstdmu", [P, NT])
    al("sa", [P, NT]); al("dsc", [P, NT])
    al("wsc", [P, 4]); al("whalf", [P, 4])
    al("stat1", [P, NT]); al("stat2", [P, NT]); al("st_s", [P, NT])
    al("s1", [P, 1]); al("s2", [P, 1]); al("s3", [P, 1])
    al("gm1", [P, NT * H]); al("gci", [P, NT * H])
    al("chn", [P, 1]); al("chp", [P, 1])
    al("identf", [P, P])
    al("onescf", [P, 1])
    al("wrow", [1, 4]); al("wrow8", [1, 32])
    al("mt", [P, 4]); al("rut", [P, 4]); al("sit", [P, 4])
    al("rsa", [P, NT]); al("rgq", [P, NT]); al("ruq", [P, NT])
    al("sitq", [P, NT])
    al("magicb", [P, 1]); al("nmagicb", [P, 1])
    al("sc_po", [P, 256], BF16); al("sc_sum", [P, 96]); al("onesL", [P, LCH * HN])
    al("l2_po", [8, 256], BF16)
    al("gsum", [1, NCORE * 96]); al("gcur", [1, 96]); al("gnx", [1, 96])
    al("goh", [1, H * 64]); al("gt0", [1, H * 64]); al("gmsk", [1, 1])
    al("h_in", [P, HN]); al("e_full", [P, HN]); al("e_ch", [P, HN])
    al("xstage", [P, D]); al("xstage2", [P, D])
    al("xTstage", [P, TCORE])
    al("wdst", [P, D], BF16); al("wdst2", [P, D], BF16)
    al("outWb_hi", [P, D], BF16); al("outWb_lo", [P, D], BF16)

    al("Wcat", [P, KD * U])
    al("Wp_hi", [P, KD * U], BF16)
    al("Wp_lo", [P, KD * U], BF16)
    al("K1R", [P, U]); al("B2R", [P, U])
    al("bcat", [1, U]); al("k1row", [1, U]); al("b2row", [1, U])
    al("wfst", [P, 2048])
    al("wist", [P, 2048], BF16)
    alias("x_intT", "Wcat", 0, [P, KD * TCORE], BF16)
    assert (nc.lookup_mloc(T["wfst"]).addr + 8192
            >= nc.lookup_mloc(T["Wcat"]).addr + KD * TCORE * 2), "x_intT span"

    al("Ubuf", [P, NT * U])
    alias("sc_oh", "Ubuf", 0, [P, LCH * H * 64], BF16)
    alias("sc_hh", "Ubuf", 8192, [P, LCH * HN])
    alias("ys_hi", "Ubuf", 12288, [P, LCH * HN], BF16)
    alias("ys_lo", "Ubuf", 13312, [P, LCH * HN], BF16)
    alias("sc_ys", "Ubuf", 10240, [P, LCH * HN])
    alias("ysT_hi", "Ubuf", 0, [P, TCORE], BF16)
    alias("ysT_lo", "Ubuf", 4096, [P, TCORE], BF16)
    alias("gsbA", "Ubuf", 0, [P, FT])
    alias("gsbB", "Ubuf", 2048, [P, FT])
    alias("Mq", "Ubuf", 4096, [P, 32])
    alias("ystg0", "Ubuf", 6144, [P, D])
    alias("ystg1", "Ubuf", 10240, [P, D])
    alias("ystg2", "Ubuf", 14336, [P, D])
    alias("ystg3", "Ubuf", 18432, [P, D])

    al("xT_hi", [P, KD * TCORE], BF16)
    alias("gtmp", "xT_hi", 0, [P, NT * 256])
    alias("sc_t0", "xT_hi", 16384, [P, LCH * 256])
    alias("l2_t0", "xT_hi", 16384, [8, LCH * 256])
    alias("zB", "xT_hi", 0, [P, 16 * FT])
    alias("iqB0", "xT_hi", 0, [P, 2 * OF], BF16)
    alias("iqB1", "xT_hi", 16384, [P, 2 * OF], BF16)

    al("xT_lo", [P, KD * TCORE], BF16)
    for i_, nm_ in enumerate(["l2_sig", "l2_rho", "l2_s", "l2_sg", "l2_dh",
                              "l2_ch", "l2_pi", "l2_sh", "l2_v"]):
        alias(nm_, "xT_lo", i_ * 2048, [8, LCH * HN])
    alias("l2_oh", "xT_lo", 9 * 2048, [8, LCH * H * 64], BF16)
    alias("zA", "xT_lo", 0, [P, 16 * FT])
    alias("iqA0", "xT_lo", 0, [P, 2 * OF], BF16)
    alias("iqA1", "xT_lo", 16384, [P, 2 * OF], BF16)

    al("S", [P, NT * 256])
    for i_, nm_ in enumerate(["sc_d", "sc_c", "sc_p", "sc_sig", "sc_dh",
                              "sc_ch", "sc_pi", "sc_sh"]):
        alias(nm_, "S", i_ * 2048, [P, LCH * HN])
    alias("wstage", "S", 0, [P, KD * FT], BF16)
    alias("wstage2", "S", 8192, [P, KD * FT], BF16)
    alias("xTstage2", "S", 0, [P, TCORE])
    alias("wstageB", "xTstage", 0, [P, KD * FT], BF16)
    assert (nc.lookup_mloc(T["xstage2"]).addr
            == nc.lookup_mloc(T["xstage"]).addr + D * 4), "xstage adjacency"
    alias("wstage2B", "xstage", 0, [P, KD * FT], BF16)

    al("sgn", [P, NT * HN])
    for nm_ in ["gam", "dvec", "cvec", "rsum", "rrec", "Rmax", "AVm", "PIDX",
                "grk", "gsel", "gcs", "gsr", "gsm"]:
        al(nm_, [P, NT * HN])
    assert nc.lookup_mloc(T["gsm"]).addr - nc.lookup_mloc(T["sgn"]).addr \
        == 13 * 2048, "RE contiguity"
    alias("xq_t", "sgn", 0, [P, D], BF16)
    alias("qv_t", "sgn", 2048, [P, D])
    alias("yt1", "sgn", 6144, [P, D])
    alias("zt1", "sgn", 10240, [P, D])
    alias("yout", "sgn", 14336, [P, D])
    alias("qf_t", "sgn", 9 * 2048, [P, D])

    c.sbuf_left = nc.sbuf_bytes_remaining
    c.PSB = [nc.alloc_psum_tensor(f"ps{b}", [P, 512], F32) for b in range(8)]

    with contextlib.ExitStack() as ctx:
        sch = Sched(nc, ctx, ["ld", "wq", "mm", "ve", "se", "ge", "st", "cc",
                              "dq", "qa", "ph", "ev", "ls", "lm", "lq", "lx0",
                              "lx1", "lxT", "lay", "lg", "lw0", "lw1", "lwd0",
                              "lwd1", "ly0"])
        c.s = sch
        block = ctx.enter_context(nc.Block())
        plan(c)
        sch.emit(block)

    return nc, c.taps, c.sbuf_left



def v_op(c, fn, wait=(), tag=""):
    c.s.op("vector", fn, wait=wait, inc=[("ve", 1)], tag=tag)


def s_op(c, fn, wait=(), tag=""):
    c.s.op("scalar", fn, wait=wait, inc=[("se", 1)], tag=tag)


def g_op(c, fn, wait=(), tag=""):
    c.s.op("gpsimd", fn, wait=wait, inc=[("ge", 1)], tag=tag)


def t_op(c, fn, wait=(), tag=""):
    c.s.op("tensor", fn, wait=wait, inc=[("mm", 1)], tag=tag)


def dma(c, fn, wait=(), sem="ld", tag=""):
    c.s.op("sync", fn, wait=wait, inc=[(sem, 16)], tag=tag)


def gdma(c, fn, wait=(), sem="ld", tag=""):
    c.s.op("gpsimd", fn, wait=wait, inc=[(sem, 16)], tag=tag)



def plan(c):
    plan_setup(c)
    plan_attn_inputs(c)
    plan_wpartials(c)
    c.proj_hooks = mk_quant_hooks(c)
    plan_attn_front(c)
    c.s.barrier()
    plan_sinkhorn(c)
    plan_greedy(c)
    c.s.barrier()
    plan_scan_local(c)
    plan_scan_global(c)
    c.s.barrier()
    plan_out_proj(c)
    plan_ffn(c)
    plan_finish(c)


def plan_setup(c):
    T, d = c.T, c.d
    for nm in ["ident", "ones1", "onescol", "cid", "lnw", "lnb", "bcat"]:
        def ld(e, nm=nm):
            if nm in ("ident", "onescol"):
                return e.dma_start(out=c.T["wfst"][:, 0:P] if nm == "ident"
                                   else c.T["wfst"][:, P:P + 1],
                                   in_=c.d[nm][:, :])
            return e.dma_start(out=c.T[nm][:, :], in_=c.d[nm][:, :])
        dma(c, ld, sem="ls", tag=f"ld_{nm}")
    v_op(c, lambda e: e.tensor_copy(T["ident"][:, :], T["wfst"][:, 0:P]),
         wait=[("ls", c.s.val("ls"))])
    v_op(c, lambda e: e.tensor_copy(T["onescol"][:, :], T["wfst"][:, P:P + 1]))
    g_op(c, lambda e: e.iota(T["iota8"][:, :], pattern=[[1, 8]], base=0,
                             channel_multiplier=0,
                             allow_small_or_imprecise_dtypes=True))
    v_op(c, lambda e: e.memset(T["onesL"][:, :], 1.0))
    v_op(c, lambda e: e.memset(T["h_in"][:, :], 0.0))
    v_op(c, lambda e: e.tensor_copy(T["identf"][:, :], T["ident"][:, :]))
    v_op(c, lambda e: e.memset(T["onescf"][:, :], 1.0))
    v_op(c, lambda e: e.memset(T["chn"][:, :], -0.5))
    v_op(c, lambda e: e.memset(T["chp"][:, :], 0.5))
    v_op(c, lambda e: e.memset(T["magicb"][:, :], MAGIC))
    v_op(c, lambda e: e.memset(T["nmagicb"][:, :], -MAGIC))
    v_op(c, lambda e: e.memset(T["PIDX"][:, :], 0.0))
    v_op(c, lambda e: e.memset(T["AVm"][:, :], 0.0))
    v_op(c, lambda e: e.tensor_copy(T["lnbh"][:, :], T["lnb"][:, :]))
    v_op(c, lambda e: e.tensor_tensor(T["lnbl"][:, :], T["lnb"][:, :],
                                      T["lnbh"][:, :], ALU.subtract))
    dma(c, lambda e: e.dma_start(out=T["xstage"][0:33, :], in_=d["outWb"][:, :]),
        sem="ls")
    v_op(c, lambda e: e.tensor_copy(T["outWb_hi"][0:33, :], T["xstage"][0:33, :]),
         wait=[("ls", c.s.val("ls"))])
    v_op(c, lambda e: e.tensor_tensor(T["outWb_lo"][0:33, :], T["xstage"][0:33, :],
                                      T["outWb_hi"][0:33, :], ALU.subtract))
    dma(c, lambda e: e.dma_start(
        out=T["Wcat"][:, :].rearrange("p (c u) -> p c u", c=KD),
        in_=bass.AP(d["Wcat"].tensor, 0, [[U, P], [P * U, KD], [1, U]])),
        sem="ls")
    v_op(c, lambda e: e.tensor_copy(T["Wp_hi"][:, :], T["Wcat"][:, :]),
         wait=[("ls", c.s.val("ls"))])
    v_op(c, lambda e: e.tensor_tensor(T["Wp_lo"][:, :], T["Wcat"][:, :],
                                      T["Wp_hi"][:, :], ALU.subtract))
    c.ve_setup = c.s.val("ve")


def _wunits(c):
    d = c.d
    units = []
    for sb in range(2):
        units.append((lambda sb=sb: d["wgs"][:, sb * 2048:(sb + 1) * 2048],
                      2048, 0, sb * 2048, [[OF, P], [1, 2048]]))
    for sb in range(2):
        units.append((lambda sb=sb: d["wus"][:, sb * 2048:(sb + 1) * 2048],
                      2048, 1, P * OF + sb * 2048, [[OF, P], [1, 2048]]))
    for j in range(4):
        units.append((lambda j=j: d["wds"][j * P:(j + 1) * P, :],
                      1024, 2, 2 * P * OF + j * P * D, [[D, P], [1, D]]))
    return units


def plan_wpartials(c):
    T, d, s = c.T, c.d, c.s
    v_op(c, lambda e: e.memset(T["wsc"][:, :], 0.0))
    prev_ve = c.ve_setup
    for (src, width, col, _off, _dims) in _wunits(c):
        gdma(c, lambda e, src=src, width=width: e.dma_start(
            out=T["wfst"][:, 0:width], in_=src()),
            sem="lm", wait=[("ve", prev_ve)], tag="wpart_ld")
        v_op(c, lambda e, width=width: e.tensor_reduce(
            T["s1"][:, :], T["wfst"][:, 0:width], AXX, ALU.add,
            apply_absolute_value=True), wait=[("lm", s.val("lm"))])
        prev_ve = s.val("ve")
        v_op(c, lambda e, col=col: e.tensor_tensor(
            T["wsc"][:, col:col + 1], T["wsc"][:, col:col + 1],
            T["s1"][:, :], ALU.add))
    t_op(c, lambda e: e.matmul(c.PSB[7][0:1, 0:4], T["onescf"][:, :],
                               T["wsc"][:, 0:4], start=True, stop=True),
         wait=[("ve", s.val("ve"))], tag="meansum")
    v_op(c, lambda e: e.tensor_copy(T["wrow"][:, :], c.PSB[7][0:1, 0:4]),
         wait=[("mm", s.val("mm"))])
    gdma(c, lambda e: e.dma_start(out=d["ccm_in"][:, :], in_=T["wrow"][:, :]),
         sem="lm", wait=[("ve", s.val("ve"))], tag="means_st")
    c.s.op("gpsimd", lambda e: e.collective_compute(
        "AllGather", ALU.bypass, replica_groups=[list(range(NCORE))],
        ins=[d["ccm_in"][:, :]], outs=[d["ccm_out"][:, :]]),
        wait=[("lm", s.val("lm"))], inc=[("cc", 1)], tag="AGmeans")
    c.cc_m = s.val("cc")
    c.qprev_ve = s.val("ve")


def mk_quant_hooks(c):
    T, d, s = c.T, c.d, c.s
    units = _wunits(c)

    def combine():
        gdma(c, lambda e: e.dma_start(
            out=T["wrow8"][0:1, :],
            in_=bass.AP(d["ccm_out"].tensor, 0, [[1, 4], [4, NCORE]]),
            allow_slow_non_contiguous=True),
            sem="lm", wait=[("cc", c.cc_m)], tag="mcomb_ld")
        v_op(c, lambda e: e.tensor_reduce(
            T["wrow"][0:1, :],
            T["wrow8"][0:1, :].rearrange("p (c r) -> p c r", c=4),
            AXX, ALU.add), wait=[("lm", s.val("lm"))])
        t_op(c, lambda e: e.matmul(c.PSB[7][:, 0:4], T["ones1"][:, :],
                                   T["wrow"][:, :], start=True, stop=True),
             wait=[("ve", s.val("ve"))], tag="meanrep")
        v_op(c, lambda e: e.tensor_copy(T["wsc"][:, 0:4], c.PSB[7][:, 0:4]),
             wait=[("mm", s.val("mm"))])
        v_op(c, lambda e: e.tensor_scalar(T["wsc"][:, 0:4], T["wsc"][:, 0:4],
                                          1.0 / (D * OF), 1e-5, ALU.mult,
                                          ALU.max))
        v_op(c, lambda e: e.tensor_scalar(T["whalf"][:, 0:4], T["wsc"][:, 0:4],
                                          0.5, None, ALU.mult))
        v_op(c, lambda e: e.reciprocal(T["wsc"][:, 0:4], T["wsc"][:, 0:4]))
        c.ve_wsc = s.val("ve")

    def mk_qunit(u):
        def qunit():
            src, width, col, off, dims = units[u]
            gdma(c, lambda e: e.dma_start(out=T["wfst"][:, 0:width],
                                          in_=src()),
                 sem="lq", wait=[("ve", c.qprev_ve)], tag=f"q_ld{u}")
            lqv = s.val("lq")
            s_op(c, lambda e: e.activation(
                T["wist"][:, 0:width], T["wfst"][:, 0:width], AF.Sign,
                bias=T["chn"][:, :], scale=T["wsc"][:, col:col + 1]),
                wait=[("lq", lqv), ("ve", c.ve_wsc)])
            s_op(c, lambda e: e.activation(
                T["wfst"][:, 0:width], T["wfst"][:, 0:width], AF.Sign,
                bias=T["chp"][:, :], scale=T["wsc"][:, col:col + 1]))
            c.s.op("vector", lambda e: e.tensor_tensor(
                T["wist"][:, 0:width], T["wist"][:, 0:width],
                T["wfst"][:, 0:width], ALU.add),
                wait=[("se", s.val("se"))], inc=[("ve", 1)])
            c.qprev_ve = s.val("ve")
            dma(c, lambda e: e.dma_start(
                out=bass.AP(d["ccw_in"].tensor, off, dims),
                in_=T["wist"][:, 0:width]),
                sem="wq", wait=[("ve", s.val("ve"))], tag=f"q_st{u}")
        return qunit

    def ag2():
        c.s.op("gpsimd", lambda e: e.collective_compute(
            "AllGather", ALU.bypass, replica_groups=[list(range(NCORE))],
            ins=[d["ccw_in"][:, :]], outs=[d["ccw_out"][:, :]]),
            wait=[("wq", s.val("wq"))], inc=[("cc", 1)], tag="AGw")
        c.cc_w = s.val("cc")

    hooks = {0: [combine]}
    for u in range(8):
        hooks.setdefault(1 + u, []).append(mk_qunit(u))
    hooks.setdefault(9, []).append(ag2)
    return hooks



def plan_attn_inputs(c):
    T, d, s = c.T, c.d, c.s
    xt_use = {0: c.s.val("ve"), 1: c.s.val("ve")}
    for k in range(KD):
        buf = "xTstage" if k % 2 == 0 else "xTstage2"
        dma(c, lambda e, k=k, buf=buf: e.dma_start(
            out=T[buf][:, :], in_=d["xT"][k * P:(k + 1) * P, :]),
            sem="lxT", wait=[("ve", xt_use[k % 2])], tag=f"xT{k}")
        s_op(c, lambda e, k=k, buf=buf: e.activation(
            T["xT_hi"][:, k * TCORE:(k + 1) * TCORE], T[buf][:, :],
            AF.Identity), wait=[("lxT", s.val("lxT"))])
        v_op(c, lambda e, k=k, buf=buf: e.tensor_tensor(
            T["xT_lo"][:, k * TCORE:(k + 1) * TCORE], T[buf][:, :],
            T["xT_hi"][:, k * TCORE:(k + 1) * TCORE], ALU.subtract),
            wait=[("se", s.val("se"))])
        xt_use[k % 2] = s.val("ve")
    last_use = {0: 0, 1: 0}
    for t in range(NT):
        buf = "xstage" if t % 2 == 0 else "xstage2"
        lsem = "lx0" if t % 2 == 0 else "lx1"
        dma(c, lambda e, t=t, buf=buf: e.dma_start(
            out=T[buf][:, :], in_=d["x"][t * P:(t + 1) * P, :]),
            sem=lsem, wait=[("ve", last_use[t % 2])], tag=f"x{t}")
        ldv = s.val(lsem)
        v_op(c, lambda e, t=t, buf=buf: e.tensor_reduce(
            T["stat1"][:, t:t + 1], T[buf][:, :], AXX, ALU.add),
            wait=[(lsem, ldv)])
        s_op(c, lambda e, t=t, buf=buf: e.activation(
            T["qf_t"][:, 0:D], T[buf][:, :], AF.Square),
            wait=[(lsem, ldv), ("ve", s.val("ve"))])
        v_op(c, lambda e, t=t: e.tensor_reduce(
            T["stat2"][:, t:t + 1], T["qf_t"][:, 0:D], AXX, ALU.add),
            wait=[("se", s.val("se"))])
        last_use[t % 2] = s.val("ve")
    c.ve_xsplit = s.val("ve")
    c.se_stats = s.val("se")
    c.ve_stats = s.val("ve")


def _rsqrt_inplace(c, ap_v, ap_r, ap_tmp):
    T, s = c.T, c.s
    s_op(c, lambda e: e.activation(ap_r, ap_v, AF.Sqrt), wait=[("ve", s.val("ve"))])
    v_op(c, lambda e: e.reciprocal(ap_r, ap_r), wait=[("se", s.val("se"))])
    for _ in range(2):
        v_op(c, lambda e: e.tensor_tensor(ap_tmp, ap_r, ap_r, ALU.mult))
        v_op(c, lambda e: e.tensor_tensor(ap_tmp, ap_tmp, ap_v, ALU.mult))
        v_op(c, lambda e: e.tensor_scalar(ap_tmp, ap_tmp, -0.5, 1.5,
                                          ALU.mult, ALU.add))
        v_op(c, lambda e: e.tensor_tensor(ap_r, ap_r, ap_tmp, ALU.mult))


def plan_attn_front(c):
    T, d, s, PSB = c.T, c.d, c.s, c.PSB

    def b2_mms(e):
        last = None
        for k in range(KD):
            for a_, b_ in (("lnbh", "Wp_hi"), ("lnbh", "Wp_lo"), ("lnbl", "Wp_hi")):
                last = e.matmul(PSB[7][0:1, 0:U], T[a_][:, k:k + 1],
                                T[b_][:, k * U:(k + 1) * U],
                                start=(k == 0 and a_ == "lnbh" and b_ == "Wp_hi"),
                                stop=(k == KD - 1 and a_ == "lnbl"))
        return last
    t_op(c, b2_mms, wait=[("ve", s.val("ve")), ("ls", s.val("ls"))], tag="b2")
    v_op(c, lambda e: e.tensor_tensor(T["b2row"][:, :], PSB[7][0:1, 0:U],
                                      T["bcat"][:, :], ALU.add),
         wait=[("mm", s.val("mm"))])

    v_op(c, lambda e: e.tensor_tensor(
        T["Wcat"][:, :].rearrange("p (c u) -> p c u", c=KD),
        T["Wcat"][:, :].rearrange("p (c u) -> p c u", c=KD),
        T["lnw"][:, :].unsqueeze(2).broadcast_to([P, KD, U]), ALU.mult))
    v_op(c, lambda e: e.tensor_copy(T["Wp_hi"][:, :], T["Wcat"][:, :]))
    v_op(c, lambda e: e.tensor_tensor(T["Wp_lo"][:, :], T["Wcat"][:, :],
                                      T["Wp_hi"][:, :], ALU.subtract))
    ve_wsplit = s.val("ve")

    def k1_mms(e):
        last = None
        for k in range(KD):
            for b_ in ("Wp_hi", "Wp_lo"):
                last = e.matmul(PSB[6][0:1, 0:U], T["onescol"][:, :],
                                T[b_][:, k * U:(k + 1) * U],
                                start=(k == 0 and b_ == "Wp_hi"),
                                stop=(k == KD - 1 and b_ == "Wp_lo"))
        return last
    t_op(c, k1_mms, wait=[("ve", s.val("ve"))], tag="k1")
    v_op(c, lambda e: e.tensor_copy(T["k1row"][:, :], PSB[6][0:1, 0:U]),
         wait=[("mm", s.val("mm"))])

    def rep_mms(e):
        e.matmul(PSB[5][:, 0:U], T["ones1"][:, :], T["k1row"][:, :],
                 start=True, stop=True)
        return e.matmul(PSB[4][:, 0:U], T["ones1"][:, :], T["b2row"][:, :],
                        start=True, stop=True)
    t_op(c, rep_mms, wait=[("ve", s.val("ve"))], tag="repk1b2")
    v_op(c, lambda e: e.tensor_copy(T["K1R"][:, :], PSB[5][:, 0:U]),
         wait=[("mm", s.val("mm"))])
    v_op(c, lambda e: e.tensor_copy(T["B2R"][:, :], PSB[4][:, 0:U]))

    v_op(c, lambda e: e.tensor_scalar(T["mu"][:, :], T["stat1"][:, :],
                                      1.0 / D, None, ALU.mult),
         wait=[("ve", c.ve_stats)])
    v_op(c, lambda e: e.tensor_tensor(T["stat1"][:, :], T["mu"][:, :],
                                      T["mu"][:, :], ALU.mult))
    v_op(c, lambda e: e.tensor_scalar(T["stat2"][:, :], T["stat2"][:, :],
                                      1.0 / D, LN_EPS, ALU.mult, ALU.add))
    v_op(c, lambda e: e.tensor_tensor(T["stat2"][:, :], T["stat2"][:, :],
                                      T["stat1"][:, :], ALU.subtract))
    _rsqrt_inplace(c, T["stat2"][:, :], T["rstd"][:, :], T["st_s"][:, :])
    v_op(c, lambda e: e.tensor_tensor(T["nrstdmu"][:, :], T["rstd"][:, :],
                                      T["mu"][:, :], ALU.mult))
    v_op(c, lambda e: e.tensor_scalar(T["nrstdmu"][:, :], T["nrstdmu"][:, :],
                                      -1.0, None, ALU.mult))
    ve_ready = s.val("ve")

    evac_se = {0: 0, 1: 0}
    for t in range(NT):
        for h in getattr(c, "proj_hooks", {}).get(t, []):
            h()
        bank = t % 2

        def proj_mms(e, t=t, bank=bank):
            last = None
            first = True
            for k in range(KD):
                xh = T["xT_hi"][:, k * TCORE + t * P:k * TCORE + (t + 1) * P]
                xl = T["xT_lo"][:, k * TCORE + t * P:k * TCORE + (t + 1) * P]
                wh = T["Wp_hi"][:, k * U:(k + 1) * U]
                wl = T["Wp_lo"][:, k * U:(k + 1) * U]
                last = e.matmul(PSB[bank][:, 0:U], xh, wh, start=first, stop=False)
                first = False
                last = e.matmul(PSB[bank][:, 0:U], xh, wl, start=False, stop=False)
                last = e.matmul(PSB[bank][:, 0:U], xl, wh, start=False,
                                stop=(k == KD - 1))
            return last
        t_op(c, proj_mms, wait=[("ve", ve_ready), ("se", evac_se[bank])],
             tag=f"proj{t}")
        mmv = s.val("mm")
        s_op(c, lambda e, t=t, bank=bank: e.activation(
            T["Ubuf"][:, t * U:(t + 1) * U], PSB[bank][:, 0:U], AF.Identity,
            scale=T["rstd"][:, t:t + 1]), wait=[("mm", mmv)])
        evac_se[bank] = s.val("se")
        c.s.op("vector", lambda e, t=t: e.scalar_tensor_tensor(
            T["Ubuf"][:, t * U:(t + 1) * U], T["K1R"][:, :],
            T["nrstdmu"][:, t:t + 1], T["Ubuf"][:, t * U:(t + 1) * U],
            ALU.mult, ALU.add), wait=[("se", s.val("se"))], inc=[("ve", 1)])
        v_op(c, lambda e, t=t: e.tensor_tensor(
            T["Ubuf"][:, t * U:(t + 1) * U], T["Ubuf"][:, t * U:(t + 1) * U],
            T["B2R"][:, :], ALU.add))

    c.wq_done = s.val("wq")
    if c.tap("Ubuf", [P, NT * U]):
        dma(c, lambda e: e.dma_start(out=c.taps["Ubuf"][:, :], in_=T["Ubuf"][:, :]),
            wait=[("ve", s.val("ve"))], sem="st")
    for nm_, sh_ in [("mu", [P, NT]), ("rstd", [P, NT]), ("K1R", [P, U]),
                     ("B2R", [P, U]), ("b2row", [1, U]), ("k1row", [1, U]),
                     ("Wcat", [P, KD * U]), ("stat2", [P, NT]),
                     ("bcat", [1, U])]:
        if c.tap(nm_, sh_):
            dma(c, lambda e, nm_=nm_: e.dma_start(out=c.taps[nm_][:, :],
                                                  in_=T[nm_][:, :]),
                wait=[("ve", s.val("ve")), ("se", s.val("se"))], sem="st")
    if c.tap("xh", [P, 2048], BF16):
        dma(c, lambda e: e.dma_start(
            out=c.taps["xh"][:, :],
            in_=T["xT_hi"][:, 0:2048]), wait=[("ve", s.val("ve"))], sem="st")

    uview = T["Ubuf"][:, :].rearrange("p (t u) -> p t u", t=NT)
    gate_in = uview[:, :, OFF_GATE:OFF_GATE + HN]
    sign_in = uview[:, :, OFF_SIGN:OFF_SIGN + HN]
    bin_in = uview[:, :, OFF_IN:OFF_IN + HN]
    perm_in = uview[:, :, OFF_PERM:OFF_PERM + 256]
    s_op(c, lambda e: e.activation(T["gam"][:, :], gate_in, AF.Sigmoid),
         wait=[("ve", s.val("ve"))])
    s_op(c, lambda e: e.activation(T["sgn"][:, :], sign_in, AF.Sigmoid))
    s_op(c, lambda e: e.activation(T["S"][:, :], perm_in, AF.Exp))
    v_op(c, lambda e: e.tensor_scalar(T["sgn"][:, :], T["sgn"][:, :], 2.0, -1.0,
                                      ALU.mult, ALU.add), wait=[("se", s.val("se"))])
    v_op(c, lambda e: e.tensor_tensor(T["dvec"][:, :], T["gam"][:, :],
                                      T["sgn"][:, :], ALU.mult))
    v_op(c, lambda e: e.tensor_tensor(T["cvec"][:, :], T["gam"][:, :], bin_in,
                                      ALU.mult))
    v_op(c, lambda e: e.tensor_tensor(T["cvec"][:, :], bin_in, T["cvec"][:, :],
                                      ALU.subtract))


def plan_sinkhorn(c):
    T, s = c.T, c.s
    NG = NT * H * 8
    Sv = T["S"][:, :].rearrange("p (g j) -> p g j", j=8)
    Sc = T["S"][:, :].rearrange("p (g i j) -> p g j i", i=8, j=8)
    rr_row = T["rrec"][:, :].rearrange("p (g o) -> p g o", o=1).broadcast_to([P, NG, 8])
    rr_col = (T["rrec"][:, :].rearrange("p (g j) -> p g j", j=8).unsqueeze(3)
              .broadcast_to([P, NT * H, 8, 8]))
    for it in range(5):
        v_op(c, lambda e: e.tensor_reduce(T["rsum"][:, :], Sv, AXX, ALU.add))
        v_op(c, lambda e: e.reciprocal(T["rrec"][:, :], T["rsum"][:, :]))
        v_op(c, lambda e: e.tensor_tensor(Sv, Sv, rr_row, ALU.mult))
        v_op(c, lambda e: e.tensor_reduce(T["rsum"][:, :], Sc, AXX, ALU.add))
        v_op(c, lambda e: e.reciprocal(T["rrec"][:, :], T["rsum"][:, :]))
        v_op(c, lambda e: e.tensor_tensor(Sc, Sc, rr_col, ALU.mult))
    if c.tap("S", [P, NT * 256]):
        dma(c, lambda e: e.dma_start(out=c.taps["S"][:, :], in_=T["S"][:, :]),
            wait=[("ve", s.val("ve"))], sem="st")


def plan_greedy(c):
    T, s = c.T, c.s
    NTH = NT * H
    v_op(c, lambda e: e.tensor_reduce(
        T["Rmax"][:, :], T["S"][:, :].rearrange("p (g j) -> p g j", j=8), AXX, ALU.max))
    Ri = T["Rmax"][:, :].rearrange("p (g i) -> p g i", i=8).unsqueeze(3).broadcast_to([P, NTH, 8, 8])
    Rp = T["Rmax"][:, :].rearrange("p (g i) -> p g i", i=8).unsqueeze(2).broadcast_to([P, NTH, 8, 8])
    gt_v = T["gtmp"][:, :].rearrange("p (g i j) -> p g i j", i=8, j=8)
    st_v = T["sc_t0"][:, :].rearrange("p (g i j) -> p g i j", i=8, j=8)[:, 0:NTH]
    ii = T["iota8"][:, :].unsqueeze(1).unsqueeze(3).broadcast_to([P, NTH, 8, 8])
    ip = T["iota8"][:, :].unsqueeze(1).unsqueeze(1).broadcast_to([P, NTH, 8, 8])
    v_op(c, lambda e: e.tensor_tensor(gt_v, Rp, Ri, ALU.is_gt))
    v_op(c, lambda e: e.tensor_reduce(T["grk"][:, :], gt_v, AXX, ALU.add))
    v_op(c, lambda e: e.tensor_tensor(gt_v, Rp, Ri, ALU.is_equal))
    v_op(c, lambda e: e.tensor_tensor(st_v, ii, ip, ALU.is_gt))
    v_op(c, lambda e: e.tensor_tensor(gt_v, gt_v, st_v, ALU.mult))
    v_op(c, lambda e: e.tensor_reduce(T["gsm"][:, :], gt_v, AXX, ALU.add))
    v_op(c, lambda e: e.tensor_tensor(T["grk"][:, :], T["grk"][:, :],
                                      T["gsm"][:, :], ALU.add))

    Sv2 = T["S"][:, :].rearrange("p (g i j) -> p g j i", i=8, j=8)
    gsel_bc = T["gsel"][:, :].rearrange("p (g i) -> p g i", i=8).unsqueeze(2).broadcast_to(
        [P, NTH, 8, 8])
    gtv2 = T["gtmp"][:, :].rearrange("p (g j i) -> p g j i", i=8, j=8)
    for r in range(8):
        v_op(c, lambda e, r=r: e.tensor_single_scalar(
            T["gsel"][:, :], T["grk"][:, :], float(r), ALU.is_equal))
        v_op(c, lambda e: e.tensor_tensor(gtv2, Sv2, gsel_bc, ALU.mult))
        v_op(c, lambda e: e.tensor_reduce(T["gsr"][:, :], gtv2, AXX, ALU.add))
        v_op(c, lambda e: e.tensor_tensor(T["gsm"][:, :], T["gsr"][:, :],
                                          T["AVm"][:, :], ALU.add))
        v_op(c, lambda e: e.tensor_reduce(
            T["gm1"][:, :], T["gsm"][:, :].rearrange("p (g j) -> p g j", j=8),
            AXX, ALU.max))
        v_op(c, lambda e: e.tensor_tensor(
            T["gcs"][:, :], T["gsm"][:, :],
            T["gm1"][:, :].unsqueeze(2).broadcast_to([P, NTH, 8]),
            ALU.is_ge))
        v_op(c, lambda e: e.scalar_tensor_tensor(
            T["gsm"][:, :], T["gcs"][:, :], -16.0,
            T["iota8"][:, :].unsqueeze(1).broadcast_to([P, NTH, 8]),
            ALU.mult, ALU.add))
        v_op(c, lambda e: e.tensor_reduce(
            T["gci"][:, :], T["gsm"][:, :].rearrange("p (g j) -> p g j", j=8),
            AXX, ALU.min))
        v_op(c, lambda e: e.tensor_scalar(T["gci"][:, :], T["gci"][:, :],
                                          1.0, 16.0, ALU.mult, ALU.add))
        v_op(c, lambda e: e.tensor_tensor(
            T["gcs"][:, :],
            T["iota8"][:, :].unsqueeze(1).broadcast_to([P, NTH, 8]),
            T["gci"][:, :].unsqueeze(2).broadcast_to([P, NTH, 8]),
            ALU.is_equal))
        v_op(c, lambda e: e.tensor_tensor(
            T["gsm"][:, :], T["gsel"][:, :],
            T["gci"][:, :].unsqueeze(2).broadcast_to([P, NTH, 8]),
            ALU.mult))
        v_op(c, lambda e: e.tensor_tensor(T["PIDX"][:, :], T["PIDX"][:, :],
                                          T["gsm"][:, :], ALU.add))
        v_op(c, lambda e: e.scalar_tensor_tensor(
            T["Rmax"][:, :], T["gsel"][:, :], -BIG, T["Rmax"][:, :],
            ALU.mult, ALU.add))
        v_op(c, lambda e: e.scalar_tensor_tensor(
            T["AVm"][:, :], T["gcs"][:, :], -BIG, T["AVm"][:, :],
            ALU.mult, ALU.add))
    if c.tap("PIDX", [P, NT * HN]):
        dma(c, lambda e: e.dma_start(out=c.taps["PIDX"][:, :], in_=T["PIDX"][:, :]),
            wait=[("ve", s.val("ve"))], sem="st")


def plan_scan_local(c):
    T, s, d = c.T, c.s, c.d
    for t in range(NT):
        for srcn, dstn in (("dvec", "sc_d"), ("cvec", "sc_c"), ("PIDX", "sc_p")):
            def mv(e, t=t, srcn=srcn, dstn=dstn):
                src = c.T[srcn][:, t * HN:(t + 1) * HN]
                dst = c.T[dstn][t * 8:(t + 1) * 8, :].rearrange(
                    "q (k f) -> q k f", f=HN)
                return e.dma_start(out=dst, in_=src)
            dma(c, mv, sem="lay", wait=[("ve", s.val("ve"))] if t == 0 else (),
                tag="lay")
    lay_ld = s.val("lay")

    pv = T["sc_p"][:, :].rearrange("p (k h i) -> p k h i", h=H, i=8)
    sv = T["sc_sig"][:, :].rearrange("p (k h i) -> p k h i", h=H, i=8)
    v_op(c, lambda e: e.tensor_copy(sv[:, 0], pv[:, 0]), wait=[("lay", lay_ld)])
    po = T["sc_po"][:, :].rearrange("p (h i m) -> p h i m", h=H, i=8, m=8)
    t0 = T["sc_t0"][:, :].rearrange("p (k h i m) -> p k h i m", k=LCH, h=H, i=8, m=8)
    for k in range(1, LCH):
        v_op(c, lambda e, k=k: e.tensor_tensor(
            po, pv[:, k].unsqueeze(3).broadcast_to([P, H, 8, 8]),
            T["iota8"][:, :].unsqueeze(1).unsqueeze(1).broadcast_to([P, H, 8, 8]),
            ALU.is_equal))
        v_op(c, lambda e, k=k: e.tensor_tensor(
            t0[:, 0], po,
            sv[:, k - 1].unsqueeze(2).broadcast_to([P, H, 8, 8]),
            ALU.mult))
        v_op(c, lambda e, k=k: e.tensor_reduce(sv[:, k], t0[:, 0], AXX, ALU.add))

    KH = LCH * H
    oh4 = T["sc_oh"][:, :].rearrange("p (q i m) -> p q i m", i=8, m=8)
    sv4 = T["sc_sig"][:, :].rearrange("p (q i) -> p q i", i=8)
    v_op(c, lambda e: e.tensor_tensor(
        oh4, sv4.unsqueeze(3).broadcast_to([P, KH, 8, 8]),
        T["iota8"][:, :].unsqueeze(1).unsqueeze(1).broadcast_to([P, KH, 8, 8]),
        ALU.is_equal))

    dv4 = T["sc_d"][:, :].rearrange("p (q i) -> p q i", i=8)
    cv4 = T["sc_c"][:, :].rearrange("p (q i) -> p q i", i=8)
    dh4 = T["sc_dh"][:, :].rearrange("p (q m) -> p q m", m=8)
    ch4 = T["sc_ch"][:, :].rearrange("p (q m) -> p q m", m=8)
    ohT4 = T["sc_oh"][:, :].rearrange("p (q i m) -> p q m i", i=8, m=8)
    t0m4 = T["sc_t0"][:, :].rearrange("p (q m i) -> p q m i", m=8, i=8)
    v_op(c, lambda e: e.tensor_tensor(
        t0m4, ohT4, dv4.unsqueeze(2).broadcast_to([P, KH, 8, 8]), ALU.mult))
    v_op(c, lambda e: e.tensor_reduce(dh4, t0m4, AXX, ALU.add))
    v_op(c, lambda e: e.tensor_tensor(
        t0m4, ohT4, cv4.unsqueeze(2).broadcast_to([P, KH, 8, 8]), ALU.mult))
    v_op(c, lambda e: e.tensor_reduce(ch4, t0m4, AXX, ALU.add))

    def hmslice(name, off):
        return (T[name][:, :].rearrange("p (k f) -> p k f", f=HN)
                [:, :, off:off + 1].squeeze(2))

    for h in range(H):
        for m in range(8):
            off = h * 8 + m
            v_op(c, lambda e, off=off: e.tensor_tensor_scan(
                hmslice("sc_pi", off), hmslice("sc_dh", off),
                hmslice("onesL", off), 1.0, ALU.mult, ALU.mult))
            v_op(c, lambda e, off=off: e.tensor_tensor_scan(
                hmslice("sc_sh", off), hmslice("sc_dh", off),
                hmslice("sc_ch", off), 0.0, ALU.mult, ALU.add))

    oh = T["sc_oh"][:, :].rearrange("p (k h i m) -> p k h i m", k=LCH, h=H,
                                    i=8, m=8)
    ohL = oh[:, LCH - 1]
    v_op(c, lambda e: e.tensor_copy(
        T["sc_sum"][:, 0:32].rearrange("p (h i) -> p h i", h=H), sv[:, LCH - 1]))
    piL = T["sc_pi"][:, :].rearrange("p (k h m) -> p k h m", h=H, m=8)[:, LCH - 1]
    shL = T["sc_sh"][:, :].rearrange("p (k h m) -> p k h m", h=H, m=8)[:, LCH - 1]
    t0g = T["sc_t0"][:, :].rearrange("p (k h i m) -> p k h i m", k=LCH, h=H, i=8, m=8)[:, 0]
    v_op(c, lambda e: e.tensor_tensor(
        t0g, ohL, piL.unsqueeze(2).broadcast_to([P, H, 8, 8]),
        ALU.mult))
    v_op(c, lambda e: e.tensor_reduce(
        T["sc_sum"][:, 32:64].rearrange("p (h i) -> p h i", h=H), t0g, AXX,
        ALU.add))
    v_op(c, lambda e: e.tensor_tensor(
        t0g, ohL, shL.unsqueeze(2).broadcast_to([P, H, 8, 8]),
        ALU.mult))
    v_op(c, lambda e: e.tensor_reduce(
        T["sc_sum"][:, 64:96].rearrange("p (h i) -> p h i", h=H), t0g, AXX,
        ALU.add))

    for srco, dstn in ((0, "l2_sg"), (32, "l2_rho"), (64, "l2_s")):
        def mv2(e, srco=srco, dstn=dstn):
            src = c.T["sc_sum"][:, srco:srco + 32]
            dst = c.T[dstn][:, :].rearrange("q (r f) -> q r f", f=HN)
            return e.dma_start(out=dst, in_=src)
        dma(c, mv2, sem="lay", wait=[("ve", s.val("ve"))], tag="lay2")
    lay2_ld = s.val("lay")

    pv2 = T["l2_sg"][:, :].rearrange("p (k h i) -> p k h i", h=H, i=8)
    sig2 = T["l2_sig"][:, :].rearrange("p (k h i) -> p k h i", h=H, i=8)
    po2 = T["l2_po"][:, :].rearrange("p (h i m) -> p h i m", h=H, i=8, m=8)
    t02 = T["l2_t0"][:, :].rearrange("p (k h i m) -> p k h i m", k=LCH, h=H, i=8, m=8)
    v_op(c, lambda e: e.tensor_copy(sig2[:, 0], pv2[:, 0]), wait=[("lay", lay2_ld)])
    for k in range(1, LCH):
        v_op(c, lambda e, k=k: e.tensor_tensor(
            po2, pv2[:, k].unsqueeze(3).broadcast_to([8, H, 8, 8]),
            T["iota8"][0:8, :].unsqueeze(1).unsqueeze(1).broadcast_to(
                [8, H, 8, 8]), ALU.is_equal))
        v_op(c, lambda e, k=k: e.tensor_tensor(
            t02[:, 0], po2,
            sig2[:, k - 1].unsqueeze(2).broadcast_to(
                [8, H, 8, 8]), ALU.mult))
        v_op(c, lambda e, k=k: e.tensor_reduce(sig2[:, k], t02[:, 0], AXX, ALU.add))
    KH = LCH * H
    oh24 = T["l2_oh"][:, :].rearrange("p (q i m) -> p q i m", i=8, m=8)
    sg24 = T["l2_sig"][:, :].rearrange("p (q i) -> p q i", i=8)
    v_op(c, lambda e: e.tensor_tensor(
        oh24, sg24.unsqueeze(3).broadcast_to([8, KH, 8, 8]),
        T["iota8"][0:8, :].unsqueeze(1).unsqueeze(1).broadcast_to([8, KH, 8, 8]),
        ALU.is_equal))
    dv24 = T["l2_rho"][:, :].rearrange("p (q i) -> p q i", i=8)
    cv24 = T["l2_s"][:, :].rearrange("p (q i) -> p q i", i=8)
    dh24 = T["l2_dh"][:, :].rearrange("p (q m) -> p q m", m=8)
    ch24 = T["l2_ch"][:, :].rearrange("p (q m) -> p q m", m=8)
    oh2T4 = T["l2_oh"][:, :].rearrange("p (q i m) -> p q m i", i=8, m=8)
    t02m4 = T["l2_t0"][:, :].rearrange("p (q m i) -> p q m i", m=8, i=8)
    v_op(c, lambda e: e.tensor_tensor(
        t02m4, oh2T4, dv24.unsqueeze(2).broadcast_to([8, KH, 8, 8]), ALU.mult))
    v_op(c, lambda e: e.tensor_reduce(dh24, t02m4, AXX, ALU.add))
    v_op(c, lambda e: e.tensor_tensor(
        t02m4, oh2T4, cv24.unsqueeze(2).broadcast_to([8, KH, 8, 8]), ALU.mult))
    v_op(c, lambda e: e.tensor_reduce(ch24, t02m4, AXX, ALU.add))
    def hmslice2(name, off):
        return (T[name][0:8, :].rearrange("p (k f) -> p k f", f=HN)
                [:, :, off:off + 1].squeeze(2))

    for h in range(H):
        for m in range(8):
            off = h * 8 + m
            v_op(c, lambda e, off=off: e.tensor_tensor_scan(
                hmslice2("l2_pi", off), hmslice2("l2_dh", off),
                hmslice2("onesL", off), 1.0, ALU.mult, ALU.mult))
            v_op(c, lambda e, off=off: e.tensor_tensor_scan(
                hmslice2("l2_sh", off), hmslice2("l2_dh", off),
                hmslice2("l2_ch", off), 0.0, ALU.mult, ALU.add))

    oh2L = (T["l2_oh"][:, :].rearrange("p (k h i m) -> p k h i m", k=LCH, h=H,
                                       i=8, m=8)[:, LCH - 1])
    pi2L = T["l2_pi"][:, :].rearrange("p (k h m) -> p k h m", h=H, m=8)[:, LCH - 1]
    sh2L = T["l2_sh"][:, :].rearrange("p (k h m) -> p k h m", h=H, m=8)[:, LCH - 1]
    t02g = (T["l2_t0"][:, :].rearrange("p (k h i m) -> p k h i m", k=LCH, h=H,
                                       i=8, m=8)[:, 0])
    v_op(c, lambda e: e.tensor_tensor(
        t02g, oh2L, pi2L.unsqueeze(2).broadcast_to([8, H, 8, 8]),
        ALU.mult))
    v_op(c, lambda e: e.tensor_reduce(
        T["l2_t0"][:, 1024:1056].rearrange("p (h i) -> p h i", h=H),
        t02g, AXX, ALU.add))
    v_op(c, lambda e: e.tensor_tensor(
        t02g, oh2L, sh2L.unsqueeze(2).broadcast_to([8, H, 8, 8]),
        ALU.mult))
    v_op(c, lambda e: e.tensor_reduce(
        T["l2_t0"][:, 1056:1088].rearrange("p (h i) -> p h i", h=H),
        t02g, AXX, ALU.add))
    ve_sum = s.val("ve")
    gdma(c, lambda e: e.dma_start(out=d["cc_in"][0:1, 0:32],
                                  in_=sig2[7:8, LCH - 1].rearrange(
                                      "p h i -> p (h i)")),
         wait=[("ve", ve_sum)], sem="st")
    gdma(c, lambda e: e.dma_start(out=d["cc_in"][0:1, 32:64],
                                  in_=T["l2_t0"][7:8, 1024:1056]), sem="st")
    gdma(c, lambda e: e.dma_start(out=d["cc_in"][0:1, 64:96],
                                  in_=T["l2_t0"][7:8, 1056:1088]), sem="st")


def plan_scan_global(c):
    T, s, d, PSB = c.T, c.s, c.d, c.PSB
    c.s.op("gpsimd", lambda e: e.collective_compute(
        "AllGather", ALU.bypass, replica_groups=[list(range(NCORE))],
        ins=[d["cc_in"][:, :]], outs=[d["cc_out"][:, :]]),
        wait=[("st", s.val("st"))], inc=[("cc", 1)], tag="AG")
    gdma(c, lambda e: e.dma_start(out=T["gsum"][0:1, :], in_=d["cc_out"][:, :]),
         wait=[("cc", s.val("cc"))], sem="lg")
    ag_ld = s.val("lg")

    v_op(c, lambda e: e.tensor_copy(
        T["gcur"][0:1, 0:32].rearrange("p (h i) -> p h i", h=H),
        T["iota8"][0:1, :].unsqueeze(1).broadcast_to([1, H, 8])))
    v_op(c, lambda e: e.memset(T["gcur"][0:1, 32:64], 1.0))
    v_op(c, lambda e: e.memset(T["gcur"][0:1, 64:96], 0.0))
    c.s.op("vector", lambda e: e.nop(), wait=[("lg", ag_ld)],
           inc=[("ve", 1)])
    gc_s = T["gcur"][0:1, 0:32].rearrange("p (h i) -> p h i", h=H)
    gc_r = T["gcur"][0:1, 32:64].rearrange("p (h i) -> p h i", h=H)
    gc_o = T["gcur"][0:1, 64:96].rearrange("p (h i) -> p h i", h=H)
    ohg = T["goh"][:, :].rearrange("p (h i m) -> p h i m", h=H, i=8, m=8)
    t0g = T["gt0"][:, :].rearrange("p (h i m) -> p h i m", h=H, i=8, m=8)
    for r in range(NCORE - 1):
        sr_s = (T["gsum"][0:1, r * 96:r * 96 + 32]
                .rearrange("p (h i) -> p h i", h=H))
        sr_r = (T["gsum"][0:1, r * 96 + 32:r * 96 + 64]
                .rearrange("p (h i) -> p h i", h=H))
        sr_o = (T["gsum"][0:1, r * 96 + 64:r * 96 + 96]
                .rearrange("p (h i) -> p h i", h=H))
        v_op(c, lambda e, sr_s=sr_s: e.tensor_tensor(
            ohg, sr_s.unsqueeze(3).broadcast_to([1, H, 8, 8]),
            T["iota8"][0:1, :].unsqueeze(1).unsqueeze(1).broadcast_to(
                [1, H, 8, 8]), ALU.is_equal))

        def gath(dst, srcv):
            v_op(c, lambda e, srcv=srcv: e.tensor_tensor(
                t0g, ohg, srcv.unsqueeze(2).broadcast_to(
                    [1, H, 8, 8]), ALU.mult))
            v_op(c, lambda e, dst=dst: e.tensor_reduce(dst, t0g, AXX, ALU.add))

        gn_s = T["gnx"][0:1, 0:32].rearrange("p (h i) -> p h i", h=H)
        gn_r = T["gnx"][0:1, 32:64].rearrange("p (h i) -> p h i", h=H)
        gn_o = T["gnx"][0:1, 64:96].rearrange("p (h i) -> p h i", h=H)
        gath(gn_s, gc_s)
        gath(gn_r, gc_r)
        gath(gn_o, gc_o)
        v_op(c, lambda e, sr_r=sr_r: e.tensor_tensor(gn_r, gn_r, sr_r, ALU.mult))
        v_op(c, lambda e, sr_r=sr_r: e.tensor_tensor(gn_o, gn_o, sr_r, ALU.mult))
        v_op(c, lambda e, sr_o=sr_o: e.tensor_tensor(gn_o, gn_o, sr_o, ALU.add))
        v_op(c, lambda e, r=r: e.tensor_single_scalar(
            T["gmsk"][:, :], T["cid"][:, :], float(r), ALU.is_gt))
        for a_, b_ in ((gn_s, gc_s), (gn_r, gc_r), (gn_o, gc_o)):
            v_op(c, lambda e, a_=a_, b_=b_: e.tensor_tensor(
                a_, a_, b_, ALU.subtract))
            v_op(c, lambda e, a_=a_: e.tensor_tensor(
                a_, a_,
                T["gmsk"][0:1, 0:1].unsqueeze(2)
                .broadcast_to([1, H, 8]), ALU.mult))
            v_op(c, lambda e, a_=a_, b_=b_: e.tensor_tensor(
                b_, b_, a_, ALU.add))

    t_op(c, lambda e: e.matmul(c.PSB[3][:, 0:HN], T["ones1"][:, :],
                               T["gcur"][0:1, 64:96], start=True, stop=True),
         wait=[("ve", s.val("ve"))], tag="rephin")
    v_op(c, lambda e: e.tensor_copy(T["h_in"][:, :], c.PSB[3][:, 0:HN]),
         wait=[("mm", s.val("mm"))])

    hv = T["h_in"][0:8, :].unsqueeze(1)
    pi2f = T["l2_pi"][:, :].rearrange("p (k f) -> p k f", f=HN)
    sh2f = T["l2_sh"][:, :].rearrange("p (k f) -> p k f", f=HN)
    dh2f = T["l2_dh"][:, :].rearrange("p (k f) -> p k f", f=HN)
    v_op(c, lambda e: e.tensor_tensor(
        dh2f, pi2f, hv.broadcast_to([8, LCH, HN]), ALU.mult))
    v_op(c, lambda e: e.tensor_tensor(dh2f, dh2f, sh2f, ALU.add))
    oh24b = T["l2_oh"][:, :].rearrange("p (q i m) -> p q i m", i=8, m=8)
    t024b = T["l2_t0"][:, :].rearrange("p (q i m) -> p q i m", i=8, m=8)
    dh24b = T["l2_dh"][:, :].rearrange("p (q m) -> p q m", m=8)
    v_op(c, lambda e: e.tensor_tensor(
        t024b, oh24b, dh24b.unsqueeze(2).broadcast_to([8, LCH * H, 8, 8]),
        ALU.mult))
    v_op(c, lambda e: e.tensor_reduce(
        T["l2_v"][:, :].rearrange("p (q i) -> p q i", i=8), t024b, AXX, ALU.add))
    ve_v = s.val("ve")

    dma(c, lambda e: e.dma_start(
        out=T["e_full"][:, :],
        in_=T["l2_v"][:, :].rearrange("q (r f) -> q r f", f=HN)),
        sem="lg", wait=[("ve", ve_v)], tag="efull")
    dma(c, lambda e: e.dma_start(out=T["e_ch"][1:P, :], in_=T["e_full"][0:P - 1, :]),
        sem="lg", wait=[("lg", s.val("lg"))])
    v_op(c, lambda e: e.tensor_copy(T["e_ch"][0:1, :], T["h_in"][0:1, :]),
         wait=[("lg", s.val("lg"))])

    ev = T["e_ch"][:, :].unsqueeze(1)
    pi1f = T["sc_pi"][:, :].rearrange("p (k f) -> p k f", f=HN)
    sh1f = T["sc_sh"][:, :].rearrange("p (k f) -> p k f", f=HN)
    hhf = T["sc_hh"][:, :].rearrange("p (k f) -> p k f", f=HN)
    v_op(c, lambda e: e.tensor_tensor(
        hhf, pi1f, ev.broadcast_to([P, LCH, HN]), ALU.mult))
    v_op(c, lambda e: e.tensor_tensor(hhf, hhf, sh1f, ALU.add))
    oh14 = T["sc_oh"][:, :].rearrange("p (q i m) -> p q i m", i=8, m=8)
    t014 = T["sc_t0"][:, :].rearrange("p (q i m) -> p q i m", i=8, m=8)
    hh14 = T["sc_hh"][:, :].rearrange("p (q m) -> p q m", m=8)
    v_op(c, lambda e: e.tensor_tensor(
        t014, oh14, hh14.unsqueeze(2).broadcast_to([P, LCH * H, 8, 8]),
        ALU.mult))
    v_op(c, lambda e: e.tensor_reduce(
        T["sc_ys"][:, :].rearrange("p (q i) -> p q i", i=8), t014, AXX, ALU.add))
    ve_ys = s.val("ve")
    if c.tap("ys", [P, LCH * HN]):
        dma(c, lambda e: e.dma_start(out=c.taps["ys"][:, :], in_=T["sc_ys"][:, :]),
            wait=[("ve", ve_ys)], sem="st")

    v_op(c, lambda e: e.tensor_copy(T["ys_hi"][:, :], T["sc_ys"][:, :]))
    v_op(c, lambda e: e.tensor_tensor(T["ys_lo"][:, :], T["sc_ys"][:, :],
                                      T["ys_hi"][:, :], ALU.subtract))
    ve_split = s.val("ve")
    for nm_src, nm_dst, bank in (("ys_hi", "ysT_hi", 4), ("ys_lo", "ysT_lo", 5)):
        for k4 in range(4):
            def trs(e, nm_src=nm_src, k4=k4, bank=bank):
                last = None
                for kk in range(4):
                    k = k4 * 4 + kk
                    inv = (T[nm_src][:, :].rearrange("p (k u) -> p k u", u=HN)
                           [:, k:k + 1, :].squeeze(1))
                    psv = PSB[bank][:, :].bitcast(BF16)
                    last = e.transpose(psv[0:HN, kk * P:(kk + 1) * P],
                                       inv, T["ident"][:, :])
                return last
            t_op(c, trs, wait=[("ve", max(ve_split, s.val("ve")))],
                 tag=f"ysT_{nm_src}_{k4}")
            mmt = s.val("mm")

            def evac(e, nm_dst=nm_dst, k4=k4, bank=bank):
                outv = (T[nm_dst][0:HN, :].rearrange("p (c k) -> p k c", k=LCH)
                        [:, k4 * 4:(k4 + 1) * 4, :])
                psv = PSB[bank][:, :].bitcast(BF16)[0:HN, 0:4 * P]
                return e.tensor_copy(outv, psv.rearrange(
                    "p (kk c) -> p kk c", kk=4))
            v_op(c, evac, wait=[("mm", mmt)])
    v_op(c, lambda e: e.memset(T["ysT_hi"][32:33, :], 1.0))
    v_op(c, lambda e: e.memset(T["ysT_lo"][32:33, :], 0.0))


def plan_out_proj(c):
    T, s, d, PSB = c.T, c.s, c.d, c.PSB
    c.s.op("vector", lambda e: e.nop(), wait=[("wq", c.wq_done)])

    def ldg0(e):
        inap = bass.AP(d["ccw_out"].tensor, 0, [[OF, P], [SEGW, KD], [1, FT]])
        return e.dma_start(out=T["wstage"].rearrange(
            "p (k f) -> p k f", k=KD), in_=inap)

    def ldu0(e):
        inap = bass.AP(d["ccw_out"].tensor, P * OF,
                       [[OF, P], [SEGW, KD], [1, FT]])
        return e.dma_start(out=T["wstage2"].rearrange(
            "p (k f) -> p k f", k=KD), in_=inap)
    dma(c, ldg0, sem="lw0", wait=[("ve", s.val("ve")), ("cc", c.cc_w)],
        tag="wg0pre")
    gdma(c, ldu0, sem="lw1", wait=[("ve", s.val("ve")), ("cc", c.cc_w)],
         tag="wu0pre")
    c.pre_lw0, c.pre_lw1 = s.val("lw0"), s.val("lw1")

    def zt(t):
        t_ = T["zA"] if t < 8 else T["zB"]
        return t_[:, (t % 8) * D:(t % 8 + 1) * D]

    evac_ve = {}
    xuse = {0: 0, 1: 0}
    for t in range(NT):
        buf = "xstage" if t % 2 == 0 else "xstage2"
        lsem = "lx0" if t % 2 == 0 else "lx1"
        dma(c, lambda e, t=t, buf=buf: e.dma_start(
            out=T[buf][:, :], in_=d["x"][t * P:(t + 1) * P, :]),
            sem=lsem, wait=[("ve", xuse[t % 2])], tag=f"xr{t}")
        ld_x = s.val(lsem)
        for dc in range(2):
            bank = (t % 2) * 2 + dc

            def mms(e, t=t, dc=dc, bank=bank):
                last = None
                first = True
                for a_, b_ in (("ysT_hi", "outWb_hi"), ("ysT_hi", "outWb_lo"),
                               ("ysT_lo", "outWb_hi")):
                    last = e.matmul(
                        PSB[bank][:, 0:512],
                        T[a_][0:33, t * P:(t + 1) * P],
                        T[b_][0:33, dc * 512:(dc + 1) * 512],
                        start=first, stop=(a_ == "ysT_lo"))
                    first = False
                return last
            t_op(c, mms, wait=[("ve", max(s.val("ve"), evac_ve.get(bank, 0)))],
                 tag=f"oproj{t}_{dc}")
            mmv = s.val("mm")
            c.s.op("vector", lambda e, t=t, dc=dc, bank=bank, buf=buf:
                   e.scalar_tensor_tensor(
                       zt(t)[:, dc * 512:(dc + 1) * 512],
                       T[buf][:, dc * 512:(dc + 1) * 512], 2.0,
                       PSB[bank][:, 0:512], ALU.mult, ALU.add),
                   wait=[("mm", mmv), (lsem, ld_x)], inc=[("ve", 1)])
            evac_ve[bank] = s.val("ve")
        xuse[t % 2] = s.val("ve")
        v_op(c, lambda e, t=t: e.tensor_reduce(T["stat1"][:, t:t + 1],
                                               zt(t), AXX, ALU.add))
        s_op(c, lambda e, t=t: e.activation(T["qf_t"][:, 0:D], zt(t),
                                            AF.Square),
             wait=[("ve", s.val("ve"))])
        v_op(c, lambda e, t=t: e.tensor_reduce(T["stat2"][:, t:t + 1],
                                               T["qf_t"][:, 0:D], AXX, ALU.add),
             wait=[("se", s.val("se"))])

    v_op(c, lambda e: e.tensor_scalar(T["mu"][:, :], T["stat1"][:, :],
                                      1.0 / D, None, ALU.mult))
    v_op(c, lambda e: e.tensor_tensor(T["stat1"][:, :], T["mu"][:, :],
                                      T["mu"][:, :], ALU.mult))
    v_op(c, lambda e: e.tensor_scalar(T["stat2"][:, :], T["stat2"][:, :],
                                      1.0 / D, LN_EPS, ALU.mult, ALU.add))
    v_op(c, lambda e: e.tensor_tensor(T["stat2"][:, :], T["stat2"][:, :],
                                      T["stat1"][:, :], ALU.subtract))
    _rsqrt_inplace(c, T["stat2"][:, :], T["rstd"][:, :], T["st_s"][:, :])

    ob_se = {"yout": 0, "yt1": 0}
    for t in range(NT):
        ob = "yout" if t % 2 == 0 else "yt1"
        v_op(c, lambda e, t=t, ob=ob: e.tensor_scalar(
            T[ob][:, :], zt(t), T["mu"][:, t:t + 1],
            T["rstd"][:, t:t + 1], ALU.subtract, ALU.mult),
            wait=[("qa", 16 * max(0, t - 1)), ("se", ob_se[ob])])
        ve_y = s.val("ve")
        dma(c, lambda e, t=t, ob=ob: e.dma_start(
            out=d["ybuf"][t * P:(t + 1) * P, :], in_=T[ob][:, :]),
            wait=[("ve", ve_y)], sem="qa", tag=f"ystore{t}")
        v_op(c, lambda e, t=t, ob=ob: e.tensor_reduce(
            T["sa"][:, t:t + 1], T[ob][:, :], AXX, ALU.max,
            apply_absolute_value=True))
        v_op(c, lambda e, t=t: e.tensor_single_scalar(
            T["sa"][:, t:t + 1], T["sa"][:, t:t + 1], 1e-5, ALU.max))
        v_op(c, lambda e, t=t: e.reciprocal(T["sa"][:, t:t + 1],
                                            T["sa"][:, t:t + 1]))
        v_op(c, lambda e, t=t: e.tensor_scalar(T["sa"][:, t:t + 1],
                                               T["sa"][:, t:t + 1], 127.0,
                                               None, ALU.mult))
        s_op(c, lambda e, t=t, ob=ob: e.activation(
            T["qv_t"][:, :], T[ob][:, :], AF.Identity,
            bias=T["magicb"][:, :], scale=T["sa"][:, t:t + 1]),
            wait=[("ve", s.val("ve"))])
        ob_se[ob] = s.val("se")
        s_op(c, lambda e: e.activation(T["xq_t"][:, :], T["qv_t"][:, :],
                                       AF.Identity, bias=T["nmagicb"][:, :]),
             wait=[("mm", getattr(c, "op_tr_mm", 0))])
        se_q = s.val("se")
        for half in range(2):
            bank = 4 + half

            def trs(e, t=t, half=half, bank=bank):
                last = None
                psv = PSB[bank][:, :].bitcast(BF16)
                for j in range(4):
                    cch = half * 4 + j
                    last = e.transpose(
                        psv[:, j * P:(j + 1) * P],
                        T["xq_t"][:, cch * P:(cch + 1) * P], T["ident"][:, :])
                return last
            t_op(c, trs, wait=[("se", se_q),
                               ("ve", getattr(c, "op_tr_ve", {}).get(half, 0))],
                 tag=f"xtr{t}_{half}")
            mmv2 = s.val("mm")
            c.op_tr_mm = mmv2

            def evac(e, t=t, half=half, bank=bank):
                outv = (T["x_intT"][:, :].rearrange("p (c s) -> p c s", c=KD)
                        [:, half * 4:(half + 1) * 4, t * P:(t + 1) * P])
                psv = PSB[bank][:, :].bitcast(BF16)[:, 0:4 * P]
                return e.tensor_copy(outv, psv.rearrange(
                    "p (j f) -> p j f", j=4))
            v_op(c, evac, wait=[("mm", mmv2)])
            if not hasattr(c, "op_tr_ve"):
                c.op_tr_ve = {}
            c.op_tr_ve[half] = s.val("ve")
    c.mm_oproj = s.val("mm")
    c.qa_oproj = s.val("qa")
    c.ve_oproj = s.val("ve")


def plan_ffn(c):
    T, s, d, PSB = c.T, c.s, c.d, c.PSB

    def z_t(ti):
        t_ = T["zA"] if ti < 2 else T["zB"]
        return t_[:, (ti % 2) * OF:(ti % 2) * OF + OF]

    def iqreg(ti):
        return T[("iqA0", "iqA1", "iqB0", "iqB1")[ti]]

    GST = ["wstage", "wstageB"]
    UST = ["wstage2", "wstage2B"]
    WDB = ["wdst", "wdst2", "outWb_hi", "outWb_lo"]

    v_op(c, lambda e: e.reciprocal(T["rsa"][:, :], T["sa"][:, :]))
    v_op(c, lambda e: e.tensor_scalar(T["rgq"][:, :], T["rsa"][:, :],
                                      T["whalf"][:, 0:1], None, ALU.mult))
    v_op(c, lambda e: e.tensor_scalar(T["ruq"][:, :], T["rsa"][:, :],
                                      T["whalf"][:, 1:2], None, ALU.mult))

    ldg_v, ldu_v = {}, {}
    mmQ = {}
    silu_se, evac_ve = {}, {}
    stt_ve = {g_: {} for g_ in range(-1, NFT)}
    mm_dn = {}

    def sched_wload(Q):
        w_ = [("mm", mmQ[Q - 2])] if Q - 2 in mmQ else []
        if Q <= 1:
            w_.append(("cc", c.cc_w))
        if Q == 1:
            w_.append(("ve", c.ve_oproj))

        def ldg(e, Q=Q):
            inap = bass.AP(d["ccw_out"].tensor, (Q % 8) * FT,
                           [[OF, P], [SEGW, KD], [1, FT]])
            return e.dma_start(out=T[GST[Q % 2]].rearrange(
                "p (k f) -> p k f", k=KD), in_=inap)

        def ldu(e, Q=Q):
            inap = bass.AP(d["ccw_out"].tensor, P * OF + (Q % 8) * FT,
                           [[OF, P], [SEGW, KD], [1, FT]])
            return e.dma_start(out=T[UST[Q % 2]].rearrange(
                "p (k f) -> p k f", k=KD), in_=inap)
        dma(c, ldg, sem="lw0", wait=w_, tag=f"wg{Q}")
        ldg_v[Q] = s.val("lw0")
        gdma(c, ldu, sem="lw1", wait=w_, tag=f"wu{Q}")
        ldu_v[Q] = s.val("lw1")

    if hasattr(c, "pre_lw0"):
        ldg_v[0], ldu_v[0] = c.pre_lw0, c.pre_lw1
        sched_wload(1)
    else:
        sched_wload(0)
        sched_wload(1)

    for g in range(NFT):
        for q8 in range(8):
            Q = g * 8 + q8
            for ti in range(4):
                t_ = g * 4 + ti
                bg, bu = ti * 2, ti * 2 + 1

                def gumm(e, Q=Q, t_=t_, bg=bg, bu=bu):
                    last = None
                    gst, ust = T[GST[Q % 2]], T[UST[Q % 2]]
                    for k in range(KD):
                        xs = T["x_intT"][:, k * TCORE + t_ * P:
                                         k * TCORE + (t_ + 1) * P]
                        last = e.matmul(PSB[bg][:, 0:FT], xs,
                                        gst[:, k * FT:(k + 1) * FT],
                                        start=(k == 0), stop=(k == KD - 1))
                        last = e.matmul(PSB[bu][:, 0:FT], xs,
                                        ust[:, k * FT:(k + 1) * FT],
                                        start=(k == 0), stop=(k == KD - 1))
                    return last
                w_ = [("lw0", ldg_v[Q]), ("lw1", ldu_v[Q])]
                if q8 == 0:
                    if ti in stt_ve[g - 1]:
                        w_.append(("ve", stt_ve[g - 1][ti]))
                    else:
                        w_.append(("ve", c.ve_oproj))
                else:
                    w_ += [("se", silu_se[(Q - 1, ti)]),
                           ("ve", evac_ve[(Q - 1, ti)])]
                t_op(c, gumm, wait=w_, tag=f"gu{Q}_{ti}")
                mmv = s.val("mm")
                mmQ[Q] = mmv
                gsb = "gsbA" if ti % 2 == 0 else "gsbB"
                sw = [("mm", mmv)]
                if (Q, ti - 2) in evac_ve:
                    sw.append(("ve", evac_ve[(Q, ti - 2)]))
                elif (Q - 1, ti + 2) in evac_ve:
                    sw.append(("ve", evac_ve[(Q - 1, ti + 2)]))
                s_op(c, lambda e, t_=t_, bg=bg, gsb=gsb: e.activation(
                    T[gsb][:, :], PSB[bg][:, 0:FT], AF.Silu,
                    scale=T["rgq"][:, t_:t_ + 1]), wait=sw, tag=f"si{Q}_{ti}")
                silu_se[(Q, ti)] = s.val("se")
                v_op(c, lambda e, q8=q8, ti=ti, t_=t_, bu=bu: e.tensor_scalar(
                    z_t(ti)[:, q8 * FT:(q8 + 1) * FT], PSB[bu][:, 0:FT],
                    T["ruq"][:, t_:t_ + 1], None, ALU.mult),
                    wait=[("mm", mmv)])
                c.s.op("vector", lambda e, q8=q8, ti=ti, gsb=gsb:
                       e.tensor_tensor(
                           z_t(ti)[:, q8 * FT:(q8 + 1) * FT],
                           z_t(ti)[:, q8 * FT:(q8 + 1) * FT],
                           T[gsb][:, :], ALU.mult),
                       wait=[("se", silu_se[(Q, ti)])], inc=[("ve", 1)])
                evac_ve[(Q, ti)] = s.val("ve")
                v_op(c, lambda e, q8=q8, ti=ti: e.tensor_reduce(
                    T["Mq"][:, ti * 8 + q8:ti * 8 + q8 + 1],
                    z_t(ti)[:, q8 * FT:(q8 + 1) * FT], AXX, ALU.max,
                    apply_absolute_value=True))
            if Q + 2 < NFT * 8:
                sched_wload(Q + 2)

        v_op(c, lambda e: e.tensor_reduce(
            T["mt"][:, :], T["Mq"][:, :].rearrange("p (i q) -> p i q", i=4),
            AXX, ALU.max))
        v_op(c, lambda e: e.tensor_single_scalar(T["mt"][:, :], T["mt"][:, :],
                                                 1e-5, ALU.max))
        v_op(c, lambda e, g=g: e.tensor_scalar(
            T["dsc"][:, g * 4:(g + 1) * 4], T["mt"][:, :],
            T["whalf"][:, 2:3], 1.0 / 127.0, ALU.mult, ALU.mult))
        v_op(c, lambda e, g=g: e.reciprocal(T["sitq"][:, g * 4:(g + 1) * 4],
                                            T["mt"][:, :]))
        v_op(c, lambda e, g=g: e.tensor_scalar(
            T["sitq"][:, g * 4:(g + 1) * 4], T["sitq"][:, g * 4:(g + 1) * 4],
            127.0, None, ALU.mult))
        ve_scales = s.val("ve")

        if g == 0 and c.tap("z0", [P, 4 * OF]):
            dma(c, lambda e: e.dma_start(out=c.taps["z0"][:, 0:2 * OF],
                                         in_=T["zA"][:, :]),
                wait=[("ve", ve_scales)], sem="st")
            dma(c, lambda e: e.dma_start(out=c.taps["z0"][:, 2 * OF:4 * OF],
                                         in_=T["zB"][:, :]),
                wait=[("ve", ve_scales)], sem="st")
        if g == 0 and c.tap("Mq0", [P, 32]):
            dma(c, lambda e: e.dma_start(out=c.taps["Mq0"][:, :],
                                         in_=T["Mq"][:, :]),
                wait=[("ve", ve_scales)], sem="st")
        tap_hold = s.val("st")

        quant_done = {}
        for ti in range(4):
            t_ = g * 4 + ti
            if ti < 1:
                s_op(c, lambda e, ti=ti, t_=t_: e.activation(
                    z_t(ti)[:, :], z_t(ti)[:, :], AF.Identity,
                    bias=T["magicb"][:, :], scale=T["sitq"][:, t_:t_ + 1]),
                    wait=[("ve", ve_scales), ("st", tap_hold)],
                    tag=f"qa{g}_{ti}")
                s_op(c, lambda e, ti=ti: e.activation(
                    iqreg(ti)[:, 0:OF], z_t(ti)[:, :], AF.Identity,
                    bias=T["nmagicb"][:, :]), tag=f"qb{g}_{ti}")
                quant_done[ti] = ("se", s.val("se"))
            else:
                v_op(c, lambda e, ti=ti, t_=t_: e.tensor_scalar(
                    z_t(ti)[:, :], z_t(ti)[:, :], T["sitq"][:, t_:t_ + 1],
                    MAGIC, ALU.mult, ALU.add),
                    wait=[("st", tap_hold)], tag=f"qa{g}_{ti}")
                s_op(c, lambda e, ti=ti: e.activation(
                    iqreg(ti)[:, 0:OF], z_t(ti)[:, :], AF.Identity,
                    bias=T["nmagicb"][:, :]),
                    wait=[("ve", s.val("ve"))], tag=f"qb{g}_{ti}")
                quant_done[ti] = ("se", s.val("se"))

        fill_done = {}
        cum = {"ve": 0, "se": 0}
        tr_prev = {}
        for f in range(4):
            for ti in range(4):
                bank = (f * 4 + ti) % 8

                def trf(e, f=f, ti=ti, bank=bank):
                    psv = PSB[bank][:, :].bitcast(BF16)
                    last = None
                    for j in range(8):
                        oc = f * 8 + j
                        last = e.transpose(
                            psv[:, j * P:(j + 1) * P],
                            iqreg(ti)[:, oc * P:(oc + 1) * P],
                            T["ident"][:, :])
                    return last
                w_ = [quant_done[ti]]
                if f == 0:
                    w_.append(("ve", ve_scales))
                if bank in tr_prev:
                    w_.append(tr_prev[bank])
                t_op(c, trf, wait=w_, tag=f"tr{g}_{f}_{ti}")
                mmv = s.val("mm")

                def evf(e, f=f, ti=ti, bank=bank, cp=None):
                    src = PSB[bank][:, :].bitcast(BF16)[:, 0:1024]
                    dst = iqreg(ti)[:, OF + f * 1024:OF + (f + 1) * 1024]
                    if cp == "s":
                        return e.activation(dst, src, AF.Identity)
                    return e.tensor_copy(dst, src)
                if (f + ti) % 2 == 0:
                    v_op(c, lambda e, evf=evf: evf(e), wait=[("mm", mmv)])
                    mk = ("ve", s.val("ve"))
                else:
                    s_op(c, lambda e, evf=evf: evf(e, cp="s"),
                         wait=[("mm", mmv)])
                    mk = ("se", s.val("se"))
                tr_prev[bank] = mk
                cum[mk[0]] = mk[1]
            fill_done[f] = [("ve", cum["ve"]), ("se", cum["se"])]

        if g == 0 and c.tap("iqt0", [P, 2 * OF], BF16):
            dma(c, lambda e: e.dma_start(out=c.taps["iqt0"][:, :],
                                         in_=iqreg(0)[:, :]),
                wait=fill_done[3], sem="st")
        if g == 0 and c.tap("iqt3", [P, 2 * OF], BF16):
            dma(c, lambda e: e.dma_start(out=c.taps["iqt3"][:, :],
                                         in_=iqreg(3)[:, :]),
                wait=fill_done[3], sem="st")
        if g == 0 and c.tap("scl0", [P, 3 * NT]):
            dma(c, lambda e: e.dma_start(out=c.taps["scl0"][:, 0:NT],
                                         in_=T["sitq"][:, :]),
                wait=[("ve", ve_scales)], sem="st")
            dma(c, lambda e: e.dma_start(out=c.taps["scl0"][:, NT:2 * NT],
                                         in_=T["dsc"][:, :]), sem="st")
            dma(c, lambda e: e.dma_start(out=c.taps["scl0"][:, 2 * NT:3 * NT],
                                         in_=T["rgq"][:, :]), sem="st")
        tap_hold2 = s.val("st")

        yld_v = {}
        for ti in range(4):
            t_ = g * 4 + ti
            w_ = ([("ve", stt_ve[g - 1][ti])] if ti in stt_ve[g - 1]
                  else [("qa", c.qa_oproj)])
            dma(c, lambda e, ti=ti, t_=t_: e.dma_start(
                out=T["ystg%d" % ti][:, :],
                in_=d["ybuf"][t_ * P:(t_ + 1) * P, :]),
                sem="ly0", wait=w_, tag=f"yld{t_}")
            yld_v[ti] = s.val("ly0")

        for oc in range(NOC):
            buf = WDB[oc % 4]
            wsem = "lwd0" if oc % 2 == 0 else "lwd1"
            w_ = []
            if (g, oc - 4) in mm_dn:
                w_.append(("mm", mm_dn[(g, oc - 4)]))
            elif (g - 1, oc + 28) in mm_dn:
                w_.append(("mm", mm_dn[(g - 1, oc + 28)]))
            if g == 0 and oc < 4:
                w_.append(("cc", c.cc_w))
                w_.append(("mm", c.mm_oproj))
            ldfn = dma if oc % 2 == 0 else gdma
            wd_ap = bass.AP(d["ccw_out"].tensor,
                            (oc // 4) * SEGW + 2 * P * OF + (oc % 4) * P * D,
                            [[D, P], [1, D]])
            ldfn(c, lambda e, buf=buf, wd_ap=wd_ap: e.dma_start(
                out=T[buf][:, :], in_=wd_ap),
                sem=wsem, wait=w_, tag=f"wd{g}_{oc}")
            ld_wd = s.val(wsem)

            def dmm(e, oc=oc, buf=buf):
                last = None
                for ti in range(4):
                    st_ap = iqreg(ti)[:, OF + oc * P:OF + (oc + 1) * P]
                    for dc in range(2):
                        last = e.matmul(
                            PSB[ti * 2 + dc][:, 0:512], st_ap,
                            T[buf][:, dc * 512:(dc + 1) * 512],
                            start=(oc == 0), stop=(oc == NOC - 1))
                return last
            t_op(c, dmm, wait=[(wsem, ld_wd)]
                 + fill_done[3 if oc == 0 else oc // 8],
                 tag=f"dn{g}_{oc}")
            mm_dn[(g, oc)] = s.val("mm")
        mmd = mm_dn[(g, NOC - 1)]

        for ti in range(4):
            t_ = g * 4 + ti
            for dc in range(2):
                c.s.op("vector", lambda e, ti=ti, t_=t_, dc=dc:
                       e.scalar_tensor_tensor(
                           z_t(ti)[:, dc * 512:(dc + 1) * 512],
                           PSB[ti * 2 + dc][:, 0:512],
                           T["dsc"][:, t_:t_ + 1],
                           T["ystg%d" % ti][:, dc * 512:(dc + 1) * 512],
                           ALU.mult, ALU.add),
                       wait=[("mm", mmd), ("ly0", yld_v[ti]),
                             ("st", tap_hold2)],
                       inc=[("ve", 1)])
            stt_ve[g][ti] = s.val("ve")
        g4 = slice(g * 4, (g + 1) * 4)
        for ti in range(4):
            t_ = g * 4 + ti
            zst = z_t(ti)[:, 0:D]
            v_op(c, lambda e, t_=t_, zst=zst: e.tensor_reduce(
                T["stat1"][:, t_:t_ + 1], zst, AXX, ALU.add))
            s_op(c, lambda e, zst=zst: e.activation(
                T["qf_t"][:, 0:D], zst, AF.Square),
                wait=[("ve", s.val("ve"))])
            v_op(c, lambda e, t_=t_: e.tensor_reduce(
                T["stat2"][:, t_:t_ + 1], T["qf_t"][:, 0:D], AXX, ALU.add),
                wait=[("se", s.val("se"))])
        v_op(c, lambda e, g4=g4: e.tensor_scalar(
            T["mu"][:, g4], T["stat1"][:, g4], 1.0 / D, None, ALU.mult))
        v_op(c, lambda e, g4=g4: e.tensor_tensor(
            T["stat1"][:, g4], T["mu"][:, g4], T["mu"][:, g4], ALU.mult))
        v_op(c, lambda e, g4=g4: e.tensor_scalar(
            T["stat2"][:, g4], T["stat2"][:, g4], 1.0 / D, LN_EPS,
            ALU.mult, ALU.add))
        v_op(c, lambda e, g4=g4: e.tensor_tensor(
            T["stat2"][:, g4], T["stat2"][:, g4], T["stat1"][:, g4],
            ALU.subtract))
        _rsqrt_inplace(c, T["stat2"][:, g4], T["rstd"][:, g4],
                       T["st_s"][:, g4])
        dq_prev = s.val("dq")
        for ti in range(4):
            t_ = g * 4 + ti
            ob = ("yout", "qv_t", "yt1", "zt1")[ti]
            v_op(c, lambda e, t_=t_, ti=ti, ob=ob: e.tensor_scalar(
                T[ob][:, :], z_t(ti)[:, 0:D], T["mu"][:, t_:t_ + 1],
                T["rstd"][:, t_:t_ + 1], ALU.subtract, ALU.mult),
                wait=[("dq", dq_prev)])
            dma(c, lambda e, t_=t_, ob=ob: e.dma_start(
                out=d["out"][t_ * P:(t_ + 1) * P, :], in_=T[ob][:, :]),
                wait=[("ve", s.val("ve"))], sem="dq", tag=f"out{t_}")


def plan_finish(c):
    s = c.s
    allsems = [(n, s.val(n)) for n in s.cnt if n != "ph"]
    s.op("gpsimd", lambda e: e.nop(), wait=allsems)



_BUILT = None


def _get_built():
    global _BUILT
    if _BUILT is None:
        _BUILT = build_core()
    return _BUILT


def prep_in_maps(inputs):
    xfull = np.ascontiguousarray(np.asarray(inputs["L0_attn_x"],
                                            dtype=np.float32)[0])
    Wcat = np.ascontiguousarray(np.concatenate(
        [np.asarray(inputs["L0_attn_sign_W"], np.float32),
         np.asarray(inputs["L0_attn_perm_W"], np.float32),
         np.asarray(inputs["L0_attn_gate_W"], np.float32),
         np.asarray(inputs["L0_attn_in_W"], np.float32)], axis=1))
    bcat = np.ascontiguousarray(np.concatenate(
        [np.asarray(inputs["L0_attn_sign_b"], np.float32),
         np.asarray(inputs["L0_attn_perm_b"], np.float32),
         np.asarray(inputs["L0_attn_gate_b"], np.float32),
         np.asarray(inputs["L0_attn_in_b"], np.float32)])[None, :])
    lnw = np.ascontiguousarray(
        np.asarray(inputs["L0_attn_ln_w"], np.float32).reshape(KD, P).T)
    lnb = np.ascontiguousarray(
        np.asarray(inputs["L0_attn_ln_b"], np.float32).reshape(KD, P).T)
    outWb = np.ascontiguousarray(np.concatenate(
        [np.asarray(inputs["L0_attn_out_W"], np.float32),
         np.asarray(inputs["L0_attn_out_b"], np.float32)[None, :]], axis=0))
    wgT = np.ascontiguousarray(np.asarray(inputs["L0_ffn_w_gate"], np.float32).T)
    wuT = np.ascontiguousarray(np.asarray(inputs["L0_ffn_w_up"], np.float32).T)
    wdT = np.ascontiguousarray(np.asarray(inputs["L0_ffn_w_down"], np.float32).T)
    ident = np.eye(P, dtype=np.float32)
    ones1 = np.ones((1, P), np.float32)
    onescol = np.ones((P, 1), np.float32)

    in_maps = []
    for core in range(NCORE):
        xs = np.ascontiguousarray(xfull[core * TCORE:(core + 1) * TCORE])
        in_maps.append(dict(
            x=xs, xT=np.ascontiguousarray(xs.T), Wcat=Wcat, bcat=bcat,
            lnw=lnw, lnb=lnb, outWb=outWb,
            wgs=np.ascontiguousarray(wgT[core * P:(core + 1) * P]),
            wus=np.ascontiguousarray(wuT[core * P:(core + 1) * P]),
            wds=np.ascontiguousarray(wdT[core * 4 * P:(core + 1) * 4 * P]),
            ident=ident, ones1=ones1, onescol=onescol,
            cid=np.full((1, 1), float(core), np.float32)))
    return in_maps


def kernel(**inputs):
    nc, taps, _ = _get_built()
    in_maps = prep_in_maps(inputs)
    res = run_bass_kernel_spmd(nc, in_maps, core_ids=list(range(NCORE)))
    out = np.concatenate([r["out"] for r in res.results], axis=0)
    return out[None, :, :]
